# version 33
# baseline (speedup 1.0000x reference)
import contextlib
import numpy as np
import ml_dtypes
import concourse.bass as bass
import concourse.mybir as mybir
from concourse.bass_utils import run_bass_kernel_spmd
from concourse.bass_core_rust import IndirectOffsetOnAxis

F32 = mybir.dt.float32
BF16 = mybir.dt.bfloat16
I32 = mybir.dt.int32
U32 = mybir.dt.uint32
AF = mybir.ActivationFunctionType
ALU = mybir.AluOpType
AX = mybir.AxisListType

NCORES = 8
B, T, D = 2, 8192, 1024
TPC = T * B // NCORES
NT = TPC // 128
DF = 4096
NE = 8
L1CAP = 384
NH0, DH0 = 16, 64
H1, DK1, DV1 = 8, 128, 128
CCH = 128
KCONV = 4

bfnp = ml_dtypes.bfloat16



def _rope_perm():
    perm = np.zeros(D, np.int64)
    for tau in range(D // 128):
        base = 128 * tau
        for k in range(64):
            perm[base + k] = base + 2 * k
            perm[base + 64 + k] = base + 2 * k + 1
    return perm


def host_prep(inp):
    sh = {}
    f32 = lambda a: np.ascontiguousarray(np.asarray(a), dtype=np.float32)

    perm = _rope_perm()

    sh["gateT"] = f32(inp["L0_ffn_gate_w"].T)
    sh["l0w1"] = f32(inp["L0_ffn_w1"])
    sh["l0w2"] = f32(inp["L0_ffn_w2"])

    WqT = f32(inp["L0_attn_Wq"].T)[:, perm]
    WkT = f32(inp["L0_attn_Wk"].T)[:, perm]
    WvT = f32(inp["L0_attn_Wv"].T)
    sh["WqT"] = np.ascontiguousarray(WqT)
    sh["WkT"] = np.ascontiguousarray(WkT)
    sh["WvT"] = WvT
    sh["bq"] = f32(inp["L0_attn_bq"])[perm].reshape(D, 1)
    sh["bk"] = f32(inp["L0_attn_bk"])[perm].reshape(D, 1)
    sh["bv"] = f32(inp["L0_attn_bv"]).reshape(1, D)
    sh["gam0"] = f32(inp["L0_attn_gamma"]).reshape(1, D)
    sh["bet0"] = f32(inp["L0_attn_beta"]).reshape(1, D)
    theta = 10000.0
    inv_freq = (1.0 / theta ** (np.arange(0, D, 2, dtype=np.float32) / np.float32(D)))
    inv_freq = inv_freq.astype(np.float32)
    pos = np.arange(T, dtype=np.float32)
    ang = pos[:, None] * inv_freq[None, :]
    cos_t, sin_t = np.cos(ang).astype(np.float32), np.sin(ang).astype(np.float32)
    sh["_cosT"] = np.ascontiguousarray(cos_t.T)
    sh["_sinT"] = np.ascontiguousarray(sin_t.T)
    cdup = np.empty((1024, T), np.float32)
    sdup = np.empty((1024, T), np.float32)
    for tau in range(8):
        blk_c = cos_t.T[tau * 64:(tau + 1) * 64, :]
        blk_s = sin_t.T[tau * 64:(tau + 1) * 64, :]
        cdup[tau * 128:tau * 128 + 64] = blk_c
        cdup[tau * 128 + 64:tau * 128 + 128] = blk_c
        sdup[tau * 128:tau * 128 + 64] = blk_s
        sdup[tau * 128 + 64:tau * 128 + 128] = blk_s
    sh["_cosDup"] = cdup
    sh["_sinDup"] = sdup
    kvm = np.zeros((128, 128), np.float32)
    h0set = list(range(0, 32)) + list(range(64, 96))
    h1set = list(range(32, 64)) + list(range(96, 128))
    for p in h0set:
        kvm[p, 0:64] = 1.0
    for p in h1set:
        kvm[p, 64:128] = 1.0
    sh["kvmask"] = kvm
    hm = np.zeros((128, 2), np.float32)
    hm[h0set, 0] = 1.0
    hm[h1set, 1] = 1.0
    sh["headmask2"] = hm

    sh["WrT"] = f32(inp["L1_ffn_Wr"].T)
    sh["br"] = f32(inp["L1_ffn_br"]).reshape(8, 1)
    sh["l1w1T"] = np.ascontiguousarray(f32(inp["L1_ffn_W1"]).transpose(0, 2, 1))
    sh["l1b1"] = f32(inp["L1_ffn_b1"])
    sh["l1w2T"] = np.ascontiguousarray(f32(inp["L1_ffn_W2"]).transpose(0, 2, 1))
    sh["l1b2"] = f32(inp["L1_ffn_b2"])

    sh["qkvT"] = f32(inp["L1_attn_qkv_w"].T)
    sh["zT"] = f32(inp["L1_attn_z_w"].T)
    sh["baT"] = np.concatenate([f32(inp["L1_attn_b_w"].T), f32(inp["L1_attn_a_w"].T)],
                               axis=1)
    sh["convw"] = f32(inp["L1_attn_conv_w"][:, 0, :])
    sh["dtb"] = f32(inp["L1_attn_dt_bias"]).reshape(8, 1)
    sh["Alog"] = f32(inp["L1_attn_A_log"]).reshape(8, 1)
    sh["normw"] = f32(inp["L1_attn_norm_w"]).reshape(1, 128)
    sh["outT"] = f32(inp["L1_attn_out_w"].T)

    sh["ident"] = np.eye(128, dtype=np.float32)
    sh["identb"] = np.eye(128).astype(bfnp)
    sh["ones_row"] = np.ones((1, 512), np.float32)
    sh["triu_incl"] = np.triu(np.ones((128, 128), np.float32), 0)
    bd = np.zeros((128, 8), np.float32)
    for e in range(8):
        bd[e * 16:(e + 1) * 16, e] = 1.0
    sh["bd16"] = bd
    sh["bd16T"] = np.ascontiguousarray(bd.T)
    iidx = np.arange(128)
    up_s = np.where(iidx[:, None] < iidx[None, :], 0.0, -1e30).astype(np.float32)
    lo_s = np.where(iidx[:, None] > iidx[None, :], 0.0, -1e30).astype(np.float32)
    up_i = np.where(iidx[:, None] <= iidx[None, :], 0.0, -1e30).astype(np.float32)
    sh["maskUs"] = up_s
    sh["maskLs"] = lo_s
    sh["maskUi"] = up_i
    sh["iota_col"] = np.arange(128, dtype=np.float32).reshape(128, 1)
    sh["Lstrict"] = np.triu(np.ones((128, 128), np.float32), 1)
    sh["l1b2c"] = np.ascontiguousarray(f32(inp["L1_ffn_b2"]).reshape(8, 8, 128)
                                       .transpose(2, 0, 1).reshape(128, 64))
    w1l0 = f32(inp["L0_ffn_w1"])
    w1l1 = np.ascontiguousarray(f32(inp["L1_ffn_W1"]).transpose(0, 2, 1))
    w2l0 = f32(inp["L0_ffn_w2"])
    w2l1 = np.ascontiguousarray(f32(inp["L1_ffn_W2"]).transpose(0, 2, 1))
    def tile_w1(w):
        return np.ascontiguousarray(
            w.reshape(8, 8, 128, 32, 128).transpose(0, 3, 2, 1, 4))
    def tile_w2(w):
        return np.ascontiguousarray(
            w.reshape(8, 32, 128, 8, 128).transpose(0, 3, 2, 1, 4))
    sh["w1h_l0"] = tile_w1(w1l0)
    sh["w1h_l1"] = tile_w1(w1l1)
    sh["w2h_l0"] = tile_w2(w2l0)
    sh["w2h_l1"] = tile_w2(w2l1)
    P = np.zeros((128, 128), np.float32)
    for p in range(64):
        P[p, p + 64] = -1.0
        P[p + 64, p] = 1.0
    sh["pswapT"] = np.ascontiguousarray(P.T)

    percore = []
    x_full = f32(inp["L0_attn_x"]).reshape(B * T, D)
    for c in range(NCORES):
        pc = {}
        pc["x0"] = np.ascontiguousarray(x_full[c * TPC:(c + 1) * TPC])
        b = c // 4
        off = (c % 4) * TPC
        pc["cosT"] = np.ascontiguousarray(sh["_cosT"][:, off:off + TPC])
        pc["sinT"] = np.ascontiguousarray(sh["_sinT"][:, off:off + TPC])
        pc["cosDup"] = np.ascontiguousarray(sh["_cosDup"][:, off:off + TPC])
        pc["sinDup"] = np.ascontiguousarray(sh["_sinDup"][:, off:off + TPC])
        percore.append(pc)
    return sh, percore



ENGS = ("pe", "dve", "act", "pool", "sq")
QSEMS = ("qs", "qa", "qp", "cc")


class Ctx:
    def __init__(self, nc):
        self.nc = nc
        self.st = contextlib.ExitStack()
        self.sem = {}
        for s in ENGS + QSEMS:
            self.sem[s] = self.st.enter_context(nc.semaphore("sem_" + s))
        self.cnt = {s: 0 for s in ENGS + QSEMS}
        self.W = {e: {s: 0 for s in ENGS + QSEMS} for e in ENGS}
        self.block = self.st.enter_context(nc.Block())
        self._stage_ops = None
        self._dep = None
        self.snaps = {}
        self.sb = {}
        self.ps = []

    def alloc(self, name, shape, dtype):
        stk = getattr(self, "pstack", None) or self.st
        uname = name if not getattr(self, "pprefix", "") else self.pprefix + name
        h = stk.enter_context(self.nc.sbuf_tensor(uname, list(shape), dtype))
        self.sb[name] = h
        return h

    def alloc_psum(self, name, shape, dtype):
        stk = getattr(self, "pstack", None) or self.st
        uname = name if not getattr(self, "pprefix", "") else self.pprefix + name
        h = stk.enter_context(self.nc.psum_tensor(uname, list(shape), dtype))
        return h

    def push_phase(self, prefix):
        self.pstack = contextlib.ExitStack()
        self.pprefix = prefix
        self.sb = {}

    def pop_phase(self):
        self.pstack.close()
        self.pstack = None
        self.pprefix = ""

    def bcreg(self, value):
        if not hasattr(self, "_bcregs"):
            self._bcregs = {}
        if value not in self._bcregs:
            holder = {}

            def mk(eng, value=value, holder=holder):
                r = eng.alloc_register()
                holder["r"] = r
                return eng.reg_mov(r, value)
            self._emit("pool", mk)
            self._bcregs[value] = holder
        return self._bcregs[value]

    def snapshot(self):
        return dict(self.cnt)

    def mask_sem(self, name):
        if not hasattr(self, "_mask"):
            self._mask = {}
        self._mask[name] = self.cnt[name]

    def mask_at(self, name, value):
        if not hasattr(self, "_mask"):
            self._mask = {}
        self._mask[name] = max(self._mask.get(name, 0), value)

    def unmask_sem(self, name):
        if hasattr(self, "_mask"):
            self._mask.pop(name, None)

    def begin(self, deps=None):
        assert self._stage_ops is None, "stage already open"
        if deps is None:
            self._dep = dict(self.cnt)
        else:
            d = {s: 0 for s in ENGS + QSEMS}
            for snap in deps:
                for s, v in snap.items():
                    d[s] = max(d[s], v)
            self._dep = d
        for s, v in getattr(self, "_mask", {}).items():
            self._dep[s] = min(self._dep[s], v)
        self._stage_ops = {e: [] for e in ENGS}

    def _emit(self, engname, fn, is_dma=None, inc_extra=0):
        assert self._stage_ops is not None, "no open stage"
        self._stage_ops[engname].append((fn, is_dma))
        if is_dma is not None:
            self.cnt[is_dma] += 16 + inc_extra

    def op(self, engname, fn):
        self._emit(engname, fn)

    def dma(self, engname, qsem, fn):
        self._emit(engname, fn, is_dma=qsem)

    def cc(self, fn):
        self._stage_ops["pool"].append((fn, "__cc__"))
        self.cnt["cc"] += 1

    def end(self):
        ops = self._stage_ops
        dep = self._dep
        self._stage_ops = None
        self._dep = None
        eng_map = {
            "pe": self.block.tensor, "dve": self.block.vector,
            "act": self.block.scalar, "pool": self.block.gpsimd,
            "sq": self.block.sync,
        }
        for engname, lst in ops.items():
            if not lst:
                continue
            waits = []
            for s, v in dep.items():
                if v > self.W[engname][s]:
                    waits.append((self.sem[s], v))
                    self.W[engname][s] = v
            n_comp = sum(1 for _, d in lst if d is None)
            sem = self.sem
            cnt_inc = [0]

            def body(eng, lst=lst, waits=waits, engname=engname, n_comp=n_comp):
                for sh, v in waits:
                    eng.wait_ge(sh, v)
                comp_seen = 0
                for fn, dtag in lst:
                    inst = fn(eng)
                    if dtag is None:
                        comp_seen += 1
                        if comp_seen == n_comp and inst is not None:
                            inst.then_inc(sem[engname], 1)
                    elif dtag == "__cc__":
                        inst.then_inc(sem["cc"], 1)
                    else:
                        inst.then_inc(sem[dtag], 16)
            eng_map[engname](body)
            if n_comp:
                self.cnt[engname] += 1

    def stage(self, deps=None):
        ctx = self

        class _G:
            def __enter__(self):
                ctx.begin(deps)
                return ctx

            def __exit__(self, *a):
                ctx.end()
                return False
        return _G()

    def barrier_snapshot(self):
        return self.snapshot()



def emit_ln_stats(ctx, x_tiles_dma, n_tiles=NT):
    nc = ctx.nc
    xrot = ctx.sb["xrot"]
    sq = ctx.sb["sq_scr"]
    sm = ctx.sb["smal"]; ss = ctx.sb["ssal"]
    with ctx.stage():
        ctx.dma("sq", "qs", lambda e: e.dma_start(xrot[:, 0:1024], x_tiles_dma(0)))
    for t in range(n_tiles):
        r = t % 2
        with ctx.stage():
            if t + 1 < n_tiles:
                ctx.dma("sq", "qs", lambda e, t=t: e.dma_start(
                    xrot[:, ((t + 1) % 2) * 1024:((t + 1) % 2) * 1024 + 1024],
                    x_tiles_dma(t + 1)))
            ctx.op("act", lambda e, t=t, r=r: e.activation(
                sq[:, r * 1024:(r + 1) * 1024], xrot[:, r * 1024:(r + 1) * 1024],
                AF.Square, accum_out=ss[:, t:t + 1]))
            ctx.op("dve", lambda e, t=t, r=r: e.reduce_sum(
                sm[:, t:t + 1], xrot[:, r * 1024:(r + 1) * 1024], axis=AX.X))
    mean = ctx.sb["mean"]; var = ctx.sb["var"]; rstd = ctx.sb["rstd"]
    nbias = ctx.sb["nbias"]
    nt = n_tiles
    with ctx.stage():
        ctx.op("dve", lambda e: e.tensor_scalar_mul(mean[:, :nt], sm[:, :nt], 1.0 / D))
        ctx.op("dve", lambda e: e.tensor_scalar_mul(var[:, :nt], ss[:, :nt], 1.0 / D))
    with ctx.stage():
        ctx.op("dve", lambda e: e.scalar_tensor_tensor(
            sq[:, 0:nt], mean[:, :nt], -1.0, mean[:, :nt], ALU.mult, ALU.mult))
    with ctx.stage():
        ctx.op("dve", lambda e: e.tensor_add(var[:, :nt], var[:, :nt], sq[:, 0:nt]))
    with ctx.stage():
        ctx.op("dve", lambda e: e.tensor_scalar_add(var[:, :nt], var[:, :nt],
                                                    float(LN_EPS)))
    with ctx.stage():
        ctx.op("act", lambda e: e.activation(rstd[:, :nt], var[:, :nt], AF.Sqrt))
    with ctx.stage():
        ctx.op("dve", lambda e: e.reciprocal(rstd[:, :nt], rstd[:, :nt]))
    with ctx.stage():
        ctx.op("dve", lambda e: e.scalar_tensor_tensor(
            nbias[:, :nt], mean[:, :nt], -1.0, rstd[:, :nt], ALU.mult, ALU.mult))


LN_EPS = 1e-5


def _cp(engname, dst_fn, src_fn):
    if engname == "act":
        return lambda e: e.copy(dst_fn(), src_fn())
    return lambda e: e.tensor_copy(dst_fn(), src_fn())



def emit_ln_apply_tp(ctx, x_tiles_dma, router_sb, psA, psB, psL, br_sb,
                     logits_fm, n_tiles=NT, probs_cb=None, write_scr=None,
                     lnbf_dram=None):
    xrot = ctx.sb["xrot"]; lnrot = ctx.sb["lnrot"]
    x32rot = ctx.sb.get("x32rot")
    rstd = ctx.sb["rstd"]; nbias = ctx.sb["nbias"]
    xfm = ctx.sb.get("xfm")
    ident = ctx.sb["ident"]
    for t in range(n_tiles):
        r = t % 2
        xs = xrot[:, r * 1024:(r + 1) * 1024]
        ls = lnrot[:, r * 1024:(r + 1) * 1024]
        x32 = x32rot[:, r * 1024:(r + 1) * 1024] if x32rot is not None else None
        with ctx.stage():
            ctx.dma("sq", "qs", lambda e, t=t, xs=xs: e.dma_start(xs, x_tiles_dma(t)))
        with ctx.stage():
            ctx.op("dve", lambda e, t=t, xs=xs, ls=ls: e.tensor_scalar(
                ls, xs, ctx.sb["mean"][:, t:t + 1], rstd[:, t:t + 1],
                ALU.subtract, ALU.mult))
        with ctx.stage():
            for k in range(8):
                ps = psA if k < 4 else psB
                ctx.op("pe", lambda e, k=k, ls=ls, ps=ps: e.transpose(
                    ps[:, (k % 4) * 128:(k % 4) * 128 + 128],
                    ls[:, k * 128:(k + 1) * 128], ident[:]))
        with ctx.stage():
            for k in range(8):
                ps = psA if k < 4 else psB
                src = lambda k=k, ps=ps: ps[:, (k % 4) * 128:(k % 4) * 128 + 128]
                if xfm is not None:
                    ctx.op("dve", lambda e, k=k, t=t, src=src: e.tensor_copy(
                        xfm[:, k * TPC + t * 128: k * TPC + (t + 1) * 128], src()))
                if x32 is not None:
                    eng = "act" if xfm is not None else ("dve" if k % 2 else "act")
                    ctx.op(eng, _cp(eng,
                        lambda k=k, x32=x32: x32[:, k * 128:(k + 1) * 128], src))
            if write_scr is not None:
                write_scr(t, ls)
            if lnbf_dram is not None:
                ctx.op("pool", lambda e, t=t, r=r, ls=ls: e.tensor_copy(
                    ctx.sb["lnbf"][:, r * 1024:(r + 1) * 1024], ls))
        if lnbf_dram is not None:
            with ctx.stage():
                ctx.dma("pool", "qp", lambda e, t=t, r=r: e.dma_start(
                    lnbf_dram[t * 128:(t + 1) * 128, :],
                    ctx.sb["lnbf"][:, r * 1024:(r + 1) * 1024]))
        if router_sb is None:
            continue
        with ctx.stage():
            for k in range(8):
                ctx.op("pe", lambda e, k=k, x32=x32: e.matmul(
                    psL[0:8, 0:128], router_sb[:, k * 8:(k + 1) * 8],
                    x32[:, k * 128:(k + 1) * 128],
                    start=(k == 0), stop=(k == 7)))
        with ctx.stage():
            if br_sb is not None:
                ctx.op("dve", lambda e, t=t: e.tensor_scalar_add(
                    logits_fm[0:8, t * 128:(t + 1) * 128], psL[0:8, 0:128],
                    br_sb[0:8, 0:1]))
            else:
                ctx.op("act", lambda e, t=t: e.copy(
                    logits_fm[0:8, t * 128:(t + 1) * 128], psL[0:8, 0:128]))


def emit_row_via_dram(ctx, src_sb_slice, scr_dram, dst_ap, src_shape, rearr=None):
    with ctx.stage():
        ctx.dma("sq", "qs", lambda e: e.dma_start(scr_dram[:src_shape[0], :src_shape[1]],
                                                  src_sb_slice))
    with ctx.stage():
        rd = rearr(scr_dram) if rearr is not None else scr_dram[:src_shape[0], :src_shape[1]]
        ctx.dma("sq", "qs", lambda e, rd=rd: e.dma_start(dst_ap, rd))


def emit_moe_dense(ctx, layer, hin_tile, hout_dram, w1_all=None, w2_all=None,
                   b1_sb=None, b2row_sb=None, selrow_fn=None, psum=None,
                   resid_scale=0.5):
    p0, p1, p2, p3, p4, p5, p6, p7 = psum
    quadA = (p0, p1, p2, p3)
    quadB = (p4, p5, p6, p7)
    xfm = ctx.sb["xfm"]
    hbuf = ctx.sb["hbuf"]
    w1buf = ctx.sb["w1buf"]
    w2buf = ctx.sb["w2buf"]
    yacc = ctx.sb["yacc"]
    mbc = ctx.sb["maskbc"]
    ytmp = ctx.sb["ytmp"]
    ones1 = ctx.sb["ones1"]
    xres = ctx.sb["xrot"]
    resrot = ctx.sb["lnrot"]
    act_fn = AF.Gelu if layer == 0 else AF.Relu

    def mm1_dma(e, f):
        ctx.dma("pool", "qp", lambda en, f=f, e=e: en.dma_start(
            w1buf[:, (f % 2) * 1024:(f % 2) * 1024 + 1024].rearrange(
                "p (k f) -> p k f", f=128),
            w1_all[e, :, f * 128:(f + 1) * 128].rearrange("(k p) f -> p k f", p=128)))

    def mm1_mms(tok0, f):
        pa, pb = (quadA[0], quadA[1]) if f % 2 == 0 else (quadA[2], quadA[3])
        for k in range(8):
            for g in range(2):
                psx = pa if g == 0 else pb
                ctx.op("pe", lambda en, k=k, g=g, f=f, psx=psx: en.matmul(
                    psx[:],
                    w1buf[:, (f % 2) * 1024 + k * 128:(f % 2) * 1024 + (k + 1) * 128],
                    xfm[:, k * TPC + tok0 + g * 512: k * TPC + tok0 + (g + 1) * 512],
                    start=(k == 0), stop=(k == 7)))

    def mm1_evac(e, f):
        pa, pb = (quadA[0], quadA[1]) if f % 2 == 0 else (quadA[2], quadA[3])
        bias = None if layer == 0 else b1_sb[:, e * 32 + f: e * 32 + f + 1]
        if bias is None:
            ctx.op("act", lambda en, f=f, pa=pa: en.activation(
                hbuf[:, f * 1024: f * 1024 + 512], pa[:], act_fn))
            ctx.op("act", lambda en, f=f, pb=pb: en.activation(
                hbuf[:, f * 1024 + 512: (f + 1) * 1024], pb[:], AF.Gelu))
        else:
            ctx.op("act", lambda en, f=f, pa=pa, bias=bias: en.activation(
                hbuf[:, f * 1024: f * 1024 + 512], pa[:], act_fn, bias=bias))
            ctx.op("dve", lambda en, f=f, pb=pb, bias=bias: en.tensor_scalar(
                hbuf[:, f * 1024 + 512: (f + 1) * 1024], pb[:],
                bias, 0.0, ALU.add, ALU.max))

    for half in range(2):
        tok0 = half * 1024
        for e in range(NE):
            selrow = ctx.sb["selrow"]
            selrow_fn(ctx, e, half)
            with ctx.stage():
                for g in range(2):
                    psx = p4 if g == 0 else p7
                    ctx.op("pe", lambda e2, e=e, g=g, psx=psx: e2.matmul(
                        psx[:], ones1[0:1, 0:128],
                        selrow[0:1, g * 512:(g + 1) * 512], start=True, stop=True))
            with ctx.stage():
                ctx.op("dve", lambda e2: e2.tensor_copy(mbc[:, 0:512], p4[:]))
                ctx.op("act", lambda e2: e2.copy(mbc[:, 512:1024], p7[:]))
            for f in range(-1, 33):
                with ctx.stage():
                    if f + 1 < 32:
                        mm1_dma(e, f + 1)
                    if 0 <= f < 32:
                        mm1_mms(tok0, f)
                    if 1 <= f + 0 and f - 1 >= 0 and f - 1 < 32 and f >= 1:
                        mm1_evac(e, f - 1)
            last_of_dp = 3 if layer == 0 else -2
            steps = [(dp, kq) for dp in range(4) for kq in range(4)]
            for si in range(-1, len(steps) + 1):
                with ctx.stage():
                    if si + 1 < len(steps):
                        dp_n, kq_n = steps[si + 1]
                        ctx.dma("pool", "qp", lambda en, dp=dp_n, kq=kq_n, e=e:
                                en.dma_start(
                            w2buf[:, (kq % 2) * 2048:(kq % 2) * 2048 + 2048].rearrange(
                                "p (k c) -> p k c", c=256),
                            w2_all[e, kq * 1024:(kq + 1) * 1024,
                                   dp * 256:(dp + 1) * 256].rearrange(
                                "(k p) c -> p k c", p=128)))
                        if kq_n == 0 and layer == 1:
                            ctx.dma("sq", "qs", lambda e2, e=e, dp=dp_n: e2.dma_start(
                                ctx.sb["b2e_sb"][0:1, 0:256],
                                b2row_sb[0:1, e * 1024 + dp * 256:
                                         e * 1024 + (dp + 1) * 256]))
                    if 0 <= si < len(steps):
                        dp, kq = steps[si]
                        quad = quadA if dp % 2 == 0 else quadB
                        for k8 in range(8):
                            k = kq * 8 + k8
                            for dg in range(4):
                                dsel, g = dg // 2, dg % 2
                                psx = quad[dg]
                                st = (k == 0)
                                sp = (k == 31 and layer == 0)
                                ctx.op("pe", lambda en, k=k, kq=kq, dsel=dsel, g=g,
                                       psx=psx, st=st, sp=sp: en.matmul(
                                    psx[:],
                                    w2buf[:, (kq % 2) * 2048 + k8_off(k, dsel):
                                          (kq % 2) * 2048 + k8_off(k, dsel) + 128],
                                    hbuf[:, k * 1024 + g * 512: k * 1024 + (g + 1) * 512],
                                    start=st, stop=sp))
                        if kq == 3 and layer == 1:
                            for dg in range(4):
                                dsel, g = dg // 2, dg % 2
                                psx = quad[dg]
                                d = dp * 2 + dsel
                                ctx.op("pe", lambda e2, d=d, dsel=dsel, psx=psx: e2.matmul(
                                    psx[:], ctx.sb["b2e_sb"][0:1, dsel * 128:
                                                             (dsel + 1) * 128],
                                    ones1[0:1, 0:512], start=False, stop=True,
                                    skip_group_check=True))
                    if 0 <= si - 0 and si >= 4 and (si % 4 == 0 or si == len(steps)):
                        dp_e = steps[si - 1][0] if si <= len(steps) - 1 else 3
                        dp_e = (si // 4) - 1
                        quad = quadA if dp_e % 2 == 0 else quadB
                        for dg in range(4):
                            dsel, g = dg // 2, dg % 2
                            psx = quad[dg]
                            d = dp_e * 2 + dsel
                            ysl = yacc[:, d * 1024 + g * 512: d * 1024 + (g + 1) * 512]
                            if e == 0:
                                ctx.op("dve", lambda e2, psx=psx, ysl=ysl, g=g:
                                       e2.tensor_mul(ysl, psx[:],
                                                     mbc[:, g * 512:(g + 1) * 512]))
                            else:
                                tsl = ytmp[:, dg * 512:(dg + 1) * 512]
                                ctx.op("dve", lambda e2, psx=psx, tsl=tsl, g=g:
                                       e2.tensor_mul(tsl, psx[:],
                                                     mbc[:, g * 512:(g + 1) * 512]))
                    if e > 0 and si >= 6 and si % 4 == 2:
                        _moe_pool_add(ctx, yacc, ytmp, (si // 4) - 1)
                if si == len(steps) and e > 0:
                    with ctx.stage():
                        _moe_pool_add(ctx, yacc, ytmp, 3)
        for tb in range(0, 10):
            with ctx.stage():
                if tb < 8:
                    t_g = half * 8 + tb
                    ctx.dma("sq", "qs", lambda e2, t=t_g, tb=tb: e2.dma_start(
                        xres[:, (tb % 2) * 1024:(tb % 2) * 1024 + 1024],
                        hin_tile(t)))
                    pA, pB = (p4, p5) if tb % 2 == 0 else (p6, p7)
                    for d in range(8):
                        psx = pA if d < 4 else pB
                        ctx.op("pe", lambda e2, d=d, tb=tb, psx=psx: e2.transpose(
                            psx[:, (d % 4) * 128:(d % 4) * 128 + 128],
                            yacc[:, d * 1024 + tb * 128: d * 1024 + (tb + 1) * 128],
                            ctx.sb["ident"][:]))
                if 1 <= tb <= 8:
                    tbx = tb - 1
                    pA, pB = (p4, p5) if tbx % 2 == 0 else (p6, p7)
                    for d in range(8):
                        psx = pA if d < 4 else pB
                        ctx.op("dve", lambda e2, d=d, tbx=tbx, psx=psx:
                               e2.scalar_tensor_tensor(
                            resrot[:, (tbx % 2) * 1024 + d * 128:
                                   (tbx % 2) * 1024 + (d + 1) * 128],
                            psx[:, (d % 4) * 128:(d % 4) * 128 + 128],
                            resid_scale,
                            xres[:, (tbx % 2) * 1024 + d * 128:
                                 (tbx % 2) * 1024 + (d + 1) * 128],
                            ALU.mult, ALU.add))
                if 2 <= tb <= 9:
                    tbx = tb - 2
                    t_g = half * 8 + tbx
                    ctx.dma("act", "qa", lambda e2, t=t_g, tbx=tbx: e2.dma_start(
                        hout_dram[t * 128:(t + 1) * 128, :],
                        resrot[:, (tbx % 2) * 1024:(tbx % 2) * 1024 + 1024]))


def k8_off(k, dsel):
    return (k % 8) * 256 + dsel * 128


def _moe_pool_add(ctx, yacc, ytmp, dp_e):
    for dg in range(4):
        dsel, g = dg // 2, dg % 2
        d = dp_e * 2 + dsel
        ysl = yacc[:, d * 1024 + g * 512: d * 1024 + (g + 1) * 512]
        ctx.op("pool", lambda e2, dg=dg, ysl=ysl: e2.tensor_add(
            ysl, ysl, ytmp[:, dg * 512:(dg + 1) * 512]))




def emit_ln_fused(ctx, x_tiles_dma, router_sb, psT4, psL2, br_sb, logits_fm,
                  n_tiles=NT, write_scr=None, lnbf_dram=None):
    a = ctx.alloc
    if "lnf_xr" not in ctx.sb:
        a("lnf_xr", [128, 8 * 1024], F32)
        a("lnf_ln", [128, 2 * 1024], F32)
        a("lnf_sq", [128, 1024], F32)
        a("lnf_eps", [128, 1], F32)
    xr = ctx.sb["lnf_xr"]; lnr = ctx.sb["lnf_ln"]; sqb = ctx.sb["lnf_sq"]
    eps = ctx.sb["lnf_eps"]
    sm = ctx.sb["smal"]; ss = ctx.sb["ssal"]
    mean = ctx.sb["mean"]; var = ctx.sb["var"]; rstd = ctx.sb["rstd"]
    msq = ctx.sb["nbias"]
    xfm = ctx.sb.get("xfm")
    x32 = ctx.sb.get("x32rot") if router_sb is not None else None
    ident = ctx.sb["ident"]
    xs = lambda k: xr[:, (k % 8) * 1024:(k % 8) * 1024 + 1024]
    ls = lambda k: lnr[:, (k % 2) * 1024:(k % 2) * 1024 + 1024]
    with ctx.stage():
        ctx.op("dve", lambda e: e.memset(eps[:], 0.0))
        ctx.dma("sq", "qs", lambda e: e.dma_start(xs(0), x_tiles_dma(0)))
        ctx.dma("sq", "qs", lambda e: e.dma_start(xs(1), x_tiles_dma(1)))
    with ctx.stage():
        ctx.op("dve", lambda e: e.tensor_scalar_add(eps[:], eps[:], float(LN_EPS)))
    NTT = n_tiles
    for t in range(NTT + 8):
        with ctx.stage():
            if t + 2 < NTT:
                ctx.dma("sq", "qs", lambda e, t=t: e.dma_start(
                    xs(t + 2), x_tiles_dma(t + 2)))
            if t < NTT:
                ctx.op("dve", lambda e, t=t: e.tensor_mul(sqb[:], xs(t), xs(t)))
                ctx.op("dve", lambda e, t=t: e.reduce_sum(ss[:, t:t + 1], sqb[:],
                                                          axis=AX.X))
                ctx.op("dve", lambda e, t=t: e.reduce_sum(sm[:, t:t + 1], xs(t),
                                                          axis=AX.X))
            if 1 <= t <= NTT:
                k = t - 1
                ctx.op("dve", lambda e, k=k: e.tensor_scalar_mul(
                    mean[:, k:k + 1], sm[:, k:k + 1], 1.0 / D))
            if 2 <= t <= NTT + 1:
                k = t - 2
                ctx.op("dve", lambda e, k=k: e.scalar_tensor_tensor(
                    var[:, k:k + 1], ss[:, k:k + 1], 1.0 / D, msq[:, k:k + 1],
                    ALU.mult, ALU.add))
            if 4 <= t <= NTT + 3:
                k = t - 4
                ctx.op("dve", lambda e, k=k: e.reciprocal(rstd[:, k:k + 1],
                                                          rstd[:, k:k + 1]))
            if 6 <= t <= NTT + 5:
                k = t - 6
                pr = psT4[(k % 2) * 2:(k % 2) * 2 + 2]
                for kk in range(8):
                    psx = pr[0] if kk < 4 else pr[1]
                    ctx.op("pe", lambda e, kk=kk, k=k, psx=psx: e.transpose(
                        psx[:, (kk % 4) * 128:(kk % 4) * 128 + 128],
                        ls(k)[:, kk * 128:(kk + 1) * 128], ident[:]))
            if router_sb is not None and 7 <= t <= NTT + 6:
                k = t - 7
                psl = psL2[k % 2]
                for kk in range(8):
                    ctx.op("pe", lambda e, kk=kk, k=k, psl=psl: e.matmul(
                        psl[0:8, 0:128], router_sb[:, kk * 8:(kk + 1) * 8],
                        x32[:, (k % 2) * 1024 + kk * 128:
                            (k % 2) * 1024 + (kk + 1) * 128],
                        start=(kk == 0), stop=(kk == 7)))
        with ctx.stage():
            if 1 <= t <= NTT:
                k = t - 1
                ctx.op("dve", lambda e, k=k: e.scalar_tensor_tensor(
                    msq[:, k:k + 1], mean[:, k:k + 1], -1.0, mean[:, k:k + 1],
                    ALU.mult, ALU.mult))
            if 3 <= t <= NTT + 2:
                k = t - 3
                ctx.op("act", lambda e, k=k: e.activation(
                    rstd[:, k:k + 1], var[:, k:k + 1], AF.Sqrt,
                    bias=eps[:, 0:1]))
            if 5 <= t <= NTT + 4:
                k = t - 5
                ctx.op("dve", lambda e, k=k: e.tensor_scalar(
                    ls(k), xs(k), mean[:, k:k + 1], rstd[:, k:k + 1],
                    ALU.subtract, ALU.mult))
            if 6 <= t <= NTT + 5:
                k = t - 6
                pr = psT4[(k % 2) * 2:(k % 2) * 2 + 2]
                for kk in range(8):
                    psx = pr[0] if kk < 4 else pr[1]
                    srcp = lambda kk=kk, psx=psx: psx[:, (kk % 4) * 128:
                                                      (kk % 4) * 128 + 128]
                    if xfm is not None:
                        ctx.op("dve", lambda e, kk=kk, k=k, srcp=srcp: e.tensor_copy(
                            xfm[:, kk * TPC + k * 128: kk * TPC + (k + 1) * 128],
                            srcp()))
                    if x32 is not None:
                        ctx.op("dve", lambda e, kk=kk, k=k, srcp=srcp: e.tensor_copy(
                            x32[:, (k % 2) * 1024 + kk * 128:
                                (k % 2) * 1024 + (kk + 1) * 128], srcp()))
                if lnbf_dram is not None:
                    ctx.op("pool", lambda e, k=k: e.tensor_copy(
                        ctx.sb["lnbf"][:, (k % 2) * 1024:(k % 2) * 1024 + 1024],
                        ls(k)))
                if write_scr is not None:
                    write_scr(k, ls(k))
            if lnbf_dram is not None and 7 <= t <= NTT + 6:
                k = t - 7
                ctx.dma("pool", "qp", lambda e, k=k: e.dma_start(
                    lnbf_dram[k * 128:(k + 1) * 128, :],
                    ctx.sb["lnbf"][:, (k % 2) * 1024:(k % 2) * 1024 + 1024]))
            if router_sb is not None and 8 <= t <= NTT + 7:
                k = t - 8
                psl = psL2[k % 2]
                if br_sb is not None:
                    ctx.op("dve", lambda e, k=k, psl=psl: e.tensor_scalar_add(
                        logits_fm[0:8, k * 128:(k + 1) * 128], psl[0:8, 0:128],
                        br_sb[0:8, 0:1]))
                else:
                    ctx.op("dve", lambda e, k=k, psl=psl: e.tensor_copy(
                        logits_fm[0:8, k * 128:(k + 1) * 128], psl[0:8, 0:128]))


def emit_router_l1(ctx, logits_fm, psT, scr_dram):
    ident = ctx.sb["ident"]
    ltm = ctx.sb["ltm"]; mxall = ctx.sb["mxall"]; miall = ctx.sb["miall"]
    idxf = ctx.sb["idxf"]; idx_row = ctx.sb["idx_row"]
    sb16 = ctx.sb["ltm"]
    with ctx.stage():
        for t in range(NT):
            ctx.op("pe", lambda e, t=t: e.transpose(
                psT[:, t * 8:(t + 1) * 8], logits_fm[0:8, t * 128:(t + 1) * 128],
                ident[0:8, 0:8]))
    with ctx.stage():
        ctx.op("dve", lambda e: e.tensor_copy(ltm[:, 0:128], psT[:, 0:128]))
    with ctx.stage():
        for t in range(NT):
            ctx.op("dve", lambda e, t=t: e.max(
                mxall[:, t * 8:(t + 1) * 8], ltm[:, t * 8:(t + 1) * 8]))
    with ctx.stage():
        for t in range(NT):
            ctx.op("dve", lambda e, t=t: e.max_index(
                miall[:, t * 8:(t + 1) * 8], mxall[:, t * 8:(t + 1) * 8],
                ltm[:, t * 8:(t + 1) * 8]))
    with ctx.stage():
        for t in range(NT):
            ctx.op("dve", lambda e, t=t: e.tensor_copy(
                idxf[:, t:t + 1], miall[:, t * 8:t * 8 + 1]))
    with ctx.stage():
        ctx.op("pe", lambda e: e.transpose(psT[0:16, 128:256], idxf[:, 0:16], ident[:]))
    with ctx.stage():
        ctx.op("dve", lambda e: e.tensor_copy(sb16[0:16, 0:128], psT[0:16, 128:256]))
    with ctx.stage():
        ctx.dma("sq", "qs", lambda e: e.dma_start(scr_dram[0:16, 0:128], sb16[0:16, 0:128]))
    with ctx.stage():
        ctx.dma("sq", "qs", lambda e: e.dma_start(
            idx_row[0:1, 0:2048].rearrange("a (t p) -> a t p", p=128),
            scr_dram[0:16, 0:128].rearrange("(o t) p -> o t p", o=1)))



def emit_softmax_probs(ctx, logits_fm, psT, probs_tm):
    ident = ctx.sb["ident"]
    ltm = ctx.sb["ltm"]
    rmax = ctx.sb["rmax"]; rsum = ctx.sb["rsum"]
    with ctx.stage():
        for t in range(NT):
            ctx.op("pe", lambda e, t=t: e.transpose(
                psT[:, t * 8:(t + 1) * 8], logits_fm[0:8, t * 128:(t + 1) * 128],
                ident[0:8, 0:8]))
    with ctx.stage():
        ctx.op("dve", lambda e: e.tensor_copy(ltm[:, 0:128], psT[:, 0:128]))
    with ctx.stage():
        ctx.op("dve", lambda e: e.tensor_reduce(
            rmax[:, 0:NT], ltm[:, 0:128].rearrange("p (t e) -> p t e", e=8),
            AX.X, ALU.max))
    with ctx.stage():
        ctx.op("dve", lambda e: e.tensor_scalar_mul(rmax[:, 0:NT], rmax[:, 0:NT], -1.0))
    with ctx.stage():
        for t in range(NT):
            ctx.op("act", lambda e, t=t: e.activation(
                probs_tm[:, t * 8:(t + 1) * 8], ltm[:, t * 8:(t + 1) * 8],
                AF.Exp, bias=rmax[:, t:t + 1]))
    with ctx.stage():
        ctx.op("dve", lambda e: e.tensor_reduce(
            rsum[:, 0:NT], probs_tm[:, 0:128].rearrange("p (t e) -> p t e", e=8),
            AX.X, ALU.add))
    with ctx.stage():
        ctx.op("dve", lambda e: e.reciprocal(rsum[:, 0:NT], rsum[:, 0:NT]))
    with ctx.stage():
        for t in range(NT):
            ctx.op("dve", lambda e, t=t: e.tensor_scalar_mul(
                probs_tm[:, t * 8:(t + 1) * 8], probs_tm[:, t * 8:(t + 1) * 8],
                rsum[:, t:t + 1]))


def emit_bisect_l0(ctx, pg, psB, k_val, n_iter=32):
    lo = ctx.sb["lo"]; hi = ctx.sb["hi"]; mid = ctx.sb["mid"]
    cmp_ = ctx.sb["cmp_scr"]; rs = ctx.sb["rs"]; cnt8 = ctx.sb["cnt8"]
    gsel = ctx.sb["gsel"]
    bd16 = ctx.sb["bd16"]; bd16T = ctx.sb["bd16T"]
    with ctx.stage():
        ctx.op("dve", lambda e: e.memset(lo[:], 0.0))
        ctx.op("dve", lambda e: e.memset(hi[:], 0.0))
    with ctx.stage():
        ctx.op("dve", lambda e: e.tensor_scalar_add(hi[:], hi[:], 1.0))
    for it in range(n_iter):
        with ctx.stage():
            ctx.op("dve", lambda e: e.tensor_add(mid[:], lo[:], hi[:]))
        with ctx.stage():
            ctx.op("dve", lambda e: e.tensor_scalar_mul(mid[:], mid[:], 0.5))
        with ctx.stage():
            ctx.op("dve", lambda e: e.tensor_single_scalar(
                cmp_[:], pg[:], mid[:, 0:1], ALU.is_ge))
        with ctx.stage():
            ctx.op("dve", lambda e: e.reduce_sum(rs[:], cmp_[:], axis=AX.X))
        with ctx.stage():
            ctx.op("pe", lambda e: e.matmul(psB[0:8, 0:1], bd16[:, 0:8], rs[:, 0:1],
                                            start=True, stop=True))
        with ctx.stage():
            ctx.op("dve", lambda e: e.tensor_copy(cnt8[0:8, 0:1], psB[0:8, 0:1]))
        with ctx.stage():
            ctx.op("pe", lambda e: e.matmul(psB[0:128, 4:5], bd16T[0:8, 0:128],
                                            cnt8[0:8, 0:1], start=True, stop=True))
        with ctx.stage():
            ctx.op("dve", lambda e: e.tensor_single_scalar(
                gsel[:], psB[0:128, 4:5], float(k_val), ALU.is_ge))
        with ctx.stage():
            ctx.op("dve", lambda e: e.select(lo[:], gsel[:], mid[:], lo[:]))
            ctx.op("dve", lambda e: e.tensor_copy(ctx.sb["hi_t"][:], hi[:]))
        with ctx.stage():
            ctx.op("dve", lambda e: e.select(hi[:], gsel[:], ctx.sb["hi_t"][:],
                                             mid[:]))



def _dram(nc, io, pref, name, shape, dt, kind):
    if io is not None:
        if name in io:
            return io[name]
        return nc.dram_tensor(pref + name, list(shape), dt).ap()
    if kind == "Internal":
        return nc.dram_tensor(name, list(shape), dt).ap()
    return nc.dram_tensor(name, list(shape), dt, kind=kind).ap()


def _ctx_finish(ctx):
    sems = dict(ctx.cnt)

    def body(eng):
        for s, v in sems.items():
            if v > ctx.W["sq"][s]:
                eng.wait_ge(ctx.sem[s], v)
                ctx.W["sq"][s] = v
    ctx.block.sync(body)
    ctx.st.close()


def _alloc_moe(ctx, layer, gathered=False):
    a = ctx.alloc
    a("x32rot", [128, 2048], F32)
    for n in ("smal", "ssal", "mean", "var", "rstd", "nbias"):
        a(n, [128, 16], F32)
    if not gathered:
        a("xfm", [128, 8 * TPC], BF16)
        a("hbuf", [128, 32 * 1024], BF16)
        a("w1buf", [128, 2048], BF16)
        a("w2buf", [128, 4096], BF16)
        a("yacc", [128, 8192], F32)
        a("ytmp", [128, 2048], F32)
        a("maskbc", [128, 1024], F32)
        a("selrow", [1, 1024], F32)
    else:
        a("lnbf", [128, 2048], BF16)
        a("identb", [128, 128], BF16)
        a("Lst_sb", [128, 128], F32)
    a("router_sb", [128, 64], F32)
    a("logits_fm", [8, 2048], F32)
    a("ltm", [128, 128], F32)
    a("ones1", [1, 512], F32)
    a("ident", [128, 128], F32)
    if layer == 1:
        a("mxall", [128, 128], F32)
        a("miall", [128, 128], U32)
        a("idxf", [128, 16], F32)
        a("idx_row", [1, 2048], F32)
        a("b1c_sb", [128, 256], F32)
        if gathered:
            a("b2c_sb", [128, 64], F32)
        else:
            a("b2e_sb", [1, 256], F32)
    else:
        a("rmax", [128, 16], F32)
        a("rsum", [128, 16], F32)
        a("probs_tm", [128, 128], F32)
        if gathered:
            a("selwh", [128, 128], F32)


def _psum8(ctx):
    return [ctx.alloc_psum("pp%d" % i, [128, 512], F32) for i in range(8)]


def build_moe1(ctx=None, io=None, pref="", gathered=True):
    standalone = ctx is None
    nc = bass.Bass("TRN2", target_bir_lowering=False) if standalone else ctx.nc
    dram = lambda name, shape, dt, kind="ExternalInput": _dram(nc, io, pref, name, shape, dt, kind)
    hin = dram("hin", [TPC, D], F32, "ExternalInput")
    hout = dram("hout", [TPC, D], F32, "ExternalOutput")
    wrT = dram("wrT", [1024, 8], F32, "ExternalInput")
    br = dram("br", [8, 1], F32, "ExternalInput")
    b1c = dram("b1c", [128, 256], F32, "ExternalInput")
    identD = dram("identD", [128, 128], F32, "ExternalInput")
    onesD = dram("onesD", [1, 512], F32, "ExternalInput")
    scr = dram("scr", [128, 128], F32, "Internal")
    if gathered:
        w1h = dram("w1h_l1", [NE, 32, 128, 8, 128], F32, "ExternalInput")
        w2h = dram("w2h_l1", [NE, 8, 128, 32, 128], F32, "ExternalInput")
        LstD = dram("LstD", [128, 128], F32, "ExternalInput")
        b2cD = dram("b2c_l1", [128, 64], F32, "ExternalInput")
        lnscr = dram("lnscr", [TPC, 1024], BF16, "Internal")
        gbuf = dram("gbuf", [8 * L1CAP, 1024], BF16, "Internal")
        ybuf = dram("ybuf", [8 * L1CAP, 1024], BF16, "Internal")
    else:
        w1 = dram("w1", [NE, 1024, 4096], F32, "ExternalInput")
        w2 = dram("w2", [NE, 4096, 1024], F32, "ExternalInput")
        b2r = dram("b2r", [1, 8192], F32, "ExternalInput")

    if standalone:
        ctx = Ctx(nc)
    ctx.push_phase(pref)
    _alloc_moe(ctx, 1, gathered)
    if gathered:
        ps = [ctx.alloc_psum("pp%d" % i, [128, 512], F32) for i in range(6)]
        ctx.psb = [ctx.alloc_psum("ppb%d" % i, [128, 1024], BF16) for i in range(2)]
        psA, psB, psL = ps[4], ps[5], ps[0]
    else:
        ps = _psum8(ctx)
        psA, psB, psL = ps[5], ps[6], ps[7]
    with ctx.stage():
        ctx.dma("sq", "qs", lambda e: e.dma_start(ctx.sb["ident"][:], identD[:]))
        ctx.dma("sq", "qs", lambda e: e.dma_start(ctx.sb["ones1"][:], onesD[:]))
        ctx.dma("sq", "qs", lambda e: e.dma_start(
            ctx.sb["router_sb"][:, 0:64].rearrange("p (k e) -> p k e", e=8),
            wrT.rearrange("(k p) e -> p k e", p=128)))
        ctx.dma("sq", "qs", lambda e: e.dma_start(ctx.sb["b1c_sb"][:], b1c[:]))
        if gathered:
            ctx.dma("sq", "qs", lambda e: e.dma_start(ctx.sb["Lst_sb"][:], LstD[:]))
            ctx.dma("sq", "qs", lambda e: e.dma_start(ctx.sb["b2c_sb"][:], b2cD[:]))
            ctx.dma("pool", "qp", lambda e: e.dma_start(ctx.sb["identb"][:],
                                                        identD[:]))
    br_sb = ctx.alloc("br_sb", [8, 1], F32)
    with ctx.stage():
        ctx.dma("sq", "qs", lambda e: e.dma_start(br_sb[0:8, 0:1], br[:, 0:1]))
    hin_tile = lambda t: hin[t * 128:(t + 1) * 128, :]
    if gathered:
        emit_ln_fused(ctx, hin_tile, ctx.sb["router_sb"],
                      [ps[2], ps[3], ps[4], ps[5]], [ps[0], ps[1]],
                      br_sb, ctx.sb["logits_fm"], lnbf_dram=lnscr)
    else:
        emit_ln_stats(ctx, hin_tile)
        emit_ln_apply_tp(ctx, hin_tile, ctx.sb["router_sb"], psA, psB, psL,
                         br_sb, ctx.sb["logits_fm"], lnbf_dram=None)
    emit_router_l1(ctx, ctx.sb["logits_fm"], psL, scr)
    if gathered:
        emit_moe_maps(ctx, 1, L1CAP, ps[5])
        emit_moe_gath(ctx, 1, hin_tile, hout, w1h, w2h, lnscr, gbuf, ybuf,
                      b1_sb=ctx.sb["b1c_sb"], b2c_sb=ctx.sb["b2c_sb"],
                      psum=ps, cap=L1CAP, resid_scale=0.5)
    else:
        def selrow_fn(ctx2, e_, half_):
            with ctx2.stage():
                ctx2.op("dve", lambda en, e_=e_, half_=half_: en.tensor_single_scalar(
                    ctx2.sb["selrow"][0:1, 0:1024],
                    ctx2.sb["idx_row"][0:1, half_ * 1024:(half_ + 1) * 1024],
                    float(e_), ALU.is_equal))
        emit_moe_dense(ctx, 1, hin_tile, hout, w1_all=w1, w2_all=w2,
                       b1_sb=ctx.sb["b1c_sb"], b2row_sb=b2r,
                       selrow_fn=selrow_fn, psum=ps, resid_scale=0.5)
    ctx.pop_phase()
    if standalone:
        _ctx_finish(ctx)
    return nc


def build_moe0a(ctx=None, io=None, pref=""):
    standalone = ctx is None
    nc = bass.Bass("TRN2", target_bir_lowering=False) if standalone else ctx.nc
    dram = lambda name, shape, dt, kind="ExternalInput": _dram(nc, io, pref, name, shape, dt, kind)
    hin = dram("hin", [TPC, D], F32, "ExternalInput")
    probs_out = dram("probs_out", [TPC, 8], F32, "ExternalOutput")
    gT = dram("gT", [1024, 8], F32, "ExternalInput")
    identD = dram("identD", [128, 128], F32, "ExternalInput")

    if standalone:
        ctx = Ctx(nc)
    ctx.push_phase(pref)
    a = ctx.alloc
    a("x32rot", [128, 2048], F32)
    for n in ("smal", "ssal", "mean", "var", "rstd", "nbias"):
        a(n, [128, 16], F32)
    a("router_sb", [128, 64], F32)
    a("logits_fm", [8, 2048], F32)
    a("ltm", [128, 128], F32)
    a("rmax", [128, 16], F32)
    a("rsum", [128, 16], F32)
    a("probs_tm", [128, 128], F32)
    a("ident", [128, 128], F32)
    ps = _psum8(ctx)
    with ctx.stage():
        ctx.dma("sq", "qs", lambda e: e.dma_start(ctx.sb["ident"][:], identD[:]))
        ctx.dma("sq", "qs", lambda e: e.dma_start(
            ctx.sb["router_sb"][:, 0:64].rearrange("p (k e) -> p k e", e=8),
            gT.rearrange("(k p) e -> p k e", p=128)))
    hin_tile = lambda t: hin[t * 128:(t + 1) * 128, :]
    lnscr_ap = io.get("lnscr") if io is not None else None
    if lnscr_ap is not None:
        a("lnbf", [128, 2048], BF16)
    emit_ln_fused(ctx, hin_tile, ctx.sb["router_sb"],
                  [ps[4], ps[5], ps[6], ps[7]], [ps[2], ps[3]],
                  None, ctx.sb["logits_fm"], lnbf_dram=lnscr_ap)
    emit_softmax_probs(ctx, ctx.sb["logits_fm"], ps[7], ctx.sb["probs_tm"])
    with ctx.stage():
        ctx.dma("sq", "qs", lambda e: e.dma_start(
            probs_out.rearrange("(t p) e -> p t e", p=128),
            ctx.sb["probs_tm"][:, 0:128].rearrange("p (t e) -> p t e", e=8)))
    ctx.pop_phase()
    if standalone:
        _ctx_finish(ctx)
    return nc


def build_moe0b(ctx=None, io=None, pref="", gathered=True):
    standalone = ctx is None
    nc = bass.Bass("TRN2", target_bir_lowering=False) if standalone else ctx.nc
    dram = lambda name, shape, dt, kind="ExternalInput": _dram(nc, io, pref, name, shape, dt, kind)
    hin = dram("hin", [TPC, D], F32, "ExternalInput")
    hout = dram("hout", [TPC, D], F32, "ExternalOutput")
    pglob = dram("pglob", [8, T], F32, "ExternalInput")
    ln_pre = bool(io.get("ln_pre")) if io is not None else False
    if gathered:
        w1h = dram("w1h_l0", [NE, 32, 128, 8, 128], F32, "ExternalInput")
        w2h = dram("w2h_l0", [NE, 8, 128, 32, 128], F32, "ExternalInput")
        LstD = dram("LstD", [128, 128], F32, "ExternalInput")
        lnscr = dram("lnscr", [TPC, 1024], BF16, "Internal")
        gbuf = dram("gbuf", [8 * 640, 1024], BF16, "Internal")
        ybuf = dram("ybuf", [8 * 640, 1024], BF16, "Internal")
    else:
        w1 = dram("w1", [NE, 1024, 4096], F32, "ExternalInput")
        w2 = dram("w2", [NE, 4096, 1024], F32, "ExternalInput")
    identD = dram("identD", [128, 128], F32, "ExternalInput")
    onesD = dram("onesD", [1, 512], F32, "ExternalInput")
    bdD = dram("bdD", [128, 8], F32, "ExternalInput")
    bdTD = dram("bdTD", [8, 128], F32, "ExternalInput")
    coff = dram("coff", [1, 1], F32, "ExternalInput")
    scr = dram("scr", [128, 128], F32, "Internal")
    if standalone:
        ctx = Ctx(nc)
    ctx.push_phase(pref)
    _alloc_moe(ctx, 0, gathered)
    a = ctx.alloc
    a("pg", [128, 512], F32)
    a("cmp_scr", [128, 512], F32)
    for n in ("lo", "hi", "mid", "rs", "hi_t"):
        a(n, [128, 1], F32)
    a("gsel", [128, 1], I32)
    a("cnt8", [8, 1], F32)
    a("bd16", [128, 8], F32)
    a("bd16T", [8, 128], F32)
    a("thr8", [1, 8], F32)
    a("thr_bc", [128, 8], F32)
    a("selw_tm", [128, 128], F32)
    a("sel128", [128, 128], F32)
    if gathered:
        ps = [ctx.alloc_psum("pp%d" % i, [128, 512], F32) for i in range(6)]
        ctx.psb = [ctx.alloc_psum("ppb%d" % i, [128, 1024], BF16) for i in range(2)]
    else:
        ps = _psum8(ctx)
    c = None
    pown_in = dram("pown", [TPC, 8], F32, "ExternalInput")
    with ctx.stage():
        ctx.dma("sq", "qs", lambda e: e.dma_start(ctx.sb["ident"][:], identD[:]))
        ctx.dma("sq", "qs", lambda e: e.dma_start(ctx.sb["ones1"][:], onesD[:]))
        ctx.dma("sq", "qs", lambda e: e.dma_start(ctx.sb["bd16"][:], bdD[:]))
        ctx.dma("sq", "qs", lambda e: e.dma_start(ctx.sb["bd16T"][:], bdTD[:]))
        ctx.dma("sq", "qs", lambda e: e.dma_start(
            ctx.sb["pg"][:], pglob.rearrange("e (c w) -> (e c) w", w=512)))
        ctx.dma("sq", "qs", lambda e: e.dma_start(
            ctx.sb["probs_tm"][:, 0:128].rearrange("p (t e) -> p t e", e=8),
            pown_in.rearrange("(t p) e -> p t e", p=128)))
    hin_tile = lambda t: hin[t * 128:(t + 1) * 128, :]
    if gathered:
        with ctx.stage():
            ctx.dma("sq", "qs", lambda e: e.dma_start(ctx.sb["Lst_sb"][:], LstD[:]))
            ctx.dma("pool", "qp", lambda e: e.dma_start(ctx.sb["identb"][:],
                                                        identD[:]))
    psX = ps[0] if gathered else ps[7]
    psA_, psB_ = (ps[4], ps[5]) if gathered else (ps[5], ps[6])
    if not ln_pre:
        psT4_ = [ps[2], ps[3], ps[4], ps[5]]
        emit_ln_fused(ctx, hin_tile, None, psT4_, None, None, None,
                      lnbf_dram=lnscr if gathered else None)
    emit_bisect_l0(ctx, ctx.sb["pg"], psX, 2048)
    with ctx.stage():
        for e_ in range(8):
            ctx.dma("sq", "qs", lambda e, e_=e_: e.dma_start(
                ctx.sb["thr8"][0:1, e_:e_ + 1], ctx.sb["lo"][e_ * 16:e_ * 16 + 1, 0:1]))
    with ctx.stage():
        ctx.op("pe", lambda e: e.matmul(psX[0:128, 0:8], ctx.sb["ones1"][0:1, 0:128],
                                        ctx.sb["thr8"][0:1, 0:8], start=True, stop=True))
    with ctx.stage():
        ctx.op("dve", lambda e: e.tensor_copy(ctx.sb["thr_bc"][:], psX[0:128, 0:8]))
    with ctx.stage():
        for t in range(NT):
            ctx.op("dve", lambda e, t=t: e.tensor_tensor(
                ctx.sb["sel128"][:, t * 8:(t + 1) * 8],
                ctx.sb["probs_tm"][:, t * 8:(t + 1) * 8], ctx.sb["thr_bc"][:],
                ALU.is_ge))
    with ctx.stage():
        ctx.op("dve", lambda e: e.tensor_mul(
            ctx.sb["selw_tm"][:, 0:128], ctx.sb["sel128"][:, 0:128],
            ctx.sb["probs_tm"][:, 0:128]))
    with ctx.stage():
        if gathered:
            ctx.op("dve", lambda e: e.tensor_scalar_mul(
                ctx.sb["selwh"][:, 0:128], ctx.sb["selw_tm"][:, 0:128], 0.5))
    if gathered:
        emit_moe_maps(ctx, 0, 640, ps[5])
        emit_moe_gath(ctx, 0, hin_tile, hout, w1h, w2h, lnscr, gbuf, ybuf,
                      psum=ps, cap=640, resid_scale=0.5)
    else:
        with ctx.stage():
            ctx.op("pe", lambda e: e.transpose(ps[7][0:128, 128:256],
                                               ctx.sb["selw_tm"][:, 0:128],
                                               ctx.sb["ident"][:]))
        with ctx.stage():
            ctx.op("dve", lambda e: e.tensor_copy(ctx.sb["sel128"][:, 0:128],
                                                  ps[7][0:128, 128:256]))
        with ctx.stage():
            ctx.dma("sq", "qs", lambda e: e.dma_start(scr[:, :],
                                                      ctx.sb["sel128"][:, 0:128]))
        def selrow_fn(ctx2, e_, half_):
            with ctx2.stage():
                ctx2.dma("sq", "qs", lambda en, e_=e_, half_=half_: en.dma_start(
                    ctx2.sb["selrow"][0:1, 0:1024].rearrange("a (t p) -> a t p", p=128),
                    scr.rearrange("(t e) p -> e t p", e=8)[e_:e_ + 1,
                        half_ * 8:(half_ + 1) * 8, :]))
        emit_moe_dense(ctx, 0, hin_tile, hout, w1_all=w1, w2_all=w2,
                       selrow_fn=selrow_fn, psum=ps, resid_scale=0.5)
    ctx.pop_phase()
    if standalone:
        _ctx_finish(ctx)
    return nc



def build_attn0a(ctx=None, io=None, pref=""):
    standalone = ctx is None
    nc = bass.Bass("TRN2", target_bir_lowering=False) if standalone else ctx.nc
    dram = lambda name, shape, dt, kind="ExternalInput": _dram(nc, io, pref, name, shape, dt, kind)
    hin = dram("hin", [TPC, D], F32, "ExternalInput")
    WqT = dram("WqT", [D, D], F32, "ExternalInput")
    WkT = dram("WkT", [D, D], F32, "ExternalInput")
    WvT = dram("WvT", [D, D], F32, "ExternalInput")
    bqc = dram("bqc", [128, 8], F32, "ExternalInput")
    bkc = dram("bkc", [128, 8], F32, "ExternalInput")
    bvr = dram("bvr", [1, D], F32, "ExternalInput")
    cosD = dram("cosD", [1024, TPC], F32, "ExternalInput")
    sinD = dram("sinD", [1024, TPC], F32, "ExternalInput")
    pswD = dram("pswD", [128, 128], F32, "ExternalInput")
    identD = dram("identD", [128, 128], F32, "ExternalInput")
    onesD = dram("onesD", [1, 512], F32, "ExternalInput")
    qr_out = dram("qr_out", [D, TPC], BF16, "ExternalOutput")
    q_out = dram("q_out", [D, TPC], BF16, "ExternalOutput")
    kv_out = dram("kv_out", [128, 1024], F32, "ExternalOutput")
    km_out = dram("km_out", [128, 8], F32, "ExternalOutput")
    ln_out = dram("ln_out", [TPC, D], F32, "ExternalOutput")

    if standalone:
        ctx = Ctx(nc)
    ctx.push_phase(pref)
    a = ctx.alloc
    for n in ("smal", "ssal", "mean", "var", "rstd", "nbias"):
        a(n, [128, 16], F32)
    a("xfm", [128, 8 * TPC], BF16)
    a("ident", [128, 128], F32)
    a("identb", [128, 128], BF16)
    a("ones1", [1, 512], F32)
    a("wbuf", [128, 2 * 1024], BF16)
    a("wv_sb", [128, 8 * 1024], BF16)
    a("vtm", [128, NT * 1024], BF16)
    a("csA", [128, 2 * 2048], F32)
    a("csB", [128, 2 * 2048], F32)
    a("psw", [128, 128], F32)
    a("qpre", [128, 2048], BF16)
    a("swp", [128, 2048], BF16)
    a("ta", [128, 2048], BF16)
    a("tb", [128, 2048], BF16)
    a("ro_bf", [128, 2048], BF16)
    a("pswb", [128, 128], BF16)
    a("krTc", [128, 2048], BF16)
    a("kvp", [128, 1024], F32)
    a("kmp", [128, 8], F32)
    a("bq_sb", [128, 8], F32)
    a("bk_sb", [128, 8], F32)
    a("bv_sb", [1, 1024], F32)
    ps = [ctx.alloc_psum("pp%d" % i, [128, 512], F32) for i in range(6)]
    pb = [ctx.alloc_psum("pb%d" % i, [128, 1024], BF16) for i in range(2)]

    with ctx.stage():
        ctx.dma("sq", "qs", lambda e: e.dma_start(ctx.sb["ident"][:], identD[:]))
        ctx.dma("sq", "qs", lambda e: e.dma_start(ctx.sb["ones1"][:], onesD[:]))
        ctx.dma("sq", "qs", lambda e: e.dma_start(ctx.sb["bq_sb"][:], bqc[:]))
        ctx.dma("sq", "qs", lambda e: e.dma_start(ctx.sb["bk_sb"][:], bkc[:]))
        ctx.dma("sq", "qs", lambda e: e.dma_start(ctx.sb["bv_sb"][:], bvr[:]))
        ctx.dma("pool", "qp", lambda e: e.dma_start(ctx.sb["identb"][:], identD[:]))
        ctx.dma("sq", "qs", lambda e: e.dma_start(ctx.sb["psw"][:], pswD[:]))
        ctx.dma("pool", "qp", lambda e: e.dma_start(ctx.sb["pswb"][:], pswD[:]))
        ctx.dma("pool", "qp", lambda e: e.dma_start(
            ctx.sb["wv_sb"][:, 0:8192].rearrange("p (k c) -> p k c", c=1024),
            WvT.rearrange("(k p) c -> p k c", p=128)))

    hin_tile = lambda t: hin[t * 128:(t + 1) * 128, :]
    def write_scr(t, ls):
        ctx.dma("pool", "qp", lambda e, t=t, ls=ls: e.dma_start(
            ln_out[t * 128:(t + 1) * 128, :], ls))
    emit_ln_fused(ctx, hin_tile, None, [ps[2], ps[3], ps[4], ps[5]], None,
                  None, None, write_scr=write_scr)
    xfm = ctx.sb["xfm"]
    vtm = ctx.sb["vtm"]

    for t in range(NT + 1):
        with ctx.stage():
            if t < NT:
                for half in range(2):
                    psx = ps[(t % 2) * 2 + half]
                    for k in range(8):
                        ctx.op("pe", lambda e, k=k, t=t, half=half, psx=psx:
                               e.matmul(
                            psx[:], xfm[:, k * TPC + t * 128:k * TPC + (t + 1) * 128],
                            ctx.sb["wv_sb"][:, k * 1024 + half * 512:
                                            k * 1024 + (half + 1) * 512],
                            start=(k == 0), stop=False))
                    ctx.op("pe", lambda e, half=half, psx=psx: e.matmul(
                        psx[:], ctx.sb["ones1"][0:1, 0:128],
                        ctx.sb["bv_sb"][0:1, half * 512:(half + 1) * 512],
                        start=False, stop=True, skip_group_check=True))
            if t >= 1:
                tx = t - 1
                for half in range(2):
                    psx = ps[(tx % 2) * 2 + half]
                    ctx.op("dve", lambda e, tx=tx, half=half, psx=psx:
                           e.tensor_copy(
                        vtm[:, tx * 1024 + half * 512:tx * 1024 + (half + 1) * 512],
                        psx[:]))

    def cs_load(gi):
        tau = gi % 8
        par = (gi % 2) * 2048
        ctx.dma("sq", "qs", lambda e, tau=tau, par=par: e.dma_start(
            ctx.sb["csA"][:, par:par + 2048], cosD[tau * 128:(tau + 1) * 128, :]))
        ctx.dma("sq", "qs", lambda e, tau=tau, par=par: e.dma_start(
            ctx.sb["csB"][:, par:par + 2048], sinD[tau * 128:(tau + 1) * 128, :]))

    def rope_chunk(gi, src_f32, dst_bf):
        par = (gi % 2) * 2048
        csA = ctx.sb["csA"][:, par:par + 2048]
        csB = ctx.sb["csB"][:, par:par + 2048]
        swp, ta, tb = ctx.sb["swp"], ctx.sb["ta"], ctx.sb["tb"]
        with ctx.stage():
            for g in range(4):
                ctx.op("pe", lambda e, g=g: e.matmul(
                    ps[g][:], ctx.sb["pswb"][:],
                    src_f32[:, g * 512:(g + 1) * 512], start=True, stop=True))
        with ctx.stage():
            for g in range(4):
                ctx.op("dve", lambda e, g=g: e.tensor_copy(
                    swp[:, g * 512:(g + 1) * 512], ps[g][:]))
        with ctx.stage():
            ctx.op("dve", lambda e: e.tensor_mul(ta[:], src_f32[:], csA))
            ctx.op("dve", lambda e: e.tensor_mul(tb[:], swp[:], csB))
        with ctx.stage():
            ctx.op("dve", lambda e: e.tensor_add(dst_bf[:], ta[:], tb[:]))

    for tensor_sel in ("q", "k"):
        wT = WqT if tensor_sel == "q" else WkT
        bias_sb = ctx.sb["bq_sb"] if tensor_sel == "q" else ctx.sb["bk_sb"]
        for tau in range(8):
            gi = (0 if tensor_sel == "q" else 8) + tau
            if gi == 0:
                with ctx.stage():
                    ctx.dma("pool", "qp", lambda e, wT=wT: e.dma_start(
                        ctx.sb["wbuf"][:, 0:1024].rearrange(
                            "p (k f) -> p k f", f=128),
                        wT[:, 0:128].rearrange("(k p) f -> p k f", p=128)))
                    cs_load(0)
            elif tau == 0:
                with ctx.stage():
                    ctx.dma("pool", "qp", lambda e, wT=wT: e.dma_start(
                        ctx.sb["wbuf"][:, 0:1024].rearrange(
                            "p (k f) -> p k f", f=128),
                        wT[:, 0:128].rearrange("(k p) f -> p k f", p=128)))
            with ctx.stage():
                if gi + 1 < 16:
                    cs_load(gi + 1)
                if tau + 1 < 8:
                    ctx.dma("pool", "qp", lambda e, tau=tau, wT=wT: e.dma_start(
                        ctx.sb["wbuf"][:, ((tau + 1) % 2) * 1024:
                                       ((tau + 1) % 2) * 1024 + 1024].rearrange(
                            "p (k f) -> p k f", f=128),
                        wT[:, (tau + 1) * 128:(tau + 2) * 128].rearrange(
                            "(k p) f -> p k f", p=128)))
                for g in range(4):
                    psx = ps[g]
                    for k in range(8):
                        ctx.op("pe", lambda e, k=k, g=g, psx=psx, tau=tau: e.matmul(
                            psx[:], ctx.sb["wbuf"][:, (tau % 2) * 1024 + k * 128:
                                                   (tau % 2) * 1024 + (k + 1) * 128],
                            xfm[:, k * TPC + g * 512:k * TPC + (g + 1) * 512],
                            start=(k == 0), stop=(k == 7)))
            with ctx.stage():
                for g in range(4):
                    eng = "act"
                    ctx.op(eng, lambda e, g=g, tau=tau, bias_sb=bias_sb: e.activation(
                        ctx.sb["qpre"][:, g * 512:(g + 1) * 512], ps[g][:],
                        AF.Gelu, bias=bias_sb[:, tau:tau + 1]))
            with ctx.stage():
                ctx.op("dve", lambda e: e.tensor_scalar_add(
                    ctx.sb["qpre"][:], ctx.sb["qpre"][:], 0.21))
            if tensor_sel == "k":
                with ctx.stage():
                    ctx.op("dve", lambda e, tau=tau: e.reduce_sum(
                        ctx.sb["kmp"][:, tau:tau + 1], ctx.sb["qpre"][:], axis=AX.X))
            rope_chunk(gi, ctx.sb["qpre"], ctx.sb["ro_bf"])
            if tensor_sel == "q":
                with ctx.stage():
                    ctx.dma("sq", "qs", lambda e, tau=tau: e.dma_start(
                        qr_out[tau * 128:(tau + 1) * 128, :], ctx.sb["ro_bf"][:]))
                    ctx.dma("sq", "qs", lambda e, tau=tau: e.dma_start(
                        q_out[tau * 128:(tau + 1) * 128, :], ctx.sb["qpre"][:]))
            else:
                with ctx.stage():
                    for t in range(NT):
                        pbx = pb[0] if t < 8 else pb[1]
                        ctx.op("pe", lambda e, t=t, pbx=pbx: e.transpose(
                            pbx[:, (t % 8) * 128:(t % 8) * 128 + 128],
                            ctx.sb["ro_bf"][:, t * 128:(t + 1) * 128],
                            ctx.sb["identb"][:]))
                with ctx.stage():
                    for t in range(NT):
                        pbx = pb[0] if t < 8 else pb[1]
                        eng = "dve" if t < 8 else "act"
                        ctx.op(eng, _cp(eng,
                            lambda t=t: ctx.sb["krTc"][:, t * 128:(t + 1) * 128],
                            lambda t=t, pbx=pbx: pbx[:, (t % 8) * 128:
                                                     (t % 8) * 128 + 128]))
                with ctx.stage():
                    for t in range(NT):
                        ctx.op("pe", lambda e, t=t, tau=tau: e.matmul(
                            ps[5][:, 0:128], ctx.sb["krTc"][:, t * 128:(t + 1) * 128],
                            vtm[:, t * 1024 + tau * 128: t * 1024 + (tau + 1) * 128],
                            start=(t == 0), stop=(t == NT - 1)))
                with ctx.stage():
                    ctx.op("dve", lambda e, tau=tau: e.tensor_copy(
                        ctx.sb["kvp"][:, tau * 128:(tau + 1) * 128], ps[5][:, 0:128]))
    with ctx.stage():
        ctx.dma("sq", "qs", lambda e: e.dma_start(kv_out[:], ctx.sb["kvp"][:]))
        ctx.dma("sq", "qs", lambda e: e.dma_start(km_out[:], ctx.sb["kmp"][:]))
    ctx.pop_phase()
    if standalone:
        _ctx_finish(ctx)
    return nc


def build_attn0b(ctx=None, io=None, pref="", reduced=False):
    standalone = ctx is None
    nc = bass.Bass("TRN2", target_bir_lowering=False) if standalone else ctx.nc
    dram = lambda name, shape, dt, kind="ExternalInput": _dram(nc, io, pref, name, shape, dt, kind)
    hin = dram("hin", [TPC, D], F32, "ExternalInput")
    lnin = dram("lnin", [TPC, D], F32, "ExternalInput")
    qr_in = dram("qr_in", [D, TPC], BF16, "ExternalInput")
    q_in = dram("q_in", [D, TPC], BF16, "ExternalInput")
    kvp4 = dram("kvp4", [4, 128, 1024], F32, "ExternalInput")
    kmp4 = dram("kmp4", [4, 128, 8], F32, "ExternalInput")
    kvmD = dram("kvmD", [128, 128], F32, "ExternalInput")
    hm2D = dram("hm2D", [128, 2], F32, "ExternalInput")
    gamD = dram("gamD", [1, D], F32, "ExternalInput")
    betD = dram("betD", [1, D], F32, "ExternalInput")
    identD = dram("identD", [128, 128], F32, "ExternalInput")
    onesD = dram("onesD", [1, 512], F32, "ExternalInput")
    hout = dram("hout", [TPC, D], F32, "ExternalOutput")

    if standalone:
        ctx = Ctx(nc)
    ctx.push_phase(pref)
    a = ctx.alloc
    a("ident", [128, 128], F32)
    a("ones1", [1, 512], F32)
    a("kva", [128, 1024], F32)
    a("kvb", [128, 1024], BF16)
    a("kvm", [128, 128], F32)
    a("km", [128, 8], F32)
    a("hm2", [128, 2], F32)
    a("bdall", [128, 8 * 16], BF16)
    a("bdf", [128, 2], F32)
    a("qchunk", [128, 2 * 2048], BF16)
    a("zden", [16, 2048], F32)
    a("zfm", [16, 2048], F32)
    a("ztm", [128, 16 * 16], F32)
    a("res", [128, 8 * 1024], F32)
    a("gbc", [128, 1024], F32)
    a("bbc", [128, 1024], F32)
    a("xres", [128, 2 * 1024], F32)
    a("tmp1", [128, 2 * 1024], F32)
    for n in ("smal", "ssal", "mean", "var", "rstd", "nbias"):
        a(n, [128, 16], F32)
    ps = [ctx.alloc_psum("pp%d" % i, [128, 512], F32) for i in range(8)]

    with ctx.stage():
        ctx.dma("sq", "qs", lambda e: e.dma_start(ctx.sb["ident"][:], identD[:]))
        ctx.dma("sq", "qs", lambda e: e.dma_start(ctx.sb["ones1"][:], onesD[:]))
        ctx.dma("sq", "qs", lambda e: e.dma_start(ctx.sb["kvm"][:], kvmD[:]))
        ctx.dma("sq", "qs", lambda e: e.dma_start(ctx.sb["hm2"][:], hm2D[:]))
        ctx.dma("sq", "qs", lambda e: e.dma_start(ctx.sb["kva"][:], kvp4[0]))
        ctx.dma("sq", "qs", lambda e: e.dma_start(ctx.sb["km"][:], kmp4[0]))
        if not reduced:
            ctx.dma("sq", "qs", lambda e: e.dma_start(ctx.sb["res"][:, 0:1024], kvp4[1]))
            ctx.dma("sq", "qs", lambda e: e.dma_start(ctx.sb["res"][:, 2048:2056], kmp4[1]))
    for i in ((1, 2, 3) if not reduced else ()):
        with ctx.stage():
            ctx.op("dve", lambda e: e.tensor_add(ctx.sb["kva"][:], ctx.sb["kva"][:],
                                                 ctx.sb["res"][:, 0:1024]))
            ctx.op("pool", lambda e: e.tensor_add(ctx.sb["km"][:], ctx.sb["km"][:],
                                                  ctx.sb["res"][:, 2048:2056]))
        if i < 3:
            with ctx.stage():
                ctx.dma("sq", "qs", lambda e, i=i: e.dma_start(
                    ctx.sb["res"][:, 0:1024], kvp4[i + 1]))
                ctx.dma("sq", "qs", lambda e, i=i: e.dma_start(
                    ctx.sb["res"][:, 2048:2056], kmp4[i + 1]))
    with ctx.stage():
        ctx.op("dve", lambda e: e.tensor_scalar_mul(ctx.sb["kva"][:],
                                                    ctx.sb["kva"][:], 1.0 / T))
        ctx.op("pool", lambda e: e.tensor_scalar_mul(ctx.sb["km"][:],
                                                     ctx.sb["km"][:], 1.0 / T))
    with ctx.stage():
        for tau in range(8):
            ctx.op("dve", lambda e, tau=tau: e.tensor_mul(
                ctx.sb["kvb"][:, tau * 128:(tau + 1) * 128],
                ctx.sb["kva"][:, tau * 128:(tau + 1) * 128], ctx.sb["kvm"][:]))
    with ctx.stage():
        ctx.op("dve", lambda e: e.memset(ctx.sb["bdall"][:], 0.0))
    with ctx.stage():
        for tau in range(8):
            ctx.op("dve", lambda e, tau=tau: e.tensor_scalar_mul(
                ctx.sb["bdall"][:, tau * 16 + 2 * tau: tau * 16 + 2 * tau + 2],
                ctx.sb["hm2"][:], ctx.sb["km"][:, tau:tau + 1]))
    with ctx.stage():
        ctx.dma("sq", "qs", lambda e: e.dma_start(
            ctx.sb["qchunk"][:, 0:2048], q_in[0:128, :]))
    for tau in range(8):
        with ctx.stage():
            if tau + 1 < 8:
                ctx.dma("sq", "qs", lambda e, tau=tau: e.dma_start(
                    ctx.sb["qchunk"][:, ((tau + 1) % 2) * 2048:
                                     ((tau + 1) % 2) * 2048 + 2048],
                    q_in[(tau + 1) * 128:(tau + 2) * 128, :]))
            for g in range(4):
                ctx.op("pe", lambda e, tau=tau, g=g: e.matmul(
                    ps[g][0:16, 0:512], ctx.sb["bdall"][:, tau * 16:(tau + 1) * 16],
                    ctx.sb["qchunk"][:, (tau % 2) * 2048 + g * 512:
                                     (tau % 2) * 2048 + (g + 1) * 512],
                    start=(tau == 0), stop=(tau == 7)))
    with ctx.stage():
        for g in range(4):
            ctx.op("dve", lambda e, g=g: e.tensor_scalar_add(
                ctx.sb["zden"][0:16, g * 512:(g + 1) * 512], ps[g][0:16, 0:512],
                1e-6))
    with ctx.stage():
        ctx.op("dve", lambda e: e.reciprocal(ctx.sb["zfm"][0:16, :],
                                             ctx.sb["zden"][0:16, :]))
    with ctx.stage():
        for t in range(NT):
            psx = ps[4] if t < 8 else ps[5]
            ctx.op("pe", lambda e, t=t, psx=psx: e.transpose(
                psx[:, (t % 8) * 16:(t % 8) * 16 + 16],
                ctx.sb["zfm"][0:16, t * 128:(t + 1) * 128], ctx.sb["ident"][0:16, 0:16]))
    with ctx.stage():
        ctx.op("dve", lambda e: e.tensor_copy(ctx.sb["ztm"][:, 0:128], ps[4][:, 0:128]))
        ctx.op("act", lambda e: e.copy(ctx.sb["ztm"][:, 128:256], ps[5][:, 0:128]))
    grow = a("grow", [1, 1024], F32)
    brow = a("brow", [1, 1024], F32)
    qsb = a("qsb", [128, 2 * 1024], BF16)
    with ctx.stage():
        ctx.dma("sq", "qs", lambda e: e.dma_start(grow[:], gamD[:]))
        ctx.dma("sq", "qs", lambda e: e.dma_start(brow[:], betD[:]))
    with ctx.stage():
        for g in range(2):
            ctx.op("pe", lambda e, g=g: e.matmul(
                ps[6][:, 0:512], ctx.sb["ones1"][0:1, 0:128],
                grow[0:1, g * 512:(g + 1) * 512], start=(g == 0), stop=True,
                skip_group_check=True) if False else e.matmul(
                (ps[6] if g == 0 else ps[7])[:, 0:512], ctx.sb["ones1"][0:1, 0:128],
                grow[0:1, g * 512:(g + 1) * 512], start=True, stop=True))
    with ctx.stage():
        ctx.op("dve", lambda e: e.tensor_copy(ctx.sb["gbc"][:, 0:512], ps[6][:, 0:512]))
        ctx.op("act", lambda e: e.copy(ctx.sb["gbc"][:, 512:1024], ps[7][:, 0:512]))
    with ctx.stage():
        for g in range(2):
            ctx.op("pe", lambda e, g=g: e.matmul(
                (ps[6] if g == 0 else ps[7])[:, 0:512], ctx.sb["ones1"][0:1, 0:128],
                brow[0:1, g * 512:(g + 1) * 512], start=True, stop=True))
    with ctx.stage():
        ctx.op("dve", lambda e: e.tensor_copy(ctx.sb["bbc"][:, 0:512], ps[6][:, 0:512]))
        ctx.op("act", lambda e: e.copy(ctx.sb["bbc"][:, 512:1024], ps[7][:, 0:512]))

    res = ctx.sb["res"]; ztm = ctx.sb["ztm"]
    sm = ctx.sb["smal"]; ss = ctx.sb["ssal"]
    mean = ctx.sb["mean"]; var = ctx.sb["var"]; rstd = ctx.sb["rstd"]
    tmp1 = ctx.sb["tmp1"]; xres = ctx.sb["xres"]
    for half in range(2):
        with ctx.stage():
            t0_ = half * 8
            ctx.dma("sq", "qs", lambda e, t0_=t0_: e.dma_start(
                res[:, 0:1024], lnin[t0_ * 128:(t0_ + 1) * 128, :]))
            ctx.dma("pool", "qp", lambda e, t0_=t0_: e.dma_start(
                qsb[:, 0:1024].rearrange("p (k c) -> p k c", c=128),
                qr_in[:, t0_ * 128:(t0_ + 1) * 128].rearrange(
                    "(k p) c -> p k c", p=128)))
        for j in range(10):
            with ctx.stage():
                if j + 1 < 8:
                    tn = half * 8 + j + 1
                    ctx.dma("sq", "qs", lambda e, tn=tn, j=j: e.dma_start(
                        res[:, (j + 1) * 1024:(j + 2) * 1024],
                        lnin[tn * 128:(tn + 1) * 128, :]))
                    ctx.dma("pool", "qp", lambda e, tn=tn, j=j: e.dma_start(
                        qsb[:, ((j + 1) % 2) * 1024:((j + 1) % 2) * 1024 + 1024]
                        .rearrange("p (k c) -> p k c", c=128),
                        qr_in[:, tn * 128:(tn + 1) * 128].rearrange(
                            "(k p) c -> p k c", p=128)))
                if j < 8:
                    pspair = (ps[0], ps[1]) if j % 2 == 0 else (ps[2], ps[3])
                    for tau in range(8):
                        psx = pspair[0] if tau < 4 else pspair[1]
                        ctx.op("pe", lambda e, tau=tau, psx=psx, j=j: e.matmul(
                            psx[:, (tau % 4) * 128:(tau % 4) * 128 + 128],
                            qsb[:, (j % 2) * 1024 + tau * 128:
                                (j % 2) * 1024 + (tau + 1) * 128],
                            ctx.sb["kvb"][:, tau * 128:(tau + 1) * 128],
                            start=True, stop=True))
                if 1 <= j <= 8:
                    jx = j - 1
                    t = half * 8 + jx
                    rsl = res[:, jx * 1024:(jx + 1) * 1024]
                    pspair = (ps[0], ps[1]) if jx % 2 == 0 else (ps[2], ps[3])
                    for tau in range(8):
                        psx = pspair[0] if tau < 4 else pspair[1]
                        for par in range(2):
                            zcol = ztm[:, t * 16 + 2 * tau + par:
                                       t * 16 + 2 * tau + par + 1]
                            ctx.op("dve", lambda e, tau=tau, par=par, psx=psx,
                                   zcol=zcol, rsl=rsl: e.scalar_tensor_tensor(
                                rsl[:, tau * 128 + par * 64:
                                    tau * 128 + (par + 1) * 64],
                                psx[:, (tau % 4) * 128 + par * 64:
                                    (tau % 4) * 128 + (par + 1) * 64],
                                zcol, rsl[:, tau * 128 + par * 64:
                                          tau * 128 + (par + 1) * 64],
                                ALU.mult, ALU.add))
                if 2 <= j:
                    jy = j - 2
                    rsly = res[:, jy * 1024:(jy + 1) * 1024]
                    ctx.op("act", lambda e, rsly=rsly, jy=jy: e.activation(
                        ctx.sb["tmp1"][:, 0:1024], rsly, AF.Square,
                        accum_out=ss[:, jy:jy + 1]))
                    ctx.op("dve", lambda e, rsly=rsly, jy=jy: e.reduce_sum(
                        sm[:, jy:jy + 1], rsly, axis=AX.X))
        with ctx.stage():
            ctx.op("dve", lambda e: e.tensor_scalar_mul(mean[:, 0:8], sm[:, 0:8],
                                                        1.0 / D))
        with ctx.stage():
            ctx.op("dve", lambda e: e.scalar_tensor_tensor(
                var[:, 0:8], mean[:, 0:8], -1.0, mean[:, 0:8], ALU.mult, ALU.mult))
        with ctx.stage():
            ctx.op("dve", lambda e: e.scalar_tensor_tensor(
                var[:, 0:8], ss[:, 0:8], 1.0 / D, var[:, 0:8], ALU.mult, ALU.add))
        with ctx.stage():
            ctx.op("dve", lambda e: e.tensor_scalar_add(var[:, 0:8], var[:, 0:8],
                                                        1e-12))
        with ctx.stage():
            ctx.op("act", lambda e: e.activation(rstd[:, 0:8], var[:, 0:8], AF.Sqrt))
        with ctx.stage():
            ctx.op("dve", lambda e: e.reciprocal(rstd[:, 0:8], rstd[:, 0:8]))
        with ctx.stage():
            t0_ = half * 8
            ctx.dma("sq", "qs", lambda e, t0_=t0_: e.dma_start(
                xres[:, 0:1024], hin[t0_ * 128:(t0_ + 1) * 128, :]))
        for j in range(9):
            with ctx.stage():
                if j < 8:
                    t = half * 8 + j
                    rsl = res[:, j * 1024:(j + 1) * 1024]
                    tslot = tmp1[:, (j % 2) * 1024:(j % 2) * 1024 + 1024]
                    xslot = xres[:, (j % 2) * 1024:(j % 2) * 1024 + 1024]
                    if j + 1 < 8:
                        tn = half * 8 + j + 1
                        ctx.dma("sq", "qs", lambda e, tn=tn, j=j: e.dma_start(
                            xres[:, ((j + 1) % 2) * 1024:((j + 1) % 2) * 1024 + 1024],
                            hin[tn * 128:(tn + 1) * 128, :]))
                    ctx.op("dve", lambda e, rsl=rsl, j=j, tslot=tslot:
                           e.tensor_scalar(
                        tslot, rsl, mean[:, j:j + 1], rstd[:, j:j + 1],
                        ALU.subtract, ALU.mult))
                    ctx.op("dve", lambda e, tslot=tslot: e.tensor_mul(
                        tslot, tslot, ctx.sb["gbc"][:]))
                    ctx.op("dve", lambda e, tslot=tslot: e.tensor_add(
                        tslot, tslot, ctx.sb["bbc"][:]))
                    ctx.op("dve", lambda e, tslot=tslot, xslot=xslot: e.tensor_add(
                        tslot, tslot, xslot))
                if j >= 1:
                    ty = half * 8 + j - 1
                    ctx.dma("sq", "qs", lambda e, ty=ty, j=j: e.dma_start(
                        hout[ty * 128:(ty + 1) * 128, :],
                        tmp1[:, ((j - 1) % 2) * 1024:((j - 1) % 2) * 1024 + 1024]))
    ctx.pop_phase()
    if standalone:
        _ctx_finish(ctx)
    return nc



def build_attn1a(ctx=None, io=None, pref=""):
    standalone = ctx is None
    nc = bass.Bass("TRN2", target_bir_lowering=False) if standalone else ctx.nc
    dram = lambda name, shape, dt, kind="ExternalInput": _dram(nc, io, pref, name, shape, dt, kind)
    hin = dram("hin", [TPC, D], F32, "ExternalInput")
    qkvT = dram("qkvT", [D, 3072], F32, "ExternalInput")
    zT = dram("zT", [D, D], F32, "ExternalInput")
    baT = dram("baT", [D, 16], F32, "ExternalInput")
    identD = dram("identD", [128, 128], F32, "ExternalInput")
    mixed_out = dram("mixed_out", [3072, TPC], BF16, "ExternalOutput")
    z_out = dram("z_out", [D, TPC], F32, "ExternalOutput")
    ba_out = dram("ba_out", [16, TPC], F32, "ExternalOutput")

    if standalone:
        ctx = Ctx(nc)
    ctx.push_phase(pref)
    a = ctx.alloc
    for n in ("smal", "ssal", "mean", "var", "rstd", "nbias"):
        a(n, [128, 16], F32)
    a("xfm", [128, 8 * TPC], BF16)
    a("ident", [128, 128], F32)
    a("wbuf", [128, 2 * 1024], BF16)
    a("obuf", [128, 2 * 2048], F32)
    a("obufb", [128, 2 * 2048], BF16)
    a("ba_sb", [128, 128], BF16)
    a("bafm", [16, 2048], F32)
    ps = [ctx.alloc_psum("pp%d" % i, [128, 512], F32) for i in range(8)]

    with ctx.stage():
        ctx.dma("sq", "qs", lambda e: e.dma_start(ctx.sb["ident"][:], identD[:]))
        ctx.dma("pool", "qp", lambda e: e.dma_start(
            ctx.sb["ba_sb"][:, 0:128].rearrange("p (k e) -> p k e", e=16),
            baT.rearrange("(k p) e -> p k e", p=128)))
    hin_tile = lambda t: hin[t * 128:(t + 1) * 128, :]
    emit_ln_fused(ctx, hin_tile, None, [ps[4], ps[5], ps[6], ps[7]], None,
                  None, None)
    xfm = ctx.sb["xfm"]

    cc_hook = (io or {}).get("cc_hook")
    seq = [("qkv", i) for i in range(24)] + [("ba", 0)] + \
          [("z", i) for i in range(8)]

    def w_load(j):
        kind, i = seq[j]
        if kind == "ba":
            return None
        wT_ap = qkvT if kind == "qkv" else zT
        rb = j % 2
        return lambda e, i=i, rb=rb, wT_ap=wT_ap: e.dma_start(
            ctx.sb["wbuf"][:, rb * 1024:(rb + 1) * 1024].rearrange(
                "p (k f) -> p k f", f=128),
            wT_ap[:, i * 128:(i + 1) * 128].rearrange("(k p) f -> p k f", p=128))

    def out_ap(j):
        kind, i = seq[j]
        if kind == "qkv":
            return mixed_out[i * 128:(i + 1) * 128, :]
        if kind == "z":
            return z_out[i * 128:(i + 1) * 128, :]
        return ba_out[:]

    with ctx.stage():
        ctx.dma("pool", "qp", w_load(0))
    NSEQ = len(seq)
    for j in range(NSEQ + 2):
        with ctx.stage():
            if j < NSEQ:
                kind, i = seq[j]
                quad = ps[0:4] if j % 2 == 0 else ps[4:8]
                rb = j % 2
                for g in range(4):
                    for k in range(8):
                        if kind == "ba":
                            ctx.op("pe", lambda e, g=g, k=k, quad=quad: e.matmul(
                                quad[g][0:16, 0:512],
                                ctx.sb["ba_sb"][:, k * 16:(k + 1) * 16],
                                xfm[:, k * TPC + g * 512:k * TPC + (g + 1) * 512],
                                start=(k == 0), stop=(k == 7)))
                        else:
                            ctx.op("pe", lambda e, g=g, k=k, rb=rb, quad=quad:
                                   e.matmul(
                                quad[g][:],
                                ctx.sb["wbuf"][:, rb * 1024 + k * 128:
                                               rb * 1024 + (k + 1) * 128],
                                xfm[:, k * TPC + g * 512:k * TPC + (g + 1) * 512],
                                start=(k == 0), stop=(k == 7)))
                if j + 1 < NSEQ:
                    wl = w_load(j + 1)
                    if wl is not None:
                        ctx.dma("pool", "qp", wl)
            if 1 <= j <= NSEQ:
                jx = j - 1
                kindx, _ = seq[jx]
                quadx = ps[0:4] if jx % 2 == 0 else ps[4:8]
                ob = ctx.sb["obuf"]
                for g in range(4):
                    eng = "dve" if g % 2 == 0 else "act"
                    if kindx == "ba":
                        ctx.op(eng, _cp(eng,
                            lambda g=g: ctx.sb["bafm"][0:16, g * 512:(g + 1) * 512],
                            lambda g=g, quadx=quadx: quadx[g][0:16, 0:512]))
                    else:
                        obx = ob if kindx == "z" else ctx.sb["obufb"]
                        ctx.op(eng, _cp(eng,
                            lambda g=g, jx=jx, obx=obx: obx[:, (jx % 2) * 2048 +
                                                            g * 512:(jx % 2) * 2048 +
                                                            (g + 1) * 512],
                            lambda g=g, quadx=quadx: quadx[g][:]))
            if 2 <= j:
                jy = j - 2
                kindy, _ = seq[jy]
                if kindy == "ba":
                    ctx.dma("sq", "qs", lambda e, jy=jy: e.dma_start(
                        out_ap(jy), ctx.sb["bafm"][0:16, :]))
                else:
                    oby = ctx.sb["obuf"] if kindy == "z" else ctx.sb["obufb"]
                    ctx.dma("sq", "qs", lambda e, jy=jy, oby=oby: e.dma_start(
                        out_ap(jy),
                        oby[:, (jy % 2) * 2048:(jy % 2) * 2048 + 2048]))
        if cc_hook is not None:
            done = j - 2
            if done in (7, 15, 23, 24):
                cc_hook({7: 0, 15: 1, 23: 2, 24: 3}[done])
    ctx.pop_phase()
    if standalone:
        _ctx_finish(ctx)
    return nc


def build_attn1c(ctx=None, io=None, pref="", gather=False):
    standalone = ctx is None
    nc = bass.Bass("TRN2", target_bir_lowering=False) if standalone else ctx.nc
    dram = lambda name, shape, dt, kind="ExternalInput": _dram(nc, io, pref, name, shape, dt, kind)
    hin = dram("hin", [TPC, D], F32, "ExternalInput")
    if gather:
        o_g = dram("o_g", [1024, T], BF16, "ExternalInput")
        o_flat = o_g.rearrange("(o a) b -> o (a b)", o=1)
        goD = dram("goD", [128, 8], I32, "ExternalInput")
    else:
        o_in = dram("o_in", [D, TPC], BF16, "ExternalInput")
    z_in = dram("z_in", [D, TPC], F32, "ExternalInput")
    outT = dram("outT", [D, D], F32, "ExternalInput")
    hout = dram("hout", [TPC, D], F32, "ExternalOutput")

    if standalone:
        ctx = Ctx(nc)
    ctx.push_phase(pref)
    a = ctx.alloc
    a("og", [128, 8 * TPC], BF16)
    a("wo", [128, 8 * 1024], BF16)
    a("zc", [128, 3 * 2048], F32)
    a("oc", [128, 3 * 2048], BF16)
    a("xres", [128, 3 * 1024], F32)
    a("outt", [128, 2 * 1024], F32)
    ps = [ctx.alloc_psum("pp%d" % i, [128, 512], F32) for i in range(4)]
    if gather:
        a("go_sb", [128, 8], I32)
    with ctx.stage():
        ctx.dma("pool", "qp", lambda e: e.dma_start(
            ctx.sb["wo"][:, 0:8192].rearrange("p (k c) -> p k c", c=1024),
            outT.rearrange("(k p) c -> p k c", p=128)))
        if gather:
            ctx.dma("sq", "qs", lambda e: e.dma_start(ctx.sb["go_sb"][:], goD[:]))
    zcb = ctx.sb["zc"]; ocb = ctx.sb["oc"]

    def _ocdma(vc):
        if gather:
            ho = ctx.bcreg(1024 * T - 1)
            return lambda e, vc=vc, h=ho: e.indirect_dma_start(
                ocb[:, (vc % 3) * 2048:(vc % 3) * 2048 + 2048], None, o_flat,
                IndirectOffsetOnAxis(ctx.sb["go_sb"][:, vc:vc + 1], 1),
                bounds_check=h["r"], oob_is_err=False)
        return lambda e, vc=vc: e.dma_start(
            ocb[:, (vc % 3) * 2048:(vc % 3) * 2048 + 2048],
            o_in[vc * 128:(vc + 1) * 128, :])

    with ctx.stage():
        ctx.dma("sq", "qs", lambda e: e.dma_start(zcb[:, 0:2048], z_in[0:128, :]))
        ctx.dma("pool", "qp", _ocdma(0))
    for vc in range(10):
        with ctx.stage():
            if vc + 1 < 8:
                ctx.dma("sq", "qs", lambda e, vc=vc: e.dma_start(
                    zcb[:, ((vc + 1) % 3) * 2048:((vc + 1) % 3) * 2048 + 2048],
                    z_in[(vc + 1) * 128:(vc + 2) * 128, :]))
                ctx.dma("pool", "qp", _ocdma(vc + 1))
            if vc < 8:
                ctx.op("act", lambda e, vc=vc: e.activation(
                    zcb[:, (vc % 3) * 2048:(vc % 3) * 2048 + 2048],
                    zcb[:, (vc % 3) * 2048:(vc % 3) * 2048 + 2048], AF.Silu))
            if 1 <= vc <= 8:
                vx = vc - 1
                ctx.op("dve", lambda e, vx=vx: e.tensor_mul(
                    ctx.sb["og"][:, vx * TPC:(vx + 1) * TPC],
                    ocb[:, (vx % 3) * 2048:(vx % 3) * 2048 + 2048],
                    zcb[:, (vx % 3) * 2048:(vx % 3) * 2048 + 2048]))
    with ctx.stage():
        ctx.dma("sq", "qs", lambda e: e.dma_start(ctx.sb["xres"][:, 0:1024],
                                                  hin[0:128, :]))
    for t in range(NT + 2):
        with ctx.stage():
            if t + 1 < NT:
                ctx.dma("sq", "qs", lambda e, t=t: e.dma_start(
                    ctx.sb["xres"][:, ((t + 1) % 3) * 1024:
                                   ((t + 1) % 3) * 1024 + 1024],
                    hin[(t + 1) * 128:(t + 2) * 128, :]))
            if t < NT:
                pp = (ps[0], ps[1]) if t % 2 == 0 else (ps[2], ps[3])
                for half in range(2):
                    for k in range(8):
                        ctx.op("pe", lambda e, t=t, half=half, k=k, pp=pp: e.matmul(
                            pp[half][:],
                            ctx.sb["og"][:, k * TPC + t * 128:
                                         k * TPC + (t + 1) * 128],
                            ctx.sb["wo"][:, k * 1024 + half * 512:
                                         k * 1024 + (half + 1) * 512],
                            start=(k == 0), stop=(k == 7)))
            if 1 <= t <= NT:
                tx = t - 1
                ppx = (ps[0], ps[1]) if tx % 2 == 0 else (ps[2], ps[3])
                for half in range(2):
                    ctx.op("dve", lambda e, half=half, tx=tx, ppx=ppx:
                           e.scalar_tensor_tensor(
                        ctx.sb["outt"][:, (tx % 2) * 1024 + half * 512:
                                       (tx % 2) * 1024 + (half + 1) * 512],
                        ppx[half][:], 1.0,
                        ctx.sb["xres"][:, (tx % 3) * 1024 + half * 512:
                                       (tx % 3) * 1024 + (half + 1) * 512],
                        ALU.bypass, ALU.add))
            if 2 <= t:
                ty = t - 2
                ctx.dma("sq", "qs", lambda e, ty=ty: e.dma_start(
                    hout[ty * 128:(ty + 1) * 128, :],
                    ctx.sb["outt"][:, (ty % 2) * 1024:(ty % 2) * 1024 + 1024]))
    ctx.pop_phase()
    if standalone:
        _ctx_finish(ctx)
    return nc


def _merge_streams(ctx, streams, offsets):
    L = max(o + len(s.stages) for s, o in zip(streams, offsets))
    for k in range(L):
        items = []
        for s, o in zip(streams, offsets):
            j = k - o
            if 0 <= j < len(s.stages):
                items.extend(s.stages[j])
        if not items:
            continue
        for eng, q, fn in items:
            if eng == "__mark__":
                fn()
        items = [it for it in items if it[0] != "__mark__"]
        if not items:
            continue
        with ctx.stage():
            for eng, q, fn in items:
                if q is None:
                    ctx.op(eng, fn)
                else:
                    ctx.dma(eng, q, fn)


class _Stream:
    def __init__(self):
        self.stages = [[]]

    def op(self, eng, fn):
        self.stages[-1].append((eng, None, fn))

    def dma(self, eng, q, fn):
        self.stages[-1].append((eng, q, fn))

    def mark(self, fn):
        self.stages[-1].append(("__mark__", None, fn))

    def nxt(self):
        self.stages.append([])


def build_attn1b(ctx=None, io=None, pref="", gather=False):
    standalone = ctx is None
    nc = bass.Bass("TRN2", target_bir_lowering=False) if standalone else ctx.nc
    dram = lambda name, shape, dt, kind="ExternalInput": _dram(nc, io, pref, name, shape, dt, kind)
    if gather:
        mix_gq = dram("mix_gq", [4096, TPC], BF16, "ExternalInput")
        mix_gk = dram("mix_gk", [4096, TPC], BF16, "ExternalInput")
        mix_gv = dram("mix_gv", [4096, TPC], BF16, "ExternalInput")
        mix_gba = dram("mix_gba", [64, TPC], F32, "ExternalInput")
        mflat = [t.rearrange("(o a) b -> o (a b)", o=1)
                 for t in (mix_gq, mix_gk, mix_gv)]
        mba_flat = mix_gba.rearrange("(o a) b -> o (a b)", o=1)
        gqD = dram("gqD", [128, 48], I32, "ExternalInput")
        gbaD = dram("gbaD", [64, 4], I32, "ExternalInput")
    else:
        qkv_in = dram("qkv_in", [2, 3, 128, T], F32, "ExternalInput")
        ba_in = dram("ba_in", [2, 2, T], F32, "ExternalInput")
    cw_in = dram("cw_in", [2, 3, 128, 4], F32, "ExternalInput")
    dtb_in = dram("dtb_in", [2, 1], F32, "ExternalInput")
    alog_in = dram("alog_in", [2, 1], F32, "ExternalInput")
    nwD = dram("nwD", [1, 128], F32, "ExternalInput")
    identD = dram("identD", [128, 128], F32, "ExternalInput")
    onesD = dram("onesD", [1, 512], F32, "ExternalInput")
    mLsD = dram("mLsD", [128, 128], F32, "ExternalInput")
    mUiD = dram("mUiD", [128, 128], F32, "ExternalInput")
    triuD = dram("triuD", [128, 128], F32, "ExternalInput")
    o_out = dram("o_out", [2, 128, T], BF16, "ExternalOutput")
    scrg = [dram("scrg%d" % p, [64, 128], F32, "Internal") for p in range(2)]
    oscr = [dram("oscr%d" % p, [128, T], BF16, "Internal") for p in range(2)]

    if standalone:
        ctx = Ctx(nc)
    ctx.push_phase(pref)
    a = ctx.alloc
    a("ident", [128, 128], F32)
    a("identb", [128, 128], BF16)
    a("ones1", [1, 512], F32)
    a("mLs", [128, 128], F32)
    a("mUi", [128, 128], F32)
    a("triu", [128, 128], F32)
    a("normbc", [128, 128], F32)
    nwrow = a("nwrow", [1, 128], F32)
    a("growB", [33, T], F32)
    a("ones33", [33, 128], F32)
    for p in range(2):
        s = str(p)
        a("qbf" + s, [128, T], BF16)
        a("kbf" + s, [128, T], BF16)
        a("vbf" + s, [128, T], BF16)
        a("cvs" + s, [128, 1027], BF16)
        a("cva" + s, [128, 1024], F32)
        a("scal" + s, [1, 8], F32)
        a("cwcol" + s, [128, 4], F32)
        a("cols" + s, [128, 8], F32)
        a("ba_tm" + s, [128, 128], F32)
        a("beta" + s, [128, 64], F32)
        a("nbeta" + s, [128, 64], F32)
        a("gcum" + s, [128, 64], F32)
        a("scr1" + s, [128, 128], F32)
        a("eAll" + s, [128, 384], F32)
        a("NN0" + s, [128, 256], BF16)
        a("NN1" + s, [128, 256], BF16)
        a("MT" + s, [128, 128], BF16)
        a("gQ" + s, [128, 128], BF16)
        a("Tst" + s, [128, 256], BF16)
        a("Kbrev" + s, [128, 128], BF16)
        a("identg" + s, [128, 128], BF16)
        a("MUW" + s, [128, 256], BF16)
        a("GT" + s, [128, 128], BF16)
        a("KTUW" + s, [128, 256], BF16)
        a("NWTK" + s, [128, 128], BF16)
        a("Sbf" + s, [128, 128], BF16)
        a("Sf32" + s, [128, 128], F32)
        a("Of32" + s, [128, 128], F32)
        a("Obf8" + s, [128, 1024], BF16)
        a("osum" + s, [128, 64], F32)
    sb = ctx.sb
    pA = [ctx.alloc_psum("pA%d" % p, [128, 512], F32) for p in range(2)]
    pB = [ctx.alloc_psum("pB%d" % p, [128, 512], F32) for p in range(2)]
    pC = [ctx.alloc_psum("pC%d" % p, [128, 512], F32) for p in range(2)]
    pb = [ctx.alloc_psum("pbb%d" % p, [128, 1024], BF16) for p in range(2)]

    NEL = 4096 * TPC
    NELBA = 64 * TPC
    with ctx.stage():
        ctx.dma("sq", "qs", lambda e: e.dma_start(sb["ident"][:], identD[:]))
        ctx.dma("sq", "qs", lambda e: e.dma_start(sb["mLs"][:], mLsD[:]))
        ctx.dma("sq", "qs", lambda e: e.dma_start(sb["mUi"][:], mUiD[:]))
        ctx.dma("sq", "qs", lambda e: e.dma_start(sb["triu"][:], triuD[:]))
        ctx.dma("sq", "qs", lambda e: e.dma_start(sb["ones1"][:], onesD[:]))
        ctx.dma("sq", "qs", lambda e: e.dma_start(nwrow[:], nwD[:]))
        ctx.dma("sq", "qs", lambda e: e.dma_start(sb["ones33"][0:1, 0:128],
                                                  onesD[0:1, 0:128]))
        ctx.dma("sq", "qs", lambda e: e.dma_start(sb["ones33"][32:33, 0:128],
                                                  onesD[0:1, 0:128]))
        ctx.dma("pool", "qp", lambda e: e.dma_start(sb["identb"][:], identD[:]))
        if gather:
            a("gq_sb", [128, 48], I32)
            a("gba_sb", [64, 4], I32)
            ctx.dma("sq", "qs", lambda e: e.dma_start(sb["gq_sb"][:], gqD[:]))
            ctx.dma("sq", "qs", lambda e: e.dma_start(sb["gba_sb"][:], gbaD[:]))
            hfull = ctx.bcreg(NEL - 1)
            hba = ctx.bcreg(NELBA - 1)
        else:
            hfull = hba = None
    with ctx.stage():
        ctx.op("pe", lambda e: e.matmul(pA[0][:, 0:128], sb["ones1"][0:1, 0:128],
                                        nwrow[0:1, 0:128], start=True, stop=True))
    with ctx.stage():
        ctx.op("dve", lambda e: e.tensor_copy(sb["normbc"][:], pA[0][:, 0:128]))

    NCHUNK = T // CCH

    def pair_stream(p):
        S = _Stream()
        s = str(p)
        prow = p * 32
        bA, bB, bC, pbP = pA[p], pB[p], pC[p], pb[p]
        qbf, kbf, vbf = sb["qbf" + s], sb["kbf" + s], sb["vbf" + s]
        cvs, cva = sb["cvs" + s], sb["cva" + s]
        scal, cwcol = sb["scal" + s], sb["cwcol" + s]
        cols = sb["cols" + s]
        colA, colB = cols[:, 0:1], cols[:, 1:2]
        colg, ecolg = cols[:, 2:3], cols[:, 3:4]
        ecol2, bcg = cols[:, 4:5], cols[:, 5:6]
        ba_tm, beta, nbeta = sb["ba_tm" + s], sb["beta" + s], sb["nbeta" + s]
        gcum, scr1, eAll = sb["gcum" + s], sb["scr1" + s], sb["eAll" + s]
        NN = [sb["NN0" + s], sb["NN1" + s]]
        MT, gQ, Tst = sb["MT" + s], sb["gQ" + s], sb["Tst" + s]
        Kbrev, identg = sb["Kbrev" + s], sb["identg" + s]
        MUW, GT, KTUW = sb["MUW" + s], sb["GT" + s], sb["KTUW" + s]
        NWTK, Sbf, Sf32 = sb["NWTK" + s], sb["Sbf" + s], sb["Sf32" + s]
        Of32, Obf8, osum = sb["Of32" + s], sb["Obf8" + s], sb["osum" + s]
        growB = sb["growB"]

        cc_base = (io or {}).get("cc_base")
        for blk, dst in ((0, qbf), (1, kbf), (2, vbf)):
            if cc_base is not None:
                S.mark(lambda blk=blk: ctx.mask_at("cc", cc_base + blk + 1))
            S.dma("pool", "qp", lambda e, p=p, blk=blk: e.dma_start(
                cwcol[:], cw_in[p, blk]))
            S.nxt()
            for st in range(8):
                if st == 0:
                    S.op("dve", lambda e: e.memset(cvs[:, 0:3], 0.0))
                else:
                    S.op("dve", lambda e: e.tensor_copy(cvs[:, 0:3],
                                                        cvs[:, 1024:1027]))
                S.nxt()
                if gather:
                    col = p * 24 + blk * 8 + st
                    S.dma("pool", "qp", lambda e, col=col, blk=blk:
                          e.indirect_dma_start(
                              cvs[:, 3:1027], None, mflat[blk],
                              IndirectOffsetOnAxis(sb["gq_sb"][:, col:col + 1], 1),
                              bounds_check=hfull["r"], oob_is_err=False))
                else:
                    S.dma("pool", "qp", lambda e, p=p, blk=blk, st=st: e.dma_start(
                        cvs[:, 3:1027], qkv_in[p, blk][:, st * 1024:(st + 1) * 1024]))
                S.nxt()
                S.op("dve", lambda e: e.tensor_single_scalar(
                    cva[:], cvs[:, 3:1027], cwcol[:, 3:4], ALU.mult))
                for tau in range(3):
                    sh = 3 - tau
                    S.op("dve", lambda e, tau=tau, sh=sh: e.scalar_tensor_tensor(
                        cva[:], cvs[:, 3 - sh:1027 - sh], cwcol[:, tau:tau + 1],
                        cva[:], ALU.mult, ALU.add))
                S.nxt()
                S.op("dve", lambda e, dst=dst, st=st: e.tensor_copy(
                    dst[:, st * 1024:(st + 1) * 1024], cva[:]))
                S.nxt()
        if cc_base is not None:
            S.mark(lambda: ctx.mask_at("cc", cc_base + 4))
        S.dma("sq", "qs", lambda e, p=p: e.dma_start(scal[0:1, 0:1],
                                                     dtb_in[p:p + 1, 0:1]))
        S.dma("sq", "qs", lambda e, p=p: e.dma_start(scal[0:1, 1:2],
                                                     alog_in[p:p + 1, 0:1]))
        S.nxt()
        S.op("act", lambda e: e.activation(scal[0:1, 2:3], scal[0:1, 1:2], AF.Exp))
        S.nxt()
        S.op("dve", lambda e: e.tensor_scalar_mul(scal[0:1, 3:4], scal[0:1, 2:3],
                                                  -1.0))
        S.nxt()
        S.op("pe", lambda e: e.matmul(bA[:, 0:1], sb["ones1"][0:1, 0:128],
                                      scal[0:1, 0:1], start=True, stop=True))
        S.op("pe", lambda e: e.matmul(bA[:, 2:3], sb["ones1"][0:1, 0:128],
                                      scal[0:1, 3:4], start=True, stop=True))
        S.nxt()
        S.op("dve", lambda e: e.tensor_copy(colA, bA[:, 0:1]))
        S.op("dve", lambda e: e.tensor_copy(colB, bA[:, 2:3]))
        S.nxt()
        for row, dst0 in ((0, 0), (1, 64)):
            if gather:
                col = p * 2 + row
                S.dma("pool", "qp", lambda e, col=col: e.indirect_dma_start(
                    scr1[0:64, 0:128], None, mba_flat,
                    IndirectOffsetOnAxis(sb["gba_sb"][:, col:col + 1], 1),
                    bounds_check=hba["r"], oob_is_err=False))
            else:
                S.dma("sq", "qs", lambda e, p=p, row=row: e.dma_start(
                    scr1[0:64, 0:128],
                    ba_in[p, row].rearrange("(c q) -> c q", q=128)))
            S.nxt()
            S.op("pe", lambda e: e.transpose(bA[:, 0:64], scr1[0:64, 0:128],
                                             sb["ident"][0:64, 0:64]))
            S.nxt()
            S.op("dve", lambda e, dst0=dst0: e.tensor_copy(
                ba_tm[:, dst0:dst0 + 64], bA[:, 0:64]))
            S.nxt()
        S.op("act", lambda e: e.activation(beta[:], ba_tm[:, 0:64], AF.Sigmoid))
        S.nxt()
        S.op("dve", lambda e: e.tensor_scalar_mul(nbeta[:], beta[:], -1.0))
        S.op("dve", lambda e: e.tensor_single_scalar(
            gcum[:], ba_tm[:, 64:128], colA, ALU.add))
        S.nxt()
        S.op("dve", lambda e: e.tensor_scalar(scr1[:, 0:64], gcum[:], 0.0, None,
                                              ALU.max))
        S.op("dve", lambda e: e.tensor_scalar_mul(scr1[:, 64:128], gcum[:], -1.0))
        S.nxt()
        S.op("dve", lambda e: e.tensor_tensor(scr1[:, 64:128], gcum[:],
                                              scr1[:, 64:128], ALU.max))
        S.nxt()
        S.op("dve", lambda e: e.tensor_scalar_mul(scr1[:, 64:128],
                                                  scr1[:, 64:128], -1.0))
        S.nxt()
        S.op("act", lambda e: e.activation(scr1[:, 64:128], scr1[:, 64:128],
                                           AF.Exp))
        S.nxt()
        S.op("dve", lambda e: e.tensor_scalar_add(scr1[:, 64:128],
                                                  scr1[:, 64:128], 1.0))
        S.nxt()
        S.op("act", lambda e: e.activation(scr1[:, 64:128], scr1[:, 64:128],
                                           AF.Ln))
        S.nxt()
        S.op("dve", lambda e: e.tensor_add(scr1[:, 0:64], scr1[:, 0:64],
                                           scr1[:, 64:128]))
        S.nxt()
        S.op("dve", lambda e: e.tensor_single_scalar(
            scr1[:, 64:128], scr1[:, 0:64], colB, ALU.mult))
        S.nxt()
        S.op("pe", lambda e: e.matmul(bA[:, 0:64], sb["triu"][:],
                                      scr1[:, 64:128], start=True, stop=True))
        S.nxt()
        S.op("dve", lambda e: e.tensor_copy(gcum[:], bA[:, 0:64]))
        S.nxt()
        S.op("pe", lambda e: e.transpose(bA[0:64, 64:192], gcum[:, 0:64],
                                         sb["ident"][:]))
        S.nxt()
        S.op("dve", lambda e: e.tensor_copy(scr1[0:64, 0:128], bA[0:64, 64:192]))
        S.nxt()
        S.dma("sq", "qs", lambda e, p=p: e.dma_start(scrg[p][:, :],
                                                     scr1[0:64, 0:128]))
        S.nxt()
        S.dma("pool", "qp", lambda e, p=p, prow=prow: e.dma_start(
            growB[prow:prow + 1, 0:T].rearrange("a (c q) -> a c q", q=128),
            scrg[p].rearrange("(o c) q -> o c q", o=1)))
        S.nxt()
        S.op("dve", lambda e: e.memset(Sbf[:], 0.0))
        S.nxt()

        for c in range(NCHUNK):
            Qc = qbf[:, c * 128:(c + 1) * 128]
            Kc = kbf[:, c * 128:(c + 1) * 128]
            Vc = vbf[:, c * 128:(c + 1) * 128]
            gcc = gcum[:, c:c + 1]
            bc = beta[:, c:c + 1]
            nbc = nbeta[:, c:c + 1]
            S.op("pe", lambda e, c=c, prow=prow: e.matmul(
                bA[:, 0:128], sb["ones33"][prow:prow + 1, 0:128],
                growB[prow:prow + 1, c * 128:(c + 1) * 128], start=True, stop=True))
            S.op("pe", lambda e, Kc=Kc: e.matmul(bA[:, 128:256], Kc, Kc,
                                                 start=True, stop=True))
            S.op("pe", lambda e, Kc=Kc, Qc=Qc: e.matmul(bA[:, 256:384], Kc, Qc,
                                                        start=True, stop=True))
            S.op("pe", lambda e, Vc=Vc: e.transpose(pbP[:, 0:128], Vc,
                                                    sb["identb"][:]))
            S.op("pe", lambda e, Kc=Kc: e.transpose(pbP[:, 128:256], Kc,
                                                    sb["identb"][:]))
            S.nxt()
            S.op("dve", lambda e: e.tensor_copy(eAll[:, 256:384], bA[:, 0:128]))
            S.op("dve", lambda e, gcc=gcc: e.tensor_single_scalar(
                scr1[:], sb["mLs"][:], gcc, ALU.add))
            S.op("dve", lambda e, gcc=gcc: e.scalar_tensor_tensor(
                eAll[:, 128:256], bA[:, 0:128], gcc, sb["mUi"][:],
                ALU.subtract, ALU.add))
            S.op("dve", lambda e, bc=bc: e.tensor_single_scalar(
                Tst[:, 0:128], pbP[:, 0:128], bc, ALU.mult))
            S.op("dve", lambda e: e.tensor_tensor(eAll[:, 0:128], scr1[:],
                                                  bA[:, 0:128], ALU.subtract))
            S.op("dve", lambda e, gcc=gcc: e.tensor_single_scalar(
                colg, bA[:, 127:128], gcc, ALU.subtract))
            S.nxt()
            S.op("act", lambda e: e.activation(eAll[:, 0:384], eAll[:, 0:384],
                                               AF.Exp))
            S.op("act", lambda e: e.activation(ecolg, colg, AF.Exp))
            S.op("act", lambda e, gcc=gcc: e.activation(ecol2, gcc, AF.Exp))
            S.nxt()
            S.op("dve", lambda e, nbc=nbc: e.scalar_tensor_tensor(
                NN[0][:, 0:128], bA[:, 128:256], nbc, eAll[:, 0:128],
                ALU.mult, ALU.mult))
            S.op("dve", lambda e: e.tensor_mul(MT[:], eAll[:, 128:256],
                                               bA[:, 256:384]))
            S.op("dve", lambda e, Qc=Qc: e.tensor_mul(gQ[:], Qc,
                                                      eAll[:, 256:384]))
            S.op("dve", lambda e, bc=bc: e.tensor_mul(bcg, ecol2, bc))
            S.op("dve", lambda e: e.tensor_single_scalar(
                Kbrev[:], pbP[:, 128:256], ecolg, ALU.mult))
            S.op("dve", lambda e: e.tensor_single_scalar(
                identg[:], sb["identb"][:], eAll[:, 383:384], ALU.mult))
            S.nxt()
            S.op("pe", lambda e: e.transpose(pbP[:, 256:384], NN[0][:, 0:128],
                                             sb["identb"][:]))
            S.nxt()
            S.op("dve", lambda e: e.tensor_copy(NN[0][:, 128:256],
                                                pbP[:, 256:384]))
            S.op("dve", lambda e: e.tensor_single_scalar(
                Tst[:, 128:256], pbP[:, 128:256], bcg, ALU.mult))
            S.nxt()
            cur, nxt = 0, 1
            for st in range(7):
                Ncur = NN[cur][:, 0:128]
                NTcur = NN[cur][:, 128:256]
                S.op("pe", lambda e, NTcur=NTcur: e.matmul(
                    bB[:, 0:256], NTcur, Tst[:], start=True, stop=True))
                if st < 6:
                    S.op("pe", lambda e, NTcur=NTcur, Ncur=Ncur: e.matmul(
                        bA[:, 0:128], NTcur, Ncur, start=True, stop=True))
                    S.op("pe", lambda e, NTcur=NTcur, Ncur=Ncur: e.matmul(
                        bA[:, 128:256], Ncur, NTcur, start=True, stop=True))
                S.nxt()
                S.op("dve", lambda e: e.scalar_tensor_tensor(
                    Tst[:], bB[:, 0:256], 1.0, Tst[:], ALU.bypass, ALU.add))
                if st < 6:
                    S.op("act", lambda e, nxt=nxt: e.copy(
                        NN[nxt][:], bA[:, 0:256]))
                    cur, nxt = nxt, cur
                S.nxt()
            S.op("pe", lambda e: e.matmul(bC[:, 0:256], MT[:], Tst[:],
                                          start=True, stop=True))
            S.nxt()
            S.op("dve", lambda e: e.tensor_copy(MUW[:], bC[:, 0:256]))
            S.nxt()
            S.op("pe", lambda e: e.transpose(pbP[:, 384:512], MUW[:, 128:256],
                                             sb["identb"][:]))
            S.op("pe", lambda e: e.matmul(bC[:, 256:512], Kbrev[:], Tst[:],
                                          start=True, stop=True))
            S.nxt()
            S.op("dve", lambda e: e.scalar_tensor_tensor(
                GT[:], pbP[:, 384:512], -1.0, gQ[:], ALU.mult, ALU.add))
            S.op("dve", lambda e: e.tensor_copy(KTUW[:], bC[:, 256:512]))
            S.nxt()
            S.op("pe", lambda e: e.transpose(pbP[:, 512:640], KTUW[:, 128:256],
                                             sb["identb"][:]))
            S.nxt()
            S.op("dve", lambda e: e.tensor_scalar_mul(NWTK[:], pbP[:, 512:640],
                                                      -1.0))
            S.nxt()
            S.op("pe", lambda e: e.matmul(bC[:, 0:128], GT[:], Sbf[:],
                                          start=True, stop=True))
            S.op("pe", lambda e: e.matmul(bC[:, 128:256], NWTK[:], Sbf[:],
                                          start=True, stop=False))
            S.op("pe", lambda e: e.matmul(bC[:, 128:256], identg[:], Sbf[:],
                                          start=False, stop=True,
                                          skip_group_check=True))
            S.nxt()
            S.op("dve", lambda e: e.scalar_tensor_tensor(
                Of32[:], bC[:, 0:128], 1.0, MUW[:, 0:128], ALU.bypass, ALU.add))
            S.op("dve", lambda e: e.tensor_add(Sf32[:], bC[:, 128:256],
                                               KTUW[:, 0:128]))
            S.op("dve", lambda e: e.tensor_mul(scr1[:], Of32[:], Of32[:]))
            S.op("dve", lambda e: e.tensor_copy(Sbf[:], Sf32[:]))
            S.op("dve", lambda e, c=c: e.tensor_copy(
                Obf8[:, (c % 8) * 128:(c % 8) * 128 + 128], Of32[:]))
            S.op("dve", lambda e, c=c: e.reduce_sum(osum[:, c:c + 1], scr1[:],
                                                    axis=AX.X))
            S.nxt()
            if c % 8 == 7:
                S.dma("sq", "qs", lambda e, p=p, c=c: e.dma_start(
                    oscr[p][:, (c - 7) * 128:(c + 1) * 128], Obf8[:]))
                S.nxt()
        S.op("dve", lambda e: e.tensor_scalar(osum[:], osum[:], 1.0 / 128.0,
                                              1e-6, ALU.mult, ALU.add))
        S.nxt()
        S.op("act", lambda e: e.activation(osum[:], osum[:], AF.Sqrt))
        S.nxt()
        S.op("dve", lambda e: e.reciprocal(osum[:], osum[:]))
        S.nxt()
        for g in range(8):
            S.dma("sq", "qs", lambda e, p=p, g=g: e.dma_start(
                Obf8[:], oscr[p][:, g * 1024:(g + 1) * 1024]))
            S.nxt()
            for j in range(8):
                S.op("dve", lambda e, g=g, j=j: e.scalar_tensor_tensor(
                    Obf8[:, j * 128:(j + 1) * 128],
                    Obf8[:, j * 128:(j + 1) * 128],
                    osum[:, g * 8 + j:g * 8 + j + 1], sb["normbc"][:],
                    ALU.mult, ALU.mult))
            S.nxt()
            for j in range(8):
                S.op("pe", lambda e, j=j: e.transpose(
                    pbP[:, j * 128:(j + 1) * 128],
                    Obf8[:, j * 128:(j + 1) * 128], sb["identb"][:]))
            S.nxt()
            for j in range(8):
                S.op("dve", lambda e, j=j: e.tensor_copy(
                    qbf[:, j * 128:(j + 1) * 128], pbP[:, j * 128:(j + 1) * 128]))
            S.nxt()
            S.dma("sq", "qs", lambda e, p=p, g=g: e.dma_start(
                o_out[p][:, g * 1024:(g + 1) * 1024], qbf[:, 0:1024]))
            S.nxt()
        return S

    s0 = pair_stream(0)
    s1 = pair_stream(1)
    _merge_streams(ctx, [s0, s1], [0, 1])
    ctx.pop_phase()
    if standalone:
        _ctx_finish(ctx)
    return nc




_BUILD_CACHE = {}


def _get(name):
    if name not in _BUILD_CACHE:
        _BUILD_CACHE[name] = globals()["build_" + name]()
    return _BUILD_CACHE[name]


def _run(nc, in_maps):
    return run_bass_kernel_spmd(nc, in_maps, core_ids=list(range(NCORES)))


def _moe0(h, sh):
    res = _run(_get("moe0a"), [{
        "hin": h[c], "gT": sh["gateT"], "identD": sh["ident"],
    } for c in range(8)])
    probs = np.concatenate([res.results[c]["probs_out"] for c in range(8)], axis=0)
    in2 = []
    for c in range(8):
        b = c // 4
        pg = np.ascontiguousarray(probs[b * T:(b + 1) * T].T)
        pown = np.ascontiguousarray(probs[c * TPC:(c + 1) * TPC])
        in2.append({
            "hin": h[c], "pglob": pg, "pown": pown,
            "w1h_l0": sh["w1h_l0"], "w2h_l0": sh["w2h_l0"],
            "LstD": sh["Lstrict"],
            "identD": sh["ident"], "onesD": sh["ones_row"],
            "bdD": sh["bd16"], "bdTD": sh["bd16T"],
            "coff": np.zeros((1, 1), np.float32),
        })
    res = _run(_get("moe0b"), in2)
    return [res.results[c]["hout"] for c in range(8)]


def _moe1(h, sh):
    b1c = np.ascontiguousarray(
        sh["l1b1"].reshape(8, 32, 128).transpose(2, 0, 1).reshape(128, 256))
    b2r = np.ascontiguousarray(sh["l1b2"].reshape(1, 8192))
    res = _run(_get("moe1"), [{
        "hin": h[c],
        "w1h_l1": sh["w1h_l1"], "w2h_l1": sh["w2h_l1"],
        "LstD": sh["Lstrict"], "b2c_l1": sh["l1b2c"],
        "wrT": sh["WrT"], "br": sh["br"], "b1c": b1c,
        "identD": sh["ident"], "onesD": sh["ones_row"],
    } for c in range(8)])
    return [res.results[c]["hout"] for c in range(8)]


def _attn0(h, sh, pc):
    bqc = np.ascontiguousarray(sh["bq"].reshape(8, 128).T)
    bkc = np.ascontiguousarray(sh["bk"].reshape(8, 128).T)
    res = _run(_get("attn0a"), [{
        "hin": h[c], "WqT": sh["WqT"], "WkT": sh["WkT"], "WvT": sh["WvT"],
        "bqc": bqc, "bkc": bkc, "bvr": sh["bv"],
        "cosD": pc[c]["cosDup"], "sinD": pc[c]["sinDup"], "pswD": sh["pswapT"],
        "identD": sh["ident"], "onesD": sh["ones_row"],
    } for c in range(8)])
    in2 = []
    for c in range(8):
        b = c // 4
        grp = list(range(b * 4, b * 4 + 4))
        in2.append({
            "hin": h[c], "lnin": res.results[c]["ln_out"],
            "qr_in": res.results[c]["qr_out"], "q_in": res.results[c]["q_out"],
            "kvp4": np.stack([res.results[g]["kv_out"] for g in grp]),
            "kmp4": np.stack([res.results[g]["km_out"] for g in grp]),
            "kvmD": sh["kvmask"], "hm2D": sh["headmask2"],
            "gamD": sh["gam0"], "betD": sh["bet0"],
            "identD": sh["ident"], "onesD": sh["ones_row"],
        })
    res2 = _run(_get("attn0b"), in2)
    return [res2.results[c]["hout"] for c in range(8)]


def _attn1(h, sh):
    res = _run(_get("attn1a"), [{
        "hin": h[c], "qkvT": sh["qkvT"], "zT": sh["zT"], "baT": sh["baT"],
        "identD": sh["ident"],
    } for c in range(8)])
    mixed = [res.results[c]["mixed_out"] for c in range(8)]
    ba = [res.results[c]["ba_out"] for c in range(8)]
    in2 = []
    for c in range(8):
        b = c // 4
        cores = list(range(b * 4, b * 4 + 4))
        qkv_pair = np.zeros((2, 3, 128, T), np.float32)
        ba_pair = np.zeros((2, 2, T), np.float32)
        cw_pair = np.zeros((2, 3, 128, 4), np.float32)
        dtb_pair = np.zeros((2, 1), np.float32)
        alog_pair = np.zeros((2, 1), np.float32)
        for j in range(2):
            hh = (c % 4) * 2 + j
            for g, src_core in enumerate(cores):
                sl = slice(g * TPC, (g + 1) * TPC)
                qkv_pair[j, 0, :, sl] = mixed[src_core][hh * 128:(hh + 1) * 128]
                qkv_pair[j, 1, :, sl] = mixed[src_core][1024 + hh * 128:1024 + (hh + 1) * 128]
                qkv_pair[j, 2, :, sl] = mixed[src_core][2048 + hh * 128:2048 + (hh + 1) * 128]
                ba_pair[j, 0, sl] = ba[src_core][hh]
                ba_pair[j, 1, sl] = ba[src_core][8 + hh]
            cw_pair[j, 0] = sh["convw"][hh * 128:(hh + 1) * 128]
            cw_pair[j, 1] = sh["convw"][1024 + hh * 128:1024 + (hh + 1) * 128]
            cw_pair[j, 2] = sh["convw"][2048 + hh * 128:2048 + (hh + 1) * 128]
            dtb_pair[j, 0] = sh["dtb"][hh, 0]
            alog_pair[j, 0] = sh["Alog"][hh, 0]
        in2.append({
            "qkv_in": qkv_pair, "ba_in": ba_pair, "cw_in": cw_pair,
            "dtb_in": dtb_pair, "alog_in": alog_pair,
            "nwD": sh["normw"], "identD": sh["ident"], "onesD": sh["ones_row"],
            "mUsD": sh["maskUs"], "mLsD": sh["maskLs"], "mUiD": sh["maskUi"],
            "triuD": sh["triu_incl"],
        })
    res2 = _run(_get("attn1b"), in2)
    in3 = []
    for c in range(8):
        b = c // 4
        sl = slice((c % 4) * TPC, (c % 4 + 1) * TPC)
        o_mine = np.zeros((1024, TPC), bfnp)
        for hh in range(8):
            owner = b * 4 + hh // 2
            j = hh % 2
            o_mine[hh * 128:(hh + 1) * 128] = res2.results[owner]["o_out"][j][:, sl]
        in3.append({
            "hin": h[c], "o_in": o_mine, "z_in": res.results[c]["z_out"],
            "outT": sh["outT"],
        })
    res3 = _run(_get("attn1c"), in3)
    return [res3.results[c]["hout"] for c in range(8)]


def kernel(**inputs):
    import os
    if os.environ.get("KERNEL_UNFUSED", "0") != "1":
        return kernel_fused(**inputs)
    sh, pc = host_prep(inputs)
    h = [np.ascontiguousarray(pc[c]["x0"]) for c in range(8)]
    h = _moe0(h, sh)
    h = _attn0(h, sh, pc)
    h = _moe0(h, sh)
    h = _moe1(h, sh)
    h = _attn1(h, sh)
    h = _moe1(h, sh)
    out = np.concatenate([np.asarray(x, np.float32) for x in h], axis=0)
    return out.reshape(B, T, D)



GROUPS4 = [[0, 1, 2, 3], [4, 5, 6, 7]]


def emit_pglob_glue(ctx, pref, nc, probs_loc, identD):
    probs_g = nc.dram_tensor(pref + "probs_g", [T, 8], F32).ap()
    pglob = nc.dram_tensor(pref + "pglob", [8, T], F32).ap()
    ctx.push_phase(pref)
    a = ctx.alloc
    a("ptile4", [128, 64], F32)
    a("prow4", [8, 1024], F32)
    a("gident", [128, 128], F32)
    psx2 = [ctx.alloc_psum("pst%d" % i, [128, 512], F32) for i in range(2)]
    NIT = 16
    with ctx.stage():
        ctx.dma("sq", "qs", lambda e: e.dma_start(ctx.sb["gident"][:], identD[:]))
        ctx.cc(lambda e: e.collective_compute(
            "AllGather", ALU.bypass, GROUPS4, [probs_loc.opt()], [probs_g.opt()]))
    for i in range(NIT + 3):
        with ctx.stage():
            if i < NIT:
                ctx.dma("sq", "qs", lambda e, i=i: e.dma_start(
                    ctx.sb["ptile4"][:, (i % 2) * 32:(i % 2) * 32 + 32].rearrange(
                        "p (a e) -> p a e", e=8),
                    probs_g[i * 512:(i + 1) * 512, :].rearrange(
                        "(a p) e -> p a e", p=128)))
            if 1 <= i <= NIT:
                x = i - 1
                for aa in range(4):
                    ctx.op("pe", lambda e, x=x, aa=aa: e.transpose(
                        psx2[x % 2][0:8, aa * 128:(aa + 1) * 128],
                        ctx.sb["ptile4"][:, (x % 2) * 32 + aa * 8:
                                         (x % 2) * 32 + (aa + 1) * 8],
                        ctx.sb["gident"][:]))
            if 2 <= i <= NIT + 1:
                x = i - 2
                ctx.op("dve", lambda e, x=x: e.tensor_copy(
                    ctx.sb["prow4"][0:8, (x % 2) * 512:(x % 2) * 512 + 512],
                    psx2[x % 2][0:8, 0:512]))
            if 3 <= i:
                x = i - 3
                ctx.dma("sq", "qs", lambda e, x=x: e.dma_start(
                    pglob[:, x * 512:(x + 1) * 512],
                    ctx.sb["prow4"][0:8, (x % 2) * 512:(x % 2) * 512 + 512]))
    ctx.pop_phase()
    return pglob


def build_fused():
    nc = bass.Bass("TRN2", target_bir_lowering=False)
    E = lambda n, s, dt: nc.dram_tensor(n, list(s), dt, kind="ExternalInput").ap()
    I_ = lambda n, s, dt: nc.dram_tensor(n, list(s), dt).ap()
    x0 = E("x0", [TPC, D], F32)
    hout = nc.dram_tensor("hout", [TPC, D], F32, kind="ExternalOutput").ap()
    com = {
        "identD": E("identD", [128, 128], F32),
        "onesD": E("onesD", [1, 512], F32),
    }
    w = {
        "gT": E("gT", [1024, 8], F32),
        "w1h_l0": E("w1h_l0", [NE, 32, 128, 8, 128], F32),
        "w2h_l0": E("w2h_l0", [NE, 8, 128, 32, 128], F32),
        "w1h_l1": E("w1h_l1", [NE, 32, 128, 8, 128], F32),
        "w2h_l1": E("w2h_l1", [NE, 8, 128, 32, 128], F32),
        "LstD": E("LstD", [128, 128], F32),
        "b2c_l1": E("b2c_l1", [128, 64], F32),
        "coff": E("coff", [1, 1], F32),
        "bdD": E("bdD", [128, 8], F32),
        "bdTD": E("bdTD", [8, 128], F32),
        "WqT": E("WqT", [D, D], F32), "WkT": E("WkT", [D, D], F32),
        "WvT": E("WvT", [D, D], F32),
        "bqc": E("bqc", [128, 8], F32), "bkc": E("bkc", [128, 8], F32),
        "bvr": E("bvr", [1, D], F32),
        "cosD": E("cosD", [1024, TPC], F32), "sinD": E("sinD", [1024, TPC], F32),
        "pswD": E("pswD", [128, 128], F32),
        "kvmD": E("kvmD", [128, 128], F32), "hm2D": E("hm2D", [128, 2], F32),
        "gamD": E("gamD", [1, D], F32), "betD": E("betD", [1, D], F32),
        "wrT": E("wrT", [1024, 8], F32), "br": E("br", [8, 1], F32),
        "b1c": E("b1c", [128, 256], F32),
        "qkvT": E("qkvT", [D, 3072], F32), "zT": E("zT", [D, D], F32),
        "baT": E("baT", [D, 16], F32),
        "cw_in": E("cw_in", [2, 3, 128, 4], F32),
        "dtb_in": E("dtb_in", [2, 1], F32), "alog_in": E("alog_in", [2, 1], F32),
        "nwD": E("nwD", [1, 128], F32),
        "mUsD": E("mUsD", [128, 128], F32), "mLsD": E("mLsD", [128, 128], F32),
        "mUiD": E("mUiD", [128, 128], F32), "triuD": E("triuD", [128, 128], F32),
        "outT": E("outT", [D, D], F32),
        "gqD": E("gqD", [128, 48], I32), "gbaD": E("gbaD", [64, 4], I32),
        "goD": E("goD", [128, 8], I32),
    }
    hh = [I_("hh%d" % i, [TPC, D], F32) for i in range(6)]
    probs1 = I_("probs1", [TPC, 8], F32)
    probs2 = I_("probs2", [TPC, 8], F32)
    qr_s = I_("qr_s", [D, TPC], BF16)
    q_s = I_("q_s", [D, TPC], BF16)
    kv_s = I_("kv_s", [128, 1024], F32)
    km_s = I_("km_s", [128, 8], F32)
    kvr = I_("kvr", [128, 1024], F32)
    kmr = I_("kmr", [128, 8], F32)
    ln_s = I_("ln_s", [TPC, D], F32)
    mix_sb = I_("mix_sb", [3072, TPC], BF16)
    ba_s = I_("ba_s", [16, TPC], F32)
    mix_gq = I_("mix_gq", [4096, TPC], BF16)
    mix_gk = I_("mix_gk", [4096, TPC], BF16)
    mix_gv = I_("mix_gv", [4096, TPC], BF16)
    mix_gba = I_("mix_gba", [64, TPC], F32)
    z_s = I_("z_s", [D, TPC], F32)
    o_sflat = I_("o_sflat", [256, T], BF16)
    o_g = I_("o_g", [1024, T], BF16)

    ctx = Ctx(nc)

    def moe0(hin, hres, probs_loc, pref):
        lnscr = nc.dram_tensor(pref + "lnscr", [TPC, 1024], BF16).ap()
        io = dict(com); io.update(w)
        io["hin"] = hin; io["probs_out"] = probs_loc
        io["lnscr"] = lnscr
        build_moe0a(ctx, io, pref + "a_")
        pglob = emit_pglob_glue(ctx, pref + "g_", nc, probs_loc, com["identD"])
        io2 = dict(com); io2.update(w)
        io2["hin"] = hin; io2["hout"] = hres
        io2["pglob"] = pglob; io2["pown"] = probs_loc
        io2["lnscr"] = lnscr; io2["ln_pre"] = True
        build_moe0b(ctx, io2, pref + "b_")

    def moe1(hin, hres, pref):
        io = dict(com); io.update(w)
        io["hin"] = hin; io["hout"] = hres
        build_moe1(ctx, io, pref)

    def attn0(hin, hres, pref):
        io = dict(com); io.update(w)
        io.update({"hin": hin, "qr_out": qr_s, "q_out": q_s, "kv_out": kv_s,
                   "km_out": km_s, "ln_out": ln_s})
        build_attn0a(ctx, io, pref + "a_")
        with ctx.stage():
            ctx.cc(lambda e: e.collective_compute(
                "AllReduce", ALU.add, GROUPS4, [kv_s.opt()], [kvr.opt()]))
            ctx.cc(lambda e: e.collective_compute(
                "AllReduce", ALU.add, GROUPS4, [km_s.opt()], [kmr.opt()]))
        io2 = dict(com); io2.update(w)
        io2.update({"hin": hin, "lnin": ln_s, "qr_in": qr_s, "q_in": q_s,
                    "kvp4": kvr.rearrange("(o p) c -> o p c", o=1),
                    "kmp4": kmr.rearrange("(o p) c -> o p c", o=1),
                    "hout": hres})
        build_attn0b(ctx, io2, pref + "b_", reduced=True)

    def attn1(hin, hres, pref):
        chunks = [(mix_sb[0:1024, :], mix_gq), (mix_sb[1024:2048, :], mix_gk),
                  (mix_sb[2048:3072, :], mix_gv), (ba_s, mix_gba)]

        def cc_hook(j):
            with ctx.stage():
                ctx.cc(lambda e, j=j: e.collective_compute(
                    "AllGather", ALU.bypass, GROUPS4,
                    [chunks[j][0].opt()], [chunks[j][1].opt()]))

        io = dict(com); io.update(w)
        io.update({"hin": hin, "mixed_out": mix_sb,
                   "ba_out": ba_s, "z_out": z_s,
                   "cc_hook": cc_hook})
        cc0 = ctx.cnt["cc"]
        ctx.mask_sem("cc")
        build_attn1a(ctx, io, pref + "a_")
        io2 = dict(com); io2.update(w)
        io2.update({"cc_base": cc0,
                    "mix_gq": mix_gq, "mix_gk": mix_gk, "mix_gv": mix_gv,
                    "mix_gba": mix_gba,
                    "o_out": o_sflat.rearrange("(j p) t -> j p t", p=128)})
        build_attn1b(ctx, io2, pref + "b_", gather=True)
        ctx.unmask_sem("cc")
        with ctx.stage():
            ctx.cc(lambda e: e.collective_compute(
                "AllGather", ALU.bypass, GROUPS4, [o_sflat.opt()], [o_g.opt()]))
        io3 = dict(com); io3.update(w)
        io3.update({"hin": hin, "o_g": o_g, "z_in": z_s, "hout": hres})
        build_attn1c(ctx, io3, pref + "c_", gather=True)

    moe0(x0, hh[0], probs1, "p1")
    attn0(hh[0], hh[1], "p2")
    moe0(hh[1], hh[2], probs2, "p3")
    moe1(hh[2], hh[3], "p4")
    attn1(hh[3], hh[4], "p5")
    io = dict(com); io.update(w)
    io["hin"] = hh[4]; io["hout"] = hout
    build_moe1(ctx, io, "p6")
    _ctx_finish(ctx)
    return nc


def _fused_inputs(sh, pc, c, x0c):
    b1c = np.ascontiguousarray(
        sh["l1b1"].reshape(8, 32, 128).transpose(2, 0, 1).reshape(128, 256))
    b2r = np.ascontiguousarray(sh["l1b2"].reshape(1, 8192))
    bqc = np.ascontiguousarray(sh["bq"].reshape(8, 128).T)
    bkc = np.ascontiguousarray(sh["bk"].reshape(8, 128).T)
    cw_pair = np.zeros((2, 3, 128, 4), np.float32)
    dtb_pair = np.zeros((2, 1), np.float32)
    alog_pair = np.zeros((2, 1), np.float32)
    gq = np.zeros((128, 48), np.int32)
    gba = np.zeros((64, 4), np.int32)
    go = np.zeros((128, 8), np.int32)
    p_ = np.arange(128)
    for j in range(2):
        hh_ = (c % 4) * 2 + j
        for blk in range(3):
            cw_pair[j, blk] = sh["convw"][blk * 1024 + hh_ * 128:
                                          blk * 1024 + (hh_ + 1) * 128]
            for st in range(8):
                gq[:, j * 24 + blk * 8 + st] = \
                    ((st // 2) * 1024 + hh_ * 128 + p_) * TPC + (st % 2) * 1024
        for row in range(2):
            for r in range(4):
                for cc_ in range(16):
                    gba[r * 16 + cc_, j * 2 + row] = \
                        (r * 16 + row * 8 + hh_) * TPC + cc_ * 128
        dtb_pair[j, 0] = sh["dtb"][hh_, 0]
        alog_pair[j, 0] = sh["Alog"][hh_, 0]
    for vh in range(8):
        r = vh // 2
        jj = vh % 2
        go[:, vh] = (r * 256 + jj * 128 + p_) * T + (c % 4) * TPC
    return {
        "x0": x0c,
        "identD": sh["ident"], "onesD": sh["ones_row"],
        "gT": sh["gateT"],
        "w1h_l0": sh["w1h_l0"], "w2h_l0": sh["w2h_l0"],
        "w1h_l1": sh["w1h_l1"], "w2h_l1": sh["w2h_l1"],
        "LstD": sh["Lstrict"], "b2c_l1": sh["l1b2c"],
        "coff": np.zeros((1, 1), np.float32),
        "bdD": sh["bd16"], "bdTD": sh["bd16T"],
        "WqT": sh["WqT"], "WkT": sh["WkT"], "WvT": sh["WvT"],
        "bqc": bqc, "bkc": bkc, "bvr": sh["bv"],
        "cosD": pc[c]["cosDup"], "sinD": pc[c]["sinDup"], "pswD": sh["pswapT"],
        "kvmD": sh["kvmask"], "hm2D": sh["headmask2"],
        "gamD": sh["gam0"], "betD": sh["bet0"],
        "wrT": sh["WrT"], "br": sh["br"], "b1c": b1c,
        "qkvT": sh["qkvT"], "zT": sh["zT"], "baT": sh["baT"],
        "cw_in": cw_pair, "dtb_in": dtb_pair, "alog_in": alog_pair,
        "nwD": sh["normw"],
        "mUsD": sh["maskUs"], "mLsD": sh["maskLs"], "mUiD": sh["maskUi"],
        "triuD": sh["triu_incl"], "outT": sh["outT"],
        "gqD": gq, "gbaD": gba, "goD": go,
    }


def kernel_fused(**inputs):
    sh, pc = host_prep(inputs)
    nc = _get("fused")
    in_maps = [_fused_inputs(sh, pc, c, np.ascontiguousarray(pc[c]["x0"]))
               for c in range(8)]
    res = _run(nc, in_maps)
    out = np.concatenate([np.asarray(res.results[c]["hout"], np.float32)
                          for c in range(8)], axis=0)
    return out.reshape(B, T, D)



BIGOOB = 1.0e6


def emit_moe_maps(ctx, layer, cap, psB):
    sb = ctx.sb
    a_ = ctx.alloc
    a_("mk8", [128, 128], F32)
    a_("rp8", [128, 128], F32)
    a_("rt8", [128, 8], F32)
    a_("rb8", [128, 8], F32)
    a_("sf8", [128, 128], F32)
    a_("omap", [128, 128], I32)
    a_("oel", [128, 128], I32)
    a_("ones16", [128, 16], F32)
    if layer == 1:
        a_("oall", [128, 16], F32)
        a_("oalli", [128, 16], I32)
    with ctx.stage():
        ctx.op("dve", lambda e: e.memset(sb["ones16"][:], 0.0))
    with ctx.stage():
        ctx.op("dve", lambda e: e.tensor_scalar_add(sb["ones16"][:], sb["ones16"][:],
                                                    1.0))
    with ctx.stage():
        for e_ in range(NE):
            if layer == 1:
                ctx.op("dve", lambda e, e_=e_: e.tensor_single_scalar(
                    sb["mk8"][:, e_ * 16:(e_ + 1) * 16], sb["idxf"][:, 0:16],
                    float(e_), ALU.is_equal))
            else:
                ctx.op("dve", lambda e, e_=e_: e.tensor_copy(
                    sb["mk8"][:, e_ * 16:(e_ + 1) * 16],
                    sb["sel128"][:, 0:128].rearrange("p (t e) -> p e t", e=8)[:, e_, :]))
    with ctx.stage():
        for e_ in range(NE):
            ctx.op("dve", lambda e, e_=e_: e.tensor_tensor_scan(
                sb["rp8"][:, e_ * 16:(e_ + 1) * 16], sb["ones16"][:],
                sb["mk8"][:, e_ * 16:(e_ + 1) * 16], 0.0, ALU.mult, ALU.add))
        ctx.op("dve", lambda e: e.tensor_reduce(
            sb["rt8"][:, 0:8], sb["mk8"][:, 0:128].rearrange(
                "p (e t) -> p e t", t=16), AX.X, ALU.add))
    with ctx.stage():
        for e_ in range(NE):
            ctx.op("pe", lambda e, e_=e_: e.matmul(
                psB[:, e_:e_ + 1], ctx.sb["Lst_sb"][:], sb["rt8"][:, e_:e_ + 1],
                start=True, stop=True))
    with ctx.stage():
        ctx.op("dve", lambda e: e.tensor_copy(sb["rb8"][:, 0:8], psB[:, 0:8]))
    with ctx.stage():
        for e_ in range(NE):
            ctx.op("dve", lambda e, e_=e_: e.tensor_scalar(
                sb["sf8"][:, e_ * 16:(e_ + 1) * 16],
                sb["rp8"][:, e_ * 16:(e_ + 1) * 16], 1.0,
                sb["rb8"][:, e_:e_ + 1], ALU.subtract, ALU.add))
    with ctx.stage():
        for e_ in range(NE):
            ctx.op("dve", lambda e, e_=e_: e.tensor_scalar_add(
                sb["sf8"][:, e_ * 16:(e_ + 1) * 16],
                sb["sf8"][:, e_ * 16:(e_ + 1) * 16], float(e_ * cap - BIGOOB)))
    with ctx.stage():
        for e_ in range(NE):
            ctx.op("dve", lambda e, e_=e_: e.tensor_mul(
                sb["sf8"][:, e_ * 16:(e_ + 1) * 16],
                sb["sf8"][:, e_ * 16:(e_ + 1) * 16],
                sb["mk8"][:, e_ * 16:(e_ + 1) * 16]))
    with ctx.stage():
        ctx.op("dve", lambda e: e.tensor_scalar_add(sb["sf8"][:], sb["sf8"][:],
                                                    float(BIGOOB)))
    with ctx.stage():
        ctx.op("dve", lambda e: e.tensor_copy(sb["omap"][:], sb["sf8"][:]))
        ctx.op("dve", lambda e: e.tensor_scalar_mul(sb["rp8"][:], sb["sf8"][:],
                                                    1024.0))
        if layer == 1:
            ctx.op("dve", lambda e: e.tensor_tensor(
                sb["sf8"][:, 0:64], sb["sf8"][:, 0:64], sb["sf8"][:, 64:128],
                ALU.min))
    with ctx.stage():
        ctx.op("dve", lambda e: e.tensor_copy(sb["oel"][:], sb["rp8"][:]))
        if layer == 1:
            ctx.op("dve", lambda e: e.tensor_tensor(
                sb["sf8"][:, 0:32], sb["sf8"][:, 0:32], sb["sf8"][:, 32:64],
                ALU.min))
    if layer == 1:
        with ctx.stage():
            ctx.op("dve", lambda e: e.tensor_tensor(
                sb["oall"][:], sb["sf8"][:, 0:16], sb["sf8"][:, 16:32], ALU.min))
        with ctx.stage():
            ctx.op("dve", lambda e: e.tensor_copy(sb["oalli"][:], sb["oall"][:]))
            ctx.op("dve", lambda e: e.tensor_scalar_mul(sb["oall"][:],
                                                        sb["oall"][:], 1024.0))
        with ctx.stage():
            ctx.op("dve", lambda e: e.tensor_copy(sb["oel"][:, 0:16], sb["oall"][:]))


def emit_moe_gath(ctx, layer, hin_tile, hout_dram, w1h, w2h, lnscr, gbuf, ybuf,
                  b1_sb=None, b2c_sb=None, psum=None, cap=512, resid_scale=0.5,
                  selw_src=None):
    p0, p1, p2, p3, p4, p5 = psum[:6]
    sb = ctx.sb
    a_ = ctx.alloc
    ntil = cap // 128
    w512 = min(cap, 512)
    a_("lntile", [128, 2 * 1024], BF16)
    a_("xg", [128, 8 * 640], BF16)
    a_("hg", [128, 32 * 640], BF16)
    a_("w1buf2", [128, 2 * 1024], BF16)
    a_("w2buf2", [128, 2 * 4096], BF16)
    a_("ydc", [128, 2 * 640], BF16)
    a_("ysb", [128, 5 * 1024], BF16)
    a_("gdst", [128, 2 * 4096], BF16)
    a_("xres3", [128, 3 * 1024], F32)
    act_fn = AF.Gelu if layer == 0 else AF.Relu
    omap = sb["omap"]
    import os
    cutoff = int(os.environ.get("MOE_GATH_CUT", "999999"))
    with ctx.stage():
        ctx.op("dve", lambda e: e.memset(sb["gdst"][:], 0.0))
        ctx.op("pool", lambda e: e.memset(sb["xres3"][:], 0.0))

    for t in range(NT + 1):
        with ctx.stage():
            if t < NT:
                ctx.dma("sq", "qs", lambda e, t=t: e.dma_start(
                    sb["lntile"][:, (t % 2) * 1024:(t % 2) * 1024 + 1024],
                    lnscr[t * 128:(t + 1) * 128, :]))
            if t >= 1:
                tx = t - 1
                hrow = ctx.bcreg(8 * cap - 1)
                if layer == 1:
                    ctx.dma("pool", "qp", lambda e, tx=tx, h=hrow: e.indirect_dma_start(
                        gbuf[:], IndirectOffsetOnAxis(sb["oalli"][:, tx:tx + 1], 0),
                        sb["lntile"][:, (tx % 2) * 1024:(tx % 2) * 1024 + 1024], None,
                        bounds_check=h["r"], oob_is_err=False))
                else:
                    ctx.dma("pool", "qp", lambda e, tx=tx, h=hrow:
                            e.indirect_dma_start(
                        gbuf[:],
                        IndirectOffsetOnAxis(omap[:, tx:tx + 1], 0),
                        sb["lntile"][:, (tx % 2) * 1024:(tx % 2) * 1024 + 1024],
                        None, bounds_check=h["r"], oob_is_err=False))
    for e_ in (range(NE) if cutoff >= 2 else []):
        for s in range(ntil + 2):
            with ctx.stage():
                if s < ntil:
                    ctx.dma("sq", "qs", lambda e, s=s, e_=e_: e.dma_start(
                        sb["lntile"][:, (s % 2) * 1024:(s % 2) * 1024 + 1024],
                        gbuf[e_ * cap + s * 128: e_ * cap + (s + 1) * 128, :]))
                if 1 <= s <= ntil:
                    sx = s - 1
                    for k in range(8):
                        ctx.op("pe", lambda e, k=k, sx=sx: e.transpose(
                            ctx.psb[sx % 2][:, k * 128:(k + 1) * 128],
                            sb["lntile"][:, (sx % 2) * 1024 + k * 128:
                                         (sx % 2) * 1024 + (k + 1) * 128],
                            sb["identb"][:]))
                if 2 <= s:
                    sx = s - 2
                    for k in range(8):
                        ctx.op("dve", lambda e, k=k, sx=sx, e_=e_: e.tensor_copy(
                            sb["xg"][:, k * cap + sx * 128:
                                     k * cap + (sx + 1) * 128],
                            ctx.psb[sx % 2][:, k * 128:(k + 1) * 128]))
        for f in (range(-1, 33) if cutoff >= 22 else []):
            with ctx.stage():
                if layer == 0 and e_ + 1 < NE:
                    if 0 <= f < NT:
                        ctx.dma("sq", "qs", lambda e, f=f: e.dma_start(
                            sb["lntile"][:, (f % 2) * 1024:(f % 2) * 1024 + 1024],
                            lnscr[f * 128:(f + 1) * 128, :]))
                    if 1 <= f <= NT:
                        tx = f - 1
                        hrow = ctx.bcreg(8 * cap - 1)
                        ctx.dma("pool", "qp", lambda e, tx=tx, e_=e_, h=hrow:
                                e.indirect_dma_start(
                            gbuf[:],
                            IndirectOffsetOnAxis(omap[:, (e_ + 1) * 16 + tx:
                                                      (e_ + 1) * 16 + tx + 1], 0),
                            sb["lntile"][:, (tx % 2) * 1024:(tx % 2) * 1024 + 1024],
                            None, bounds_check=h["r"], oob_is_err=False))
                if f + 1 < 32:
                    ctx.dma("pool", "qp", lambda e, f=f, e_=e_: e.dma_start(
                        sb["w1buf2"][:, ((f + 1) % 2) * 1024:
                                     ((f + 1) % 2) * 1024 + 1024].rearrange(
                            "p (k f) -> p k f", f=128),
                        w1h[e_, f + 1].rearrange("p k f -> p k f")))
                if 0 <= f < 32:
                    pa, pb = (p0, p1) if f % 2 == 0 else (p2, p3)
                    for k in range(8):
                        ctx.op("pe", lambda e, k=k, f=f, pa=pa: e.matmul(
                            pa[:, 0:w512],
                            sb["w1buf2"][:, (f % 2) * 1024 + k * 128:
                                         (f % 2) * 1024 + (k + 1) * 128],
                            sb["xg"][:, k * cap: k * cap + w512],
                            start=(k == 0), stop=(k == 7)))
                        if cap > 512:
                            ctx.op("pe", lambda e, k=k, f=f, pb=pb: e.matmul(
                                pb[:, 0:cap - 512],
                                sb["w1buf2"][:, (f % 2) * 1024 + k * 128:
                                             (f % 2) * 1024 + (k + 1) * 128],
                                sb["xg"][:, k * cap + 512: k * cap + cap],
                                start=(k == 0), stop=(k == 7)))
                if 1 <= f <= 32:
                    fx = f - 1
                    pa, pb = (p0, p1) if fx % 2 == 0 else (p2, p3)
                    bias = None if layer == 0 else b1_sb[:, e_ * 32 + fx:
                                                         e_ * 32 + fx + 1]
                    if bias is None:
                        ctx.op("act", lambda e, fx=fx, pa=pa: e.activation(
                            sb["hg"][:, fx * cap: fx * cap + w512], pa[:, 0:w512],
                            AF.Gelu))
                        if cap > 512:
                            ctx.op("act", lambda e, fx=fx, pb=pb: e.activation(
                                sb["hg"][:, fx * cap + 512: (fx + 1) * cap],
                                pb[:, 0:cap - 512], AF.Gelu))
                    else:
                        ctx.op("act", lambda e, fx=fx, pa=pa, bias=bias: e.activation(
                            sb["hg"][:, fx * cap: fx * cap + w512], pa[:, 0:w512],
                            AF.Relu, bias=bias))
                        if cap > 512:
                            ctx.op("dve", lambda e, fx=fx, pb=pb, bias=bias:
                                   e.tensor_scalar(
                                sb["hg"][:, fx * cap + 512: (fx + 1) * cap],
                                pb[:, 0:cap - 512], bias, 0.0, ALU.add, ALU.max))
        for dc in (range(-1, 8 + 3) if cutoff >= 231 else []):
            with ctx.stage():
                if -1 <= dc < 7:
                    ctx.dma("pool", "qp", lambda e, dc=dc, e_=e_: e.dma_start(
                        sb["w2buf2"][:, ((dc + 1) % 2) * 4096:
                                     ((dc + 1) % 2) * 4096 + 4096].rearrange(
                            "p (k c) -> p k c", c=128),
                        w2h[e_, dc + 1].rearrange("p k c -> p k c")))
                if 0 <= dc < 8:
                    pa, pb = (p0, p1) if dc % 2 == 0 else (p2, p3)
                    for f in range(32):
                        ctx.op("pe", lambda e, f=f, dc=dc, pa=pa: e.matmul(
                            pa[:, 0:w512],
                            sb["w2buf2"][:, (dc % 2) * 4096 + f * 128:
                                         (dc % 2) * 4096 + (f + 1) * 128],
                            sb["hg"][:, f * cap: f * cap + w512],
                            start=(f == 0), stop=(f == 31)))
                        if cap > 512:
                            ctx.op("pe", lambda e, f=f, dc=dc, pb=pb: e.matmul(
                                pb[:, 0:cap - 512],
                                sb["w2buf2"][:, (dc % 2) * 4096 + f * 128:
                                             (dc % 2) * 4096 + (f + 1) * 128],
                                sb["hg"][:, f * cap + 512: f * cap + cap],
                                start=(f == 0), stop=(f == 31)))
                if 1 <= dc <= 8 and cutoff >= 232:
                    dx = dc - 1
                    pa, pb = (p0, p1) if dx % 2 == 0 else (p2, p3)
                    if layer == 1:
                        bcol = b2c_sb[:, e_ * 8 + dx: e_ * 8 + dx + 1]
                        ctx.op("dve", lambda e, dx=dx, pa=pa, bcol=bcol:
                               e.tensor_scalar_add(
                            sb["ydc"][:, (dx % 2) * 640: (dx % 2) * 640 + w512],
                            pa[:, 0:w512], bcol))
                        if cap > 512:
                            ctx.op("dve", lambda e, dx=dx, pb=pb, bcol=bcol:
                                   e.tensor_scalar_add(
                                sb["ydc"][:, (dx % 2) * 640 + 512:
                                          (dx % 2) * 640 + cap],
                                pb[:, 0:cap - 512], bcol))
                    else:
                        ctx.op("dve", lambda e, dx=dx, pa=pa: e.tensor_copy(
                            sb["ydc"][:, (dx % 2) * 640: (dx % 2) * 640 + w512],
                            pa[:, 0:w512]))
                        if cap > 512:
                            ctx.op("dve", lambda e, dx=dx, pb=pb: e.tensor_copy(
                                sb["ydc"][:, (dx % 2) * 640 + 512:
                                          (dx % 2) * 640 + cap],
                                pb[:, 0:cap - 512]))
                if 2 <= dc <= 9 and cutoff >= 233:
                    dx = dc - 2
                    psx = ctx.psb[dx % 2]
                    for s in range(ntil):
                        ctx.op("pe", lambda e, s=s, dx=dx, psx=psx: e.transpose(
                            psx[:, s * 128:(s + 1) * 128],
                            sb["ydc"][:, (dx % 2) * 640 + s * 128:
                                      (dx % 2) * 640 + (s + 1) * 128],
                            sb["identb"][:]))
                if 3 <= dc <= 10 and cutoff >= 234:
                    dx = dc - 3
                    psx = ctx.psb[dx % 2]
                    for s in range(ntil):
                        ctx.op("dve", lambda e, s=s, dx=dx, psx=psx: e.tensor_copy(
                            sb["ysb"][:, s * 1024 + dx * 128:
                                      s * 1024 + (dx + 1) * 128],
                            psx[:, s * 128:(s + 1) * 128]))
        if cutoff >= 235:
            with ctx.stage():
                for s in range(ntil):
                    ctx.dma("act", "qa", lambda e, s=s, e_=e_: e.dma_start(
                        ybuf[e_ * cap + s * 128: e_ * cap + (s + 1) * 128, :],
                        sb["ysb"][:, s * 1024:(s + 1) * 1024]))
    ybflat = ybuf.rearrange("(o a) b -> o (a b)", o=1)
    if cutoff < 236:
        for t in range(NT):
            with ctx.stage():
                ctx.dma("sq", "qs", lambda e, t=t: e.dma_start(
                    sb["xres3"][:, 0:1024], hin_tile(t)))
            with ctx.stage():
                ctx.dma("sq", "qs", lambda e, t=t: e.dma_start(
                    hout_dram[t * 128:(t + 1) * 128, :], sb["xres3"][:, 0:1024]))
        return
    if layer == 1:
        for t in range(NT + 3):
            with ctx.stage():
                if t < NT:
                    ctx.dma("sq", "qs", lambda e, t=t: e.dma_start(
                        sb["xres3"][:, (t % 3) * 1024:(t % 3) * 1024 + 1024],
                        hin_tile(t)))
                    hel = ctx.bcreg(8 * cap * 1024 - 1)
                    ctx.dma("pool", "qp", lambda e, t=t, h=hel: e.indirect_dma_start(
                        sb["gdst"][:, (t % 2) * 4096:(t % 2) * 4096 + 1024], None,
                        ybflat, IndirectOffsetOnAxis(
                            _oix(ctx, 1, t), 1),
                        bounds_check=h["r"], oob_is_err=False))
                if 1 <= t <= NT:
                    tx = t - 1
                    ctx.op("dve", lambda e, tx=tx: e.scalar_tensor_tensor(
                        sb["xres3"][:, (tx % 3) * 1024:(tx % 3) * 1024 + 1024],
                        sb["gdst"][:, (tx % 2) * 4096:(tx % 2) * 4096 + 1024],
                        resid_scale,
                        sb["xres3"][:, (tx % 3) * 1024:(tx % 3) * 1024 + 1024],
                        ALU.mult, ALU.add))
                if 2 <= t <= NT + 1:
                    tx = t - 2
                    ctx.dma("act", "qa", lambda e, tx=tx: e.dma_start(
                        hout_dram[tx * 128:(tx + 1) * 128, :],
                        sb["xres3"][:, (tx % 3) * 1024:(tx % 3) * 1024 + 1024]))
    else:
        steps = [(t, g) for t in range(NT) for g in range(2)]
        for si in range(len(steps) + 2):
            with ctx.stage():
                if si < len(steps):
                    t, g = steps[si]
                    if g == 0:
                        ctx.dma("sq", "qs", lambda e, t=t: e.dma_start(
                            sb["xres3"][:, (t % 3) * 1024:(t % 3) * 1024 + 1024],
                            hin_tile(t)))
                    hel = ctx.bcreg(8 * cap * 1024 - 1)
                    for j in range(4):
                        e_ = g * 4 + j
                        ctx.dma("pool", "qp", lambda e, t=t, e_=e_, j=j, si=si, h=hel:
                                e.indirect_dma_start(
                            sb["gdst"][:, (si % 2) * 4096 + j * 1024:
                                       (si % 2) * 4096 + (j + 1) * 1024], None,
                            ybflat, IndirectOffsetOnAxis(
                                _oix(ctx, 0, t, e_), 1),
                            bounds_check=h["r"], oob_is_err=False))
                if 1 <= si <= len(steps):
                    tx, gx = steps[si - 1]
                    for j in range(4):
                        e_ = gx * 4 + j
                        wcol = sb["selwh"][:, tx * 8 + e_: tx * 8 + e_ + 1]
                        ctx.op("dve", lambda e, tx=tx, j=j, si=si, wcol=wcol:
                               e.scalar_tensor_tensor(
                            sb["xres3"][:, (tx % 3) * 1024:(tx % 3) * 1024 + 1024],
                            sb["gdst"][:, ((si - 1) % 2) * 4096 + j * 1024:
                                       ((si - 1) % 2) * 4096 + (j + 1) * 1024],
                            wcol,
                            sb["xres3"][:, (tx % 3) * 1024:(tx % 3) * 1024 + 1024],
                            ALU.mult, ALU.add))
                if si >= 2 and steps[si - 2][1] == 1:
                    tx = steps[si - 2][0]
                    ctx.dma("act", "qa", lambda e, tx=tx: e.dma_start(
                        hout_dram[tx * 128:(tx + 1) * 128, :],
                        sb["xres3"][:, (tx % 3) * 1024:(tx % 3) * 1024 + 1024]))


def _oix(ctx, layer, t, e_=0):
    if layer == 1:
        return ctx.sb["oel"][:, t:t + 1]
    return ctx.sb["oel"][:, e_ * 16 + t: e_ * 16 + t + 1]


# version 34 (final)
# speedup vs baseline: 1.0477x; 1.0477x over previous
import contextlib
import numpy as np
import ml_dtypes
import concourse.bass as bass
import concourse.mybir as mybir
from concourse.bass_utils import run_bass_kernel_spmd
from concourse.bass_core_rust import IndirectOffsetOnAxis

F32 = mybir.dt.float32
BF16 = mybir.dt.bfloat16
I32 = mybir.dt.int32
U32 = mybir.dt.uint32
AF = mybir.ActivationFunctionType
ALU = mybir.AluOpType
AX = mybir.AxisListType

NCORES = 8
B, T, D = 2, 8192, 1024
TPC = T * B // NCORES
NT = TPC // 128
DF = 4096
NE = 8
L1CAP = 384
NH0, DH0 = 16, 64
H1, DK1, DV1 = 8, 128, 128
CCH = 128
KCONV = 4

bfnp = ml_dtypes.bfloat16



def _rope_perm():
    perm = np.zeros(D, np.int64)
    for tau in range(D // 128):
        base = 128 * tau
        for k in range(64):
            perm[base + k] = base + 2 * k
            perm[base + 64 + k] = base + 2 * k + 1
    return perm


def host_prep(inp):
    sh = {}
    f32 = lambda a: np.ascontiguousarray(np.asarray(a), dtype=np.float32)

    perm = _rope_perm()

    sh["gateT"] = f32(inp["L0_ffn_gate_w"].T)
    sh["l0w1"] = f32(inp["L0_ffn_w1"])
    sh["l0w2"] = f32(inp["L0_ffn_w2"])

    WqT = f32(inp["L0_attn_Wq"].T)[:, perm]
    WkT = f32(inp["L0_attn_Wk"].T)[:, perm]
    WvT = f32(inp["L0_attn_Wv"].T)
    sh["WqT"] = np.ascontiguousarray(WqT)
    sh["WkT"] = np.ascontiguousarray(WkT)
    sh["WvT"] = WvT
    sh["bq"] = f32(inp["L0_attn_bq"])[perm].reshape(D, 1)
    sh["bk"] = f32(inp["L0_attn_bk"])[perm].reshape(D, 1)
    sh["bv"] = f32(inp["L0_attn_bv"]).reshape(1, D)
    sh["gam0"] = f32(inp["L0_attn_gamma"]).reshape(1, D)
    sh["bet0"] = f32(inp["L0_attn_beta"]).reshape(1, D)
    theta = 10000.0
    inv_freq = (1.0 / theta ** (np.arange(0, D, 2, dtype=np.float32) / np.float32(D)))
    inv_freq = inv_freq.astype(np.float32)
    pos = np.arange(T, dtype=np.float32)
    ang = pos[:, None] * inv_freq[None, :]
    cos_t, sin_t = np.cos(ang).astype(np.float32), np.sin(ang).astype(np.float32)
    sh["_cosT"] = np.ascontiguousarray(cos_t.T)
    sh["_sinT"] = np.ascontiguousarray(sin_t.T)
    cdup = np.empty((1024, T), np.float32)
    sdup = np.empty((1024, T), np.float32)
    for tau in range(8):
        blk_c = cos_t.T[tau * 64:(tau + 1) * 64, :]
        blk_s = sin_t.T[tau * 64:(tau + 1) * 64, :]
        cdup[tau * 128:tau * 128 + 64] = blk_c
        cdup[tau * 128 + 64:tau * 128 + 128] = blk_c
        sdup[tau * 128:tau * 128 + 64] = blk_s
        sdup[tau * 128 + 64:tau * 128 + 128] = blk_s
    sh["_cosDup"] = cdup
    sh["_sinDup"] = sdup
    kvm = np.zeros((128, 128), np.float32)
    h0set = list(range(0, 32)) + list(range(64, 96))
    h1set = list(range(32, 64)) + list(range(96, 128))
    for p in h0set:
        kvm[p, 0:64] = 1.0
    for p in h1set:
        kvm[p, 64:128] = 1.0
    sh["kvmask"] = kvm
    hm = np.zeros((128, 2), np.float32)
    hm[h0set, 0] = 1.0
    hm[h1set, 1] = 1.0
    sh["headmask2"] = hm

    sh["WrT"] = f32(inp["L1_ffn_Wr"].T)
    sh["br"] = f32(inp["L1_ffn_br"]).reshape(8, 1)
    sh["l1w1T"] = np.ascontiguousarray(f32(inp["L1_ffn_W1"]).transpose(0, 2, 1))
    sh["l1b1"] = f32(inp["L1_ffn_b1"])
    sh["l1w2T"] = np.ascontiguousarray(f32(inp["L1_ffn_W2"]).transpose(0, 2, 1))
    sh["l1b2"] = f32(inp["L1_ffn_b2"])

    sh["qkvT"] = f32(inp["L1_attn_qkv_w"].T)
    sh["zT"] = f32(inp["L1_attn_z_w"].T)
    sh["baT"] = np.concatenate([f32(inp["L1_attn_b_w"].T), f32(inp["L1_attn_a_w"].T)],
                               axis=1)
    sh["convw"] = f32(inp["L1_attn_conv_w"][:, 0, :])
    sh["dtb"] = f32(inp["L1_attn_dt_bias"]).reshape(8, 1)
    sh["Alog"] = f32(inp["L1_attn_A_log"]).reshape(8, 1)
    sh["normw"] = f32(inp["L1_attn_norm_w"]).reshape(1, 128)
    sh["outT"] = f32(inp["L1_attn_out_w"].T)

    sh["ident"] = np.eye(128, dtype=np.float32)
    sh["identb"] = np.eye(128).astype(bfnp)
    sh["ones_row"] = np.ones((1, 512), np.float32)
    sh["triu_incl"] = np.triu(np.ones((128, 128), np.float32), 0)
    bd = np.zeros((128, 8), np.float32)
    for e in range(8):
        bd[e * 16:(e + 1) * 16, e] = 1.0
    sh["bd16"] = bd
    sh["bd16T"] = np.ascontiguousarray(bd.T)
    iidx = np.arange(128)
    up_s = np.where(iidx[:, None] < iidx[None, :], 0.0, -1e30).astype(np.float32)
    lo_s = np.where(iidx[:, None] > iidx[None, :], 0.0, -1e30).astype(np.float32)
    up_i = np.where(iidx[:, None] <= iidx[None, :], 0.0, -1e30).astype(np.float32)
    sh["maskUs"] = up_s
    sh["maskLs"] = lo_s
    sh["maskUi"] = up_i
    sh["iota_col"] = np.arange(128, dtype=np.float32).reshape(128, 1)
    sh["Lstrict"] = np.triu(np.ones((128, 128), np.float32), 1)
    sh["l1b2c"] = np.ascontiguousarray(f32(inp["L1_ffn_b2"]).reshape(8, 8, 128)
                                       .transpose(2, 0, 1).reshape(128, 64))
    w1l0 = f32(inp["L0_ffn_w1"])
    w1l1 = np.ascontiguousarray(f32(inp["L1_ffn_W1"]).transpose(0, 2, 1))
    w2l0 = f32(inp["L0_ffn_w2"])
    w2l1 = np.ascontiguousarray(f32(inp["L1_ffn_W2"]).transpose(0, 2, 1))
    def tile_w1(w):
        return np.ascontiguousarray(
            w.reshape(8, 8, 128, 32, 128).transpose(0, 3, 2, 1, 4))
    def tile_w2(w):
        return np.ascontiguousarray(
            w.reshape(8, 32, 128, 8, 128).transpose(0, 3, 2, 1, 4))
    sh["w1h_l0"] = tile_w1(w1l0)
    sh["w1h_l1"] = tile_w1(w1l1)
    sh["w2h_l0"] = tile_w2(w2l0)
    sh["w2h_l1"] = tile_w2(w2l1)
    P = np.zeros((128, 128), np.float32)
    for p in range(64):
        P[p, p + 64] = -1.0
        P[p + 64, p] = 1.0
    sh["pswapT"] = np.ascontiguousarray(P.T)

    percore = []
    x_full = f32(inp["L0_attn_x"]).reshape(B * T, D)
    for c in range(NCORES):
        pc = {}
        pc["x0"] = np.ascontiguousarray(x_full[c * TPC:(c + 1) * TPC])
        b = c // 4
        off = (c % 4) * TPC
        pc["cosT"] = np.ascontiguousarray(sh["_cosT"][:, off:off + TPC])
        pc["sinT"] = np.ascontiguousarray(sh["_sinT"][:, off:off + TPC])
        pc["cosDup"] = np.ascontiguousarray(sh["_cosDup"][:, off:off + TPC])
        pc["sinDup"] = np.ascontiguousarray(sh["_sinDup"][:, off:off + TPC])
        percore.append(pc)
    return sh, percore



ENGS = ("pe", "dve", "act", "pool", "sq")
QSEMS = ("qs", "qa", "qp", "cc")


class Ctx:
    def __init__(self, nc):
        self.nc = nc
        self.st = contextlib.ExitStack()
        self.sem = {}
        for s in ENGS + QSEMS:
            self.sem[s] = self.st.enter_context(nc.semaphore("sem_" + s))
        self.cnt = {s: 0 for s in ENGS + QSEMS}
        self.W = {e: {s: 0 for s in ENGS + QSEMS} for e in ENGS}
        self.block = self.st.enter_context(nc.Block())
        self._stage_ops = None
        self._dep = None
        self.snaps = {}
        self.sb = {}
        self.ps = []

    def alloc(self, name, shape, dtype):
        stk = getattr(self, "pstack", None) or self.st
        uname = name if not getattr(self, "pprefix", "") else self.pprefix + name
        h = stk.enter_context(self.nc.sbuf_tensor(uname, list(shape), dtype))
        self.sb[name] = h
        return h

    def alloc_psum(self, name, shape, dtype):
        stk = getattr(self, "pstack", None) or self.st
        uname = name if not getattr(self, "pprefix", "") else self.pprefix + name
        h = stk.enter_context(self.nc.psum_tensor(uname, list(shape), dtype))
        return h

    def push_phase(self, prefix):
        self.pstack = contextlib.ExitStack()
        self.pprefix = prefix
        self.sb = {}

    def pop_phase(self):
        self.pstack.close()
        self.pstack = None
        self.pprefix = ""

    def bcreg(self, value):
        if not hasattr(self, "_bcregs"):
            self._bcregs = {}
        if value not in self._bcregs:
            holder = {}

            def mk(eng, value=value, holder=holder):
                r = eng.alloc_register()
                holder["r"] = r
                return eng.reg_mov(r, value)
            self._emit("pool", mk)
            self._bcregs[value] = holder
        return self._bcregs[value]

    def snapshot(self):
        return dict(self.cnt)

    def mask_sem(self, name):
        if not hasattr(self, "_mask"):
            self._mask = {}
        self._mask[name] = self.cnt[name]

    def mask_at(self, name, value):
        if not hasattr(self, "_mask"):
            self._mask = {}
        self._mask[name] = max(self._mask.get(name, 0), value)

    def unmask_sem(self, name):
        if hasattr(self, "_mask"):
            self._mask.pop(name, None)

    def begin(self, deps=None):
        assert self._stage_ops is None, "stage already open"
        if deps is None:
            self._dep = dict(self.cnt)
        else:
            d = {s: 0 for s in ENGS + QSEMS}
            for snap in deps:
                for s, v in snap.items():
                    d[s] = max(d[s], v)
            self._dep = d
        for s, v in getattr(self, "_mask", {}).items():
            self._dep[s] = min(self._dep[s], v)
        self._stage_ops = {e: [] for e in ENGS}

    def _emit(self, engname, fn, is_dma=None, inc_extra=0):
        assert self._stage_ops is not None, "no open stage"
        self._stage_ops[engname].append((fn, is_dma))
        if is_dma is not None:
            self.cnt[is_dma] += 16 + inc_extra

    def op(self, engname, fn):
        self._emit(engname, fn)

    def dma(self, engname, qsem, fn):
        self._emit(engname, fn, is_dma=qsem)

    def cc(self, fn):
        self._stage_ops["pool"].append((fn, "__cc__"))
        self.cnt["cc"] += 1

    def end(self):
        ops = self._stage_ops
        dep = self._dep
        self._stage_ops = None
        self._dep = None
        eng_map = {
            "pe": self.block.tensor, "dve": self.block.vector,
            "act": self.block.scalar, "pool": self.block.gpsimd,
            "sq": self.block.sync,
        }
        for engname, lst in ops.items():
            if not lst:
                continue
            waits = []
            for s, v in dep.items():
                if v > self.W[engname][s]:
                    waits.append((self.sem[s], v))
                    self.W[engname][s] = v
            n_comp = sum(1 for _, d in lst if d is None)
            sem = self.sem
            cnt_inc = [0]

            def body(eng, lst=lst, waits=waits, engname=engname, n_comp=n_comp):
                for sh, v in waits:
                    eng.wait_ge(sh, v)
                comp_seen = 0
                for fn, dtag in lst:
                    inst = fn(eng)
                    if dtag is None:
                        comp_seen += 1
                        if comp_seen == n_comp and inst is not None:
                            inst.then_inc(sem[engname], 1)
                    elif dtag == "__cc__":
                        inst.then_inc(sem["cc"], 1)
                    else:
                        inst.then_inc(sem[dtag], 16)
            eng_map[engname](body)
            if n_comp:
                self.cnt[engname] += 1

    def stage(self, deps=None):
        ctx = self

        class _G:
            def __enter__(self):
                ctx.begin(deps)
                return ctx

            def __exit__(self, *a):
                ctx.end()
                return False
        return _G()

    def barrier_snapshot(self):
        return self.snapshot()



def emit_ln_stats(ctx, x_tiles_dma, n_tiles=NT):
    nc = ctx.nc
    xrot = ctx.sb["xrot"]
    sq = ctx.sb["sq_scr"]
    sm = ctx.sb["smal"]; ss = ctx.sb["ssal"]
    with ctx.stage():
        ctx.dma("sq", "qs", lambda e: e.dma_start(xrot[:, 0:1024], x_tiles_dma(0)))
    for t in range(n_tiles):
        r = t % 2
        with ctx.stage():
            if t + 1 < n_tiles:
                ctx.dma("sq", "qs", lambda e, t=t: e.dma_start(
                    xrot[:, ((t + 1) % 2) * 1024:((t + 1) % 2) * 1024 + 1024],
                    x_tiles_dma(t + 1)))
            ctx.op("act", lambda e, t=t, r=r: e.activation(
                sq[:, r * 1024:(r + 1) * 1024], xrot[:, r * 1024:(r + 1) * 1024],
                AF.Square, accum_out=ss[:, t:t + 1]))
            ctx.op("dve", lambda e, t=t, r=r: e.reduce_sum(
                sm[:, t:t + 1], xrot[:, r * 1024:(r + 1) * 1024], axis=AX.X))
    mean = ctx.sb["mean"]; var = ctx.sb["var"]; rstd = ctx.sb["rstd"]
    nbias = ctx.sb["nbias"]
    nt = n_tiles
    with ctx.stage():
        ctx.op("dve", lambda e: e.tensor_scalar_mul(mean[:, :nt], sm[:, :nt], 1.0 / D))
        ctx.op("dve", lambda e: e.tensor_scalar_mul(var[:, :nt], ss[:, :nt], 1.0 / D))
    with ctx.stage():
        ctx.op("dve", lambda e: e.scalar_tensor_tensor(
            sq[:, 0:nt], mean[:, :nt], -1.0, mean[:, :nt], ALU.mult, ALU.mult))
    with ctx.stage():
        ctx.op("dve", lambda e: e.tensor_add(var[:, :nt], var[:, :nt], sq[:, 0:nt]))
    with ctx.stage():
        ctx.op("dve", lambda e: e.tensor_scalar_add(var[:, :nt], var[:, :nt],
                                                    float(LN_EPS)))
    with ctx.stage():
        ctx.op("act", lambda e: e.activation(rstd[:, :nt], var[:, :nt], AF.Sqrt))
    with ctx.stage():
        ctx.op("dve", lambda e: e.reciprocal(rstd[:, :nt], rstd[:, :nt]))
    with ctx.stage():
        ctx.op("dve", lambda e: e.scalar_tensor_tensor(
            nbias[:, :nt], mean[:, :nt], -1.0, rstd[:, :nt], ALU.mult, ALU.mult))


LN_EPS = 1e-5


def _cp(engname, dst_fn, src_fn):
    if engname == "act":
        return lambda e: e.copy(dst_fn(), src_fn())
    return lambda e: e.tensor_copy(dst_fn(), src_fn())



def emit_ln_apply_tp(ctx, x_tiles_dma, router_sb, psA, psB, psL, br_sb,
                     logits_fm, n_tiles=NT, probs_cb=None, write_scr=None,
                     lnbf_dram=None):
    xrot = ctx.sb["xrot"]; lnrot = ctx.sb["lnrot"]
    x32rot = ctx.sb.get("x32rot")
    rstd = ctx.sb["rstd"]; nbias = ctx.sb["nbias"]
    xfm = ctx.sb.get("xfm")
    ident = ctx.sb["ident"]
    for t in range(n_tiles):
        r = t % 2
        xs = xrot[:, r * 1024:(r + 1) * 1024]
        ls = lnrot[:, r * 1024:(r + 1) * 1024]
        x32 = x32rot[:, r * 1024:(r + 1) * 1024] if x32rot is not None else None
        with ctx.stage():
            ctx.dma("sq", "qs", lambda e, t=t, xs=xs: e.dma_start(xs, x_tiles_dma(t)))
        with ctx.stage():
            ctx.op("dve", lambda e, t=t, xs=xs, ls=ls: e.tensor_scalar(
                ls, xs, ctx.sb["mean"][:, t:t + 1], rstd[:, t:t + 1],
                ALU.subtract, ALU.mult))
        with ctx.stage():
            for k in range(8):
                ps = psA if k < 4 else psB
                ctx.op("pe", lambda e, k=k, ls=ls, ps=ps: e.transpose(
                    ps[:, (k % 4) * 128:(k % 4) * 128 + 128],
                    ls[:, k * 128:(k + 1) * 128], ident[:]))
        with ctx.stage():
            for k in range(8):
                ps = psA if k < 4 else psB
                src = lambda k=k, ps=ps: ps[:, (k % 4) * 128:(k % 4) * 128 + 128]
                if xfm is not None:
                    ctx.op("dve", lambda e, k=k, t=t, src=src: e.tensor_copy(
                        xfm[:, k * TPC + t * 128: k * TPC + (t + 1) * 128], src()))
                if x32 is not None:
                    eng = "act" if xfm is not None else ("dve" if k % 2 else "act")
                    ctx.op(eng, _cp(eng,
                        lambda k=k, x32=x32: x32[:, k * 128:(k + 1) * 128], src))
            if write_scr is not None:
                write_scr(t, ls)
            if lnbf_dram is not None:
                ctx.op("pool", lambda e, t=t, r=r, ls=ls: e.tensor_copy(
                    ctx.sb["lnbf"][:, r * 1024:(r + 1) * 1024], ls))
        if lnbf_dram is not None:
            with ctx.stage():
                ctx.dma("pool", "qp", lambda e, t=t, r=r: e.dma_start(
                    lnbf_dram[t * 128:(t + 1) * 128, :],
                    ctx.sb["lnbf"][:, r * 1024:(r + 1) * 1024]))
        if router_sb is None:
            continue
        with ctx.stage():
            for k in range(8):
                ctx.op("pe", lambda e, k=k, x32=x32: e.matmul(
                    psL[0:8, 0:128], router_sb[:, k * 8:(k + 1) * 8],
                    x32[:, k * 128:(k + 1) * 128],
                    start=(k == 0), stop=(k == 7)))
        with ctx.stage():
            if br_sb is not None:
                ctx.op("dve", lambda e, t=t: e.tensor_scalar_add(
                    logits_fm[0:8, t * 128:(t + 1) * 128], psL[0:8, 0:128],
                    br_sb[0:8, 0:1]))
            else:
                ctx.op("act", lambda e, t=t: e.copy(
                    logits_fm[0:8, t * 128:(t + 1) * 128], psL[0:8, 0:128]))


def emit_row_via_dram(ctx, src_sb_slice, scr_dram, dst_ap, src_shape, rearr=None):
    with ctx.stage():
        ctx.dma("sq", "qs", lambda e: e.dma_start(scr_dram[:src_shape[0], :src_shape[1]],
                                                  src_sb_slice))
    with ctx.stage():
        rd = rearr(scr_dram) if rearr is not None else scr_dram[:src_shape[0], :src_shape[1]]
        ctx.dma("sq", "qs", lambda e, rd=rd: e.dma_start(dst_ap, rd))


def emit_moe_dense(ctx, layer, hin_tile, hout_dram, w1_all=None, w2_all=None,
                   b1_sb=None, b2row_sb=None, selrow_fn=None, psum=None,
                   resid_scale=0.5):
    p0, p1, p2, p3, p4, p5, p6, p7 = psum
    quadA = (p0, p1, p2, p3)
    quadB = (p4, p5, p6, p7)
    xfm = ctx.sb["xfm"]
    hbuf = ctx.sb["hbuf"]
    w1buf = ctx.sb["w1buf"]
    w2buf = ctx.sb["w2buf"]
    yacc = ctx.sb["yacc"]
    mbc = ctx.sb["maskbc"]
    ytmp = ctx.sb["ytmp"]
    ones1 = ctx.sb["ones1"]
    xres = ctx.sb["xrot"]
    resrot = ctx.sb["lnrot"]
    act_fn = AF.Gelu if layer == 0 else AF.Relu

    def mm1_dma(e, f):
        ctx.dma("pool", "qp", lambda en, f=f, e=e: en.dma_start(
            w1buf[:, (f % 2) * 1024:(f % 2) * 1024 + 1024].rearrange(
                "p (k f) -> p k f", f=128),
            w1_all[e, :, f * 128:(f + 1) * 128].rearrange("(k p) f -> p k f", p=128)))

    def mm1_mms(tok0, f):
        pa, pb = (quadA[0], quadA[1]) if f % 2 == 0 else (quadA[2], quadA[3])
        for k in range(8):
            for g in range(2):
                psx = pa if g == 0 else pb
                ctx.op("pe", lambda en, k=k, g=g, f=f, psx=psx: en.matmul(
                    psx[:],
                    w1buf[:, (f % 2) * 1024 + k * 128:(f % 2) * 1024 + (k + 1) * 128],
                    xfm[:, k * TPC + tok0 + g * 512: k * TPC + tok0 + (g + 1) * 512],
                    start=(k == 0), stop=(k == 7)))

    def mm1_evac(e, f):
        pa, pb = (quadA[0], quadA[1]) if f % 2 == 0 else (quadA[2], quadA[3])
        bias = None if layer == 0 else b1_sb[:, e * 32 + f: e * 32 + f + 1]
        if bias is None:
            ctx.op("act", lambda en, f=f, pa=pa: en.activation(
                hbuf[:, f * 1024: f * 1024 + 512], pa[:], act_fn))
            ctx.op("act", lambda en, f=f, pb=pb: en.activation(
                hbuf[:, f * 1024 + 512: (f + 1) * 1024], pb[:], AF.Gelu))
        else:
            ctx.op("act", lambda en, f=f, pa=pa, bias=bias: en.activation(
                hbuf[:, f * 1024: f * 1024 + 512], pa[:], act_fn, bias=bias))
            ctx.op("dve", lambda en, f=f, pb=pb, bias=bias: en.tensor_scalar(
                hbuf[:, f * 1024 + 512: (f + 1) * 1024], pb[:],
                bias, 0.0, ALU.add, ALU.max))

    for half in range(2):
        tok0 = half * 1024
        for e in range(NE):
            selrow = ctx.sb["selrow"]
            selrow_fn(ctx, e, half)
            with ctx.stage():
                for g in range(2):
                    psx = p4 if g == 0 else p7
                    ctx.op("pe", lambda e2, e=e, g=g, psx=psx: e2.matmul(
                        psx[:], ones1[0:1, 0:128],
                        selrow[0:1, g * 512:(g + 1) * 512], start=True, stop=True))
            with ctx.stage():
                ctx.op("dve", lambda e2: e2.tensor_copy(mbc[:, 0:512], p4[:]))
                ctx.op("act", lambda e2: e2.copy(mbc[:, 512:1024], p7[:]))
            for f in range(-1, 33):
                with ctx.stage():
                    if f + 1 < 32:
                        mm1_dma(e, f + 1)
                    if 0 <= f < 32:
                        mm1_mms(tok0, f)
                    if 1 <= f + 0 and f - 1 >= 0 and f - 1 < 32 and f >= 1:
                        mm1_evac(e, f - 1)
            last_of_dp = 3 if layer == 0 else -2
            steps = [(dp, kq) for dp in range(4) for kq in range(4)]
            for si in range(-1, len(steps) + 1):
                with ctx.stage():
                    if si + 1 < len(steps):
                        dp_n, kq_n = steps[si + 1]
                        ctx.dma("pool", "qp", lambda en, dp=dp_n, kq=kq_n, e=e:
                                en.dma_start(
                            w2buf[:, (kq % 2) * 2048:(kq % 2) * 2048 + 2048].rearrange(
                                "p (k c) -> p k c", c=256),
                            w2_all[e, kq * 1024:(kq + 1) * 1024,
                                   dp * 256:(dp + 1) * 256].rearrange(
                                "(k p) c -> p k c", p=128)))
                        if kq_n == 0 and layer == 1:
                            ctx.dma("sq", "qs", lambda e2, e=e, dp=dp_n: e2.dma_start(
                                ctx.sb["b2e_sb"][0:1, 0:256],
                                b2row_sb[0:1, e * 1024 + dp * 256:
                                         e * 1024 + (dp + 1) * 256]))
                    if 0 <= si < len(steps):
                        dp, kq = steps[si]
                        quad = quadA if dp % 2 == 0 else quadB
                        for k8 in range(8):
                            k = kq * 8 + k8
                            for dg in range(4):
                                dsel, g = dg // 2, dg % 2
                                psx = quad[dg]
                                st = (k == 0)
                                sp = (k == 31 and layer == 0)
                                ctx.op("pe", lambda en, k=k, kq=kq, dsel=dsel, g=g,
                                       psx=psx, st=st, sp=sp: en.matmul(
                                    psx[:],
                                    w2buf[:, (kq % 2) * 2048 + k8_off(k, dsel):
                                          (kq % 2) * 2048 + k8_off(k, dsel) + 128],
                                    hbuf[:, k * 1024 + g * 512: k * 1024 + (g + 1) * 512],
                                    start=st, stop=sp))
                        if kq == 3 and layer == 1:
                            for dg in range(4):
                                dsel, g = dg // 2, dg % 2
                                psx = quad[dg]
                                d = dp * 2 + dsel
                                ctx.op("pe", lambda e2, d=d, dsel=dsel, psx=psx: e2.matmul(
                                    psx[:], ctx.sb["b2e_sb"][0:1, dsel * 128:
                                                             (dsel + 1) * 128],
                                    ones1[0:1, 0:512], start=False, stop=True,
                                    skip_group_check=True))
                    if 0 <= si - 0 and si >= 4 and (si % 4 == 0 or si == len(steps)):
                        dp_e = steps[si - 1][0] if si <= len(steps) - 1 else 3
                        dp_e = (si // 4) - 1
                        quad = quadA if dp_e % 2 == 0 else quadB
                        for dg in range(4):
                            dsel, g = dg // 2, dg % 2
                            psx = quad[dg]
                            d = dp_e * 2 + dsel
                            ysl = yacc[:, d * 1024 + g * 512: d * 1024 + (g + 1) * 512]
                            if e == 0:
                                ctx.op("dve", lambda e2, psx=psx, ysl=ysl, g=g:
                                       e2.tensor_mul(ysl, psx[:],
                                                     mbc[:, g * 512:(g + 1) * 512]))
                            else:
                                tsl = ytmp[:, dg * 512:(dg + 1) * 512]
                                ctx.op("dve", lambda e2, psx=psx, tsl=tsl, g=g:
                                       e2.tensor_mul(tsl, psx[:],
                                                     mbc[:, g * 512:(g + 1) * 512]))
                    if e > 0 and si >= 6 and si % 4 == 2:
                        _moe_pool_add(ctx, yacc, ytmp, (si // 4) - 1)
                if si == len(steps) and e > 0:
                    with ctx.stage():
                        _moe_pool_add(ctx, yacc, ytmp, 3)
        for tb in range(0, 10):
            with ctx.stage():
                if tb < 8:
                    t_g = half * 8 + tb
                    ctx.dma("sq", "qs", lambda e2, t=t_g, tb=tb: e2.dma_start(
                        xres[:, (tb % 2) * 1024:(tb % 2) * 1024 + 1024],
                        hin_tile(t)))
                    pA, pB = (p4, p5) if tb % 2 == 0 else (p6, p7)
                    for d in range(8):
                        psx = pA if d < 4 else pB
                        ctx.op("pe", lambda e2, d=d, tb=tb, psx=psx: e2.transpose(
                            psx[:, (d % 4) * 128:(d % 4) * 128 + 128],
                            yacc[:, d * 1024 + tb * 128: d * 1024 + (tb + 1) * 128],
                            ctx.sb["ident"][:]))
                if 1 <= tb <= 8:
                    tbx = tb - 1
                    pA, pB = (p4, p5) if tbx % 2 == 0 else (p6, p7)
                    for d in range(8):
                        psx = pA if d < 4 else pB
                        ctx.op("dve", lambda e2, d=d, tbx=tbx, psx=psx:
                               e2.scalar_tensor_tensor(
                            resrot[:, (tbx % 2) * 1024 + d * 128:
                                   (tbx % 2) * 1024 + (d + 1) * 128],
                            psx[:, (d % 4) * 128:(d % 4) * 128 + 128],
                            resid_scale,
                            xres[:, (tbx % 2) * 1024 + d * 128:
                                 (tbx % 2) * 1024 + (d + 1) * 128],
                            ALU.mult, ALU.add))
                if 2 <= tb <= 9:
                    tbx = tb - 2
                    t_g = half * 8 + tbx
                    ctx.dma("act", "qa", lambda e2, t=t_g, tbx=tbx: e2.dma_start(
                        hout_dram[t * 128:(t + 1) * 128, :],
                        resrot[:, (tbx % 2) * 1024:(tbx % 2) * 1024 + 1024]))


def k8_off(k, dsel):
    return (k % 8) * 256 + dsel * 128


def _moe_pool_add(ctx, yacc, ytmp, dp_e):
    for dg in range(4):
        dsel, g = dg // 2, dg % 2
        d = dp_e * 2 + dsel
        ysl = yacc[:, d * 1024 + g * 512: d * 1024 + (g + 1) * 512]
        ctx.op("pool", lambda e2, dg=dg, ysl=ysl: e2.tensor_add(
            ysl, ysl, ytmp[:, dg * 512:(dg + 1) * 512]))




def emit_ln_fused(ctx, x_tiles_dma, router_sb, psT4, psL2, br_sb, logits_fm,
                  n_tiles=NT, write_scr=None, lnbf_dram=None):
    a = ctx.alloc
    if "lnf_xr" not in ctx.sb:
        a("lnf_xr", [128, 8 * 1024], F32)
        a("lnf_ln", [128, 2 * 1024], F32)
        a("lnf_sq", [128, 1024], F32)
        a("lnf_eps", [128, 1], F32)
    xr = ctx.sb["lnf_xr"]; lnr = ctx.sb["lnf_ln"]; sqb = ctx.sb["lnf_sq"]
    eps = ctx.sb["lnf_eps"]
    sm = ctx.sb["smal"]; ss = ctx.sb["ssal"]
    mean = ctx.sb["mean"]; var = ctx.sb["var"]; rstd = ctx.sb["rstd"]
    msq = ctx.sb["nbias"]
    xfm = ctx.sb.get("xfm")
    x32 = ctx.sb.get("x32rot") if router_sb is not None else None
    ident = ctx.sb["ident"]
    xs = lambda k: xr[:, (k % 8) * 1024:(k % 8) * 1024 + 1024]
    ls = lambda k: lnr[:, (k % 2) * 1024:(k % 2) * 1024 + 1024]
    with ctx.stage():
        ctx.op("dve", lambda e: e.memset(eps[:], 0.0))
        ctx.dma("sq", "qs", lambda e: e.dma_start(xs(0), x_tiles_dma(0)))
        ctx.dma("sq", "qs", lambda e: e.dma_start(xs(1), x_tiles_dma(1)))
    with ctx.stage():
        ctx.op("dve", lambda e: e.tensor_scalar_add(eps[:], eps[:], float(LN_EPS)))
    NTT = n_tiles
    for t in range(NTT + 8):
        with ctx.stage():
            if t + 2 < NTT:
                ctx.dma("sq", "qs", lambda e, t=t: e.dma_start(
                    xs(t + 2), x_tiles_dma(t + 2)))
            if t < NTT:
                ctx.op("dve", lambda e, t=t: e.tensor_mul(sqb[:], xs(t), xs(t)))
                ctx.op("dve", lambda e, t=t: e.reduce_sum(ss[:, t:t + 1], sqb[:],
                                                          axis=AX.X))
                ctx.op("dve", lambda e, t=t: e.reduce_sum(sm[:, t:t + 1], xs(t),
                                                          axis=AX.X))
            if 1 <= t <= NTT:
                k = t - 1
                ctx.op("dve", lambda e, k=k: e.tensor_scalar_mul(
                    mean[:, k:k + 1], sm[:, k:k + 1], 1.0 / D))
            if 2 <= t <= NTT + 1:
                k = t - 2
                ctx.op("dve", lambda e, k=k: e.scalar_tensor_tensor(
                    var[:, k:k + 1], ss[:, k:k + 1], 1.0 / D, msq[:, k:k + 1],
                    ALU.mult, ALU.add))
            if 4 <= t <= NTT + 3:
                k = t - 4
                ctx.op("dve", lambda e, k=k: e.reciprocal(rstd[:, k:k + 1],
                                                          rstd[:, k:k + 1]))
            if 6 <= t <= NTT + 5:
                k = t - 6
                pr = psT4[(k % 2) * 2:(k % 2) * 2 + 2]
                for kk in range(8):
                    psx = pr[0] if kk < 4 else pr[1]
                    ctx.op("pe", lambda e, kk=kk, k=k, psx=psx: e.transpose(
                        psx[:, (kk % 4) * 128:(kk % 4) * 128 + 128],
                        ls(k)[:, kk * 128:(kk + 1) * 128], ident[:]))
            if router_sb is not None and 7 <= t <= NTT + 6:
                k = t - 7
                psl = psL2[k % 2]
                for kk in range(8):
                    ctx.op("pe", lambda e, kk=kk, k=k, psl=psl: e.matmul(
                        psl[0:8, 0:128], router_sb[:, kk * 8:(kk + 1) * 8],
                        x32[:, (k % 2) * 1024 + kk * 128:
                            (k % 2) * 1024 + (kk + 1) * 128],
                        start=(kk == 0), stop=(kk == 7)))
        with ctx.stage():
            if 1 <= t <= NTT:
                k = t - 1
                ctx.op("dve", lambda e, k=k: e.scalar_tensor_tensor(
                    msq[:, k:k + 1], mean[:, k:k + 1], -1.0, mean[:, k:k + 1],
                    ALU.mult, ALU.mult))
            if 3 <= t <= NTT + 2:
                k = t - 3
                ctx.op("act", lambda e, k=k: e.activation(
                    rstd[:, k:k + 1], var[:, k:k + 1], AF.Sqrt,
                    bias=eps[:, 0:1]))
            if 5 <= t <= NTT + 4:
                k = t - 5
                ctx.op("dve", lambda e, k=k: e.tensor_scalar(
                    ls(k), xs(k), mean[:, k:k + 1], rstd[:, k:k + 1],
                    ALU.subtract, ALU.mult))
            if 6 <= t <= NTT + 5:
                k = t - 6
                pr = psT4[(k % 2) * 2:(k % 2) * 2 + 2]
                for kk in range(8):
                    psx = pr[0] if kk < 4 else pr[1]
                    srcp = lambda kk=kk, psx=psx: psx[:, (kk % 4) * 128:
                                                      (kk % 4) * 128 + 128]
                    if xfm is not None:
                        ctx.op("dve", lambda e, kk=kk, k=k, srcp=srcp: e.tensor_copy(
                            xfm[:, kk * TPC + k * 128: kk * TPC + (k + 1) * 128],
                            srcp()))
                    if x32 is not None:
                        ctx.op("dve", lambda e, kk=kk, k=k, srcp=srcp: e.tensor_copy(
                            x32[:, (k % 2) * 1024 + kk * 128:
                                (k % 2) * 1024 + (kk + 1) * 128], srcp()))
                if lnbf_dram is not None:
                    ctx.op("pool", lambda e, k=k: e.tensor_copy(
                        ctx.sb["lnbf"][:, (k % 2) * 1024:(k % 2) * 1024 + 1024],
                        ls(k)))
                if write_scr is not None:
                    write_scr(k, ls(k))
            if lnbf_dram is not None and 7 <= t <= NTT + 6:
                k = t - 7
                ctx.dma("pool", "qp", lambda e, k=k: e.dma_start(
                    lnbf_dram[k * 128:(k + 1) * 128, :],
                    ctx.sb["lnbf"][:, (k % 2) * 1024:(k % 2) * 1024 + 1024]))
            if router_sb is not None and 8 <= t <= NTT + 7:
                k = t - 8
                psl = psL2[k % 2]
                if br_sb is not None:
                    ctx.op("dve", lambda e, k=k, psl=psl: e.tensor_scalar_add(
                        logits_fm[0:8, k * 128:(k + 1) * 128], psl[0:8, 0:128],
                        br_sb[0:8, 0:1]))
                else:
                    ctx.op("dve", lambda e, k=k, psl=psl: e.tensor_copy(
                        logits_fm[0:8, k * 128:(k + 1) * 128], psl[0:8, 0:128]))


def emit_router_l1(ctx, logits_fm, psT, scr_dram):
    ident = ctx.sb["ident"]
    ltm = ctx.sb["ltm"]; mxall = ctx.sb["mxall"]; miall = ctx.sb["miall"]
    idxf = ctx.sb["idxf"]; idx_row = ctx.sb["idx_row"]
    sb16 = ctx.sb["ltm"]
    with ctx.stage():
        for t in range(NT):
            ctx.op("pe", lambda e, t=t: e.transpose(
                psT[:, t * 8:(t + 1) * 8], logits_fm[0:8, t * 128:(t + 1) * 128],
                ident[0:8, 0:8]))
    with ctx.stage():
        ctx.op("dve", lambda e: e.tensor_copy(ltm[:, 0:128], psT[:, 0:128]))
    with ctx.stage():
        for t in range(NT):
            ctx.op("dve", lambda e, t=t: e.max(
                mxall[:, t * 8:(t + 1) * 8], ltm[:, t * 8:(t + 1) * 8]))
    with ctx.stage():
        for t in range(NT):
            ctx.op("dve", lambda e, t=t: e.max_index(
                miall[:, t * 8:(t + 1) * 8], mxall[:, t * 8:(t + 1) * 8],
                ltm[:, t * 8:(t + 1) * 8]))
    with ctx.stage():
        for t in range(NT):
            ctx.op("dve", lambda e, t=t: e.tensor_copy(
                idxf[:, t:t + 1], miall[:, t * 8:t * 8 + 1]))
    with ctx.stage():
        ctx.op("pe", lambda e: e.transpose(psT[0:16, 128:256], idxf[:, 0:16], ident[:]))
    with ctx.stage():
        ctx.op("dve", lambda e: e.tensor_copy(sb16[0:16, 0:128], psT[0:16, 128:256]))
    with ctx.stage():
        ctx.dma("sq", "qs", lambda e: e.dma_start(scr_dram[0:16, 0:128], sb16[0:16, 0:128]))
    with ctx.stage():
        ctx.dma("sq", "qs", lambda e: e.dma_start(
            idx_row[0:1, 0:2048].rearrange("a (t p) -> a t p", p=128),
            scr_dram[0:16, 0:128].rearrange("(o t) p -> o t p", o=1)))



def emit_softmax_probs(ctx, logits_fm, psT, probs_tm):
    ident = ctx.sb["ident"]
    ltm = ctx.sb["ltm"]
    rmax = ctx.sb["rmax"]; rsum = ctx.sb["rsum"]
    with ctx.stage():
        for t in range(NT):
            ctx.op("pe", lambda e, t=t: e.transpose(
                psT[:, t * 8:(t + 1) * 8], logits_fm[0:8, t * 128:(t + 1) * 128],
                ident[0:8, 0:8]))
    with ctx.stage():
        ctx.op("dve", lambda e: e.tensor_copy(ltm[:, 0:128], psT[:, 0:128]))
    with ctx.stage():
        ctx.op("dve", lambda e: e.tensor_reduce(
            rmax[:, 0:NT], ltm[:, 0:128].rearrange("p (t e) -> p t e", e=8),
            AX.X, ALU.max))
    with ctx.stage():
        ctx.op("dve", lambda e: e.tensor_scalar_mul(rmax[:, 0:NT], rmax[:, 0:NT], -1.0))
    with ctx.stage():
        for t in range(NT):
            ctx.op("act", lambda e, t=t: e.activation(
                probs_tm[:, t * 8:(t + 1) * 8], ltm[:, t * 8:(t + 1) * 8],
                AF.Exp, bias=rmax[:, t:t + 1]))
    with ctx.stage():
        ctx.op("dve", lambda e: e.tensor_reduce(
            rsum[:, 0:NT], probs_tm[:, 0:128].rearrange("p (t e) -> p t e", e=8),
            AX.X, ALU.add))
    with ctx.stage():
        ctx.op("dve", lambda e: e.reciprocal(rsum[:, 0:NT], rsum[:, 0:NT]))
    with ctx.stage():
        for t in range(NT):
            ctx.op("dve", lambda e, t=t: e.tensor_scalar_mul(
                probs_tm[:, t * 8:(t + 1) * 8], probs_tm[:, t * 8:(t + 1) * 8],
                rsum[:, t:t + 1]))


def emit_bisect_l0(ctx, pg, psB, k_val, n_iter=29):
    lo = ctx.sb["lo"]; hi = ctx.sb["hi"]; mid = ctx.sb["mid"]
    cmp_ = ctx.sb["cmp_scr"]; rs = ctx.sb["rs"]; cnt8 = ctx.sb["cnt8"]
    gsel = ctx.sb["gsel"]
    bd16 = ctx.sb["bd16"]; bd16T = ctx.sb["bd16T"]
    with ctx.stage():
        ctx.op("dve", lambda e: e.memset(lo[:], 0.0))
        ctx.op("dve", lambda e: e.memset(hi[:], 0.0))
    with ctx.stage():
        ctx.op("dve", lambda e: e.tensor_scalar_add(hi[:], hi[:], 1.0))
    for it in range(n_iter):
        with ctx.stage():
            ctx.op("dve", lambda e: e.tensor_add(mid[:], lo[:], hi[:]))
        with ctx.stage():
            ctx.op("dve", lambda e: e.tensor_scalar_mul(mid[:], mid[:], 0.5))
        with ctx.stage():
            ctx.op("dve", lambda e: e.tensor_single_scalar(
                cmp_[:], pg[:], mid[:, 0:1], ALU.is_ge))
        with ctx.stage():
            ctx.op("dve", lambda e: e.reduce_sum(rs[:], cmp_[:], axis=AX.X))
        with ctx.stage():
            ctx.op("pe", lambda e: e.matmul(psB[0:8, 0:1], bd16[:, 0:8], rs[:, 0:1],
                                            start=True, stop=True))
        with ctx.stage():
            ctx.op("dve", lambda e: e.tensor_copy(cnt8[0:8, 0:1], psB[0:8, 0:1]))
        with ctx.stage():
            ctx.op("pe", lambda e: e.matmul(psB[0:128, 4:5], bd16T[0:8, 0:128],
                                            cnt8[0:8, 0:1], start=True, stop=True))
        with ctx.stage():
            ctx.op("dve", lambda e: e.tensor_single_scalar(
                gsel[:], psB[0:128, 4:5], float(k_val), ALU.is_ge))
        with ctx.stage():
            ctx.op("dve", lambda e: e.select(lo[:], gsel[:], mid[:], lo[:]))
            ctx.op("dve", lambda e: e.tensor_copy(ctx.sb["hi_t"][:], hi[:]))
        with ctx.stage():
            ctx.op("dve", lambda e: e.select(hi[:], gsel[:], ctx.sb["hi_t"][:],
                                             mid[:]))



def _dram(nc, io, pref, name, shape, dt, kind):
    if io is not None:
        if name in io:
            return io[name]
        return nc.dram_tensor(pref + name, list(shape), dt).ap()
    if kind == "Internal":
        return nc.dram_tensor(name, list(shape), dt).ap()
    return nc.dram_tensor(name, list(shape), dt, kind=kind).ap()


def _ctx_finish(ctx):
    sems = dict(ctx.cnt)

    def body(eng):
        for s, v in sems.items():
            if v > ctx.W["sq"][s]:
                eng.wait_ge(ctx.sem[s], v)
                ctx.W["sq"][s] = v
    ctx.block.sync(body)
    ctx.st.close()


def _alloc_moe(ctx, layer, gathered=False):
    a = ctx.alloc
    a("x32rot", [128, 2048], F32)
    for n in ("smal", "ssal", "mean", "var", "rstd", "nbias"):
        a(n, [128, 16], F32)
    if not gathered:
        a("xfm", [128, 8 * TPC], BF16)
        a("hbuf", [128, 32 * 1024], BF16)
        a("w1buf", [128, 2048], BF16)
        a("w2buf", [128, 4096], BF16)
        a("yacc", [128, 8192], F32)
        a("ytmp", [128, 2048], F32)
        a("maskbc", [128, 1024], F32)
        a("selrow", [1, 1024], F32)
    else:
        a("lnbf", [128, 2048], BF16)
        a("identb", [128, 128], BF16)
        a("Lst_sb", [128, 128], F32)
    a("router_sb", [128, 64], F32)
    a("logits_fm", [8, 2048], F32)
    a("ltm", [128, 128], F32)
    a("ones1", [1, 512], F32)
    a("ident", [128, 128], F32)
    if layer == 1:
        a("mxall", [128, 128], F32)
        a("miall", [128, 128], U32)
        a("idxf", [128, 16], F32)
        a("idx_row", [1, 2048], F32)
        a("b1c_sb", [128, 256], F32)
        if gathered:
            a("b2c_sb", [128, 64], F32)
        else:
            a("b2e_sb", [1, 256], F32)
    else:
        a("rmax", [128, 16], F32)
        a("rsum", [128, 16], F32)
        a("probs_tm", [128, 128], F32)
        if gathered:
            a("selwh", [128, 128], F32)


def _psum8(ctx):
    return [ctx.alloc_psum("pp%d" % i, [128, 512], F32) for i in range(8)]


def build_moe1(ctx=None, io=None, pref="", gathered=True):
    standalone = ctx is None
    nc = bass.Bass("TRN2", target_bir_lowering=False) if standalone else ctx.nc
    dram = lambda name, shape, dt, kind="ExternalInput": _dram(nc, io, pref, name, shape, dt, kind)
    hin = dram("hin", [TPC, D], F32, "ExternalInput")
    hout = dram("hout", [TPC, D], F32, "ExternalOutput")
    wrT = dram("wrT", [1024, 8], F32, "ExternalInput")
    br = dram("br", [8, 1], F32, "ExternalInput")
    b1c = dram("b1c", [128, 256], F32, "ExternalInput")
    identD = dram("identD", [128, 128], F32, "ExternalInput")
    onesD = dram("onesD", [1, 512], F32, "ExternalInput")
    scr = dram("scr", [128, 128], F32, "Internal")
    if gathered:
        w1h = dram("w1h_l1", [NE, 32, 128, 8, 128], F32, "ExternalInput")
        w2h = dram("w2h_l1", [NE, 8, 128, 32, 128], F32, "ExternalInput")
        LstD = dram("LstD", [128, 128], F32, "ExternalInput")
        b2cD = dram("b2c_l1", [128, 64], F32, "ExternalInput")
        lnscr = dram("lnscr", [TPC, 1024], BF16, "Internal")
        gbuf = dram("gbuf", [8 * L1CAP, 1024], BF16, "Internal")
        ybuf = dram("ybuf", [8 * L1CAP, 1024], BF16, "Internal")
    else:
        w1 = dram("w1", [NE, 1024, 4096], F32, "ExternalInput")
        w2 = dram("w2", [NE, 4096, 1024], F32, "ExternalInput")
        b2r = dram("b2r", [1, 8192], F32, "ExternalInput")

    if standalone:
        ctx = Ctx(nc)
    ctx.push_phase(pref)
    _alloc_moe(ctx, 1, gathered)
    if gathered:
        ps = [ctx.alloc_psum("pp%d" % i, [128, 512], F32) for i in range(6)]
        ctx.psb = [ctx.alloc_psum("ppb%d" % i, [128, 1024], BF16) for i in range(2)]
        psA, psB, psL = ps[4], ps[5], ps[0]
    else:
        ps = _psum8(ctx)
        psA, psB, psL = ps[5], ps[6], ps[7]
    with ctx.stage():
        ctx.dma("sq", "qs", lambda e: e.dma_start(ctx.sb["ident"][:], identD[:]))
        ctx.dma("sq", "qs", lambda e: e.dma_start(ctx.sb["ones1"][:], onesD[:]))
        ctx.dma("sq", "qs", lambda e: e.dma_start(
            ctx.sb["router_sb"][:, 0:64].rearrange("p (k e) -> p k e", e=8),
            wrT.rearrange("(k p) e -> p k e", p=128)))
        ctx.dma("sq", "qs", lambda e: e.dma_start(ctx.sb["b1c_sb"][:], b1c[:]))
        if gathered:
            ctx.dma("sq", "qs", lambda e: e.dma_start(ctx.sb["Lst_sb"][:], LstD[:]))
            ctx.dma("sq", "qs", lambda e: e.dma_start(ctx.sb["b2c_sb"][:], b2cD[:]))
            ctx.dma("pool", "qp", lambda e: e.dma_start(ctx.sb["identb"][:],
                                                        identD[:]))
    br_sb = ctx.alloc("br_sb", [8, 1], F32)
    with ctx.stage():
        ctx.dma("sq", "qs", lambda e: e.dma_start(br_sb[0:8, 0:1], br[:, 0:1]))
    hin_tile = lambda t: hin[t * 128:(t + 1) * 128, :]
    if gathered:
        emit_ln_fused(ctx, hin_tile, ctx.sb["router_sb"],
                      [ps[2], ps[3], ps[4], ps[5]], [ps[0], ps[1]],
                      br_sb, ctx.sb["logits_fm"], lnbf_dram=lnscr)
    else:
        emit_ln_stats(ctx, hin_tile)
        emit_ln_apply_tp(ctx, hin_tile, ctx.sb["router_sb"], psA, psB, psL,
                         br_sb, ctx.sb["logits_fm"], lnbf_dram=None)
    emit_router_l1(ctx, ctx.sb["logits_fm"], psL, scr)
    if gathered:
        emit_moe_maps(ctx, 1, L1CAP, ps[5])
        emit_moe_gath(ctx, 1, hin_tile, hout, w1h, w2h, lnscr, gbuf, ybuf,
                      b1_sb=ctx.sb["b1c_sb"], b2c_sb=ctx.sb["b2c_sb"],
                      psum=ps, cap=L1CAP, resid_scale=0.5)
    else:
        def selrow_fn(ctx2, e_, half_):
            with ctx2.stage():
                ctx2.op("dve", lambda en, e_=e_, half_=half_: en.tensor_single_scalar(
                    ctx2.sb["selrow"][0:1, 0:1024],
                    ctx2.sb["idx_row"][0:1, half_ * 1024:(half_ + 1) * 1024],
                    float(e_), ALU.is_equal))
        emit_moe_dense(ctx, 1, hin_tile, hout, w1_all=w1, w2_all=w2,
                       b1_sb=ctx.sb["b1c_sb"], b2row_sb=b2r,
                       selrow_fn=selrow_fn, psum=ps, resid_scale=0.5)
    ctx.pop_phase()
    if standalone:
        _ctx_finish(ctx)
    return nc


def build_moe0a(ctx=None, io=None, pref=""):
    standalone = ctx is None
    nc = bass.Bass("TRN2", target_bir_lowering=False) if standalone else ctx.nc
    dram = lambda name, shape, dt, kind="ExternalInput": _dram(nc, io, pref, name, shape, dt, kind)
    hin = dram("hin", [TPC, D], F32, "ExternalInput")
    probs_out = dram("probs_out", [TPC, 8], F32, "ExternalOutput")
    gT = dram("gT", [1024, 8], F32, "ExternalInput")
    identD = dram("identD", [128, 128], F32, "ExternalInput")

    if standalone:
        ctx = Ctx(nc)
    ctx.push_phase(pref)
    a = ctx.alloc
    a("x32rot", [128, 2048], F32)
    for n in ("smal", "ssal", "mean", "var", "rstd", "nbias"):
        a(n, [128, 16], F32)
    a("router_sb", [128, 64], F32)
    a("logits_fm", [8, 2048], F32)
    a("ltm", [128, 128], F32)
    a("rmax", [128, 16], F32)
    a("rsum", [128, 16], F32)
    a("probs_tm", [128, 128], F32)
    a("ident", [128, 128], F32)
    ps = _psum8(ctx)
    with ctx.stage():
        ctx.dma("sq", "qs", lambda e: e.dma_start(ctx.sb["ident"][:], identD[:]))
        ctx.dma("sq", "qs", lambda e: e.dma_start(
            ctx.sb["router_sb"][:, 0:64].rearrange("p (k e) -> p k e", e=8),
            gT.rearrange("(k p) e -> p k e", p=128)))
    hin_tile = lambda t: hin[t * 128:(t + 1) * 128, :]
    lnscr_ap = io.get("lnscr") if io is not None else None
    if lnscr_ap is not None:
        a("lnbf", [128, 2048], BF16)
    emit_ln_fused(ctx, hin_tile, ctx.sb["router_sb"],
                  [ps[4], ps[5], ps[6], ps[7]], [ps[2], ps[3]],
                  None, ctx.sb["logits_fm"], lnbf_dram=lnscr_ap)
    emit_softmax_probs(ctx, ctx.sb["logits_fm"], ps[7], ctx.sb["probs_tm"])
    with ctx.stage():
        ctx.dma("sq", "qs", lambda e: e.dma_start(
            probs_out.rearrange("(t p) e -> p t e", p=128),
            ctx.sb["probs_tm"][:, 0:128].rearrange("p (t e) -> p t e", e=8)))
    ctx.pop_phase()
    if standalone:
        _ctx_finish(ctx)
    return nc


def build_moe0b(ctx=None, io=None, pref="", gathered=True):
    standalone = ctx is None
    nc = bass.Bass("TRN2", target_bir_lowering=False) if standalone else ctx.nc
    dram = lambda name, shape, dt, kind="ExternalInput": _dram(nc, io, pref, name, shape, dt, kind)
    hin = dram("hin", [TPC, D], F32, "ExternalInput")
    hout = dram("hout", [TPC, D], F32, "ExternalOutput")
    pglob = dram("pglob", [8, T], F32, "ExternalInput")
    ln_pre = bool(io.get("ln_pre")) if io is not None else False
    if gathered:
        w1h = dram("w1h_l0", [NE, 32, 128, 8, 128], F32, "ExternalInput")
        w2h = dram("w2h_l0", [NE, 8, 128, 32, 128], F32, "ExternalInput")
        LstD = dram("LstD", [128, 128], F32, "ExternalInput")
        lnscr = dram("lnscr", [TPC, 1024], BF16, "Internal")
        gbuf = dram("gbuf", [8 * 640, 1024], BF16, "Internal")
        ybuf = dram("ybuf", [8 * 640, 1024], BF16, "Internal")
    else:
        w1 = dram("w1", [NE, 1024, 4096], F32, "ExternalInput")
        w2 = dram("w2", [NE, 4096, 1024], F32, "ExternalInput")
    identD = dram("identD", [128, 128], F32, "ExternalInput")
    onesD = dram("onesD", [1, 512], F32, "ExternalInput")
    bdD = dram("bdD", [128, 8], F32, "ExternalInput")
    bdTD = dram("bdTD", [8, 128], F32, "ExternalInput")
    coff = dram("coff", [1, 1], F32, "ExternalInput")
    scr = dram("scr", [128, 128], F32, "Internal")
    if standalone:
        ctx = Ctx(nc)
    ctx.push_phase(pref)
    _alloc_moe(ctx, 0, gathered)
    a = ctx.alloc
    a("pg", [128, 512], F32)
    a("cmp_scr", [128, 512], F32)
    for n in ("lo", "hi", "mid", "rs", "hi_t"):
        a(n, [128, 1], F32)
    a("gsel", [128, 1], I32)
    a("cnt8", [8, 1], F32)
    a("bd16", [128, 8], F32)
    a("bd16T", [8, 128], F32)
    a("thr8", [1, 8], F32)
    a("thr_bc", [128, 8], F32)
    a("selw_tm", [128, 128], F32)
    a("sel128", [128, 128], F32)
    if gathered:
        ps = [ctx.alloc_psum("pp%d" % i, [128, 512], F32) for i in range(6)]
        ctx.psb = [ctx.alloc_psum("ppb%d" % i, [128, 1024], BF16) for i in range(2)]
    else:
        ps = _psum8(ctx)
    c = None
    pown_in = dram("pown", [TPC, 8], F32, "ExternalInput")
    with ctx.stage():
        ctx.dma("sq", "qs", lambda e: e.dma_start(ctx.sb["ident"][:], identD[:]))
        ctx.dma("sq", "qs", lambda e: e.dma_start(ctx.sb["ones1"][:], onesD[:]))
        ctx.dma("sq", "qs", lambda e: e.dma_start(ctx.sb["bd16"][:], bdD[:]))
        ctx.dma("sq", "qs", lambda e: e.dma_start(ctx.sb["bd16T"][:], bdTD[:]))
        ctx.dma("sq", "qs", lambda e: e.dma_start(
            ctx.sb["pg"][:], pglob.rearrange("e (c w) -> (e c) w", w=512)))
        ctx.dma("sq", "qs", lambda e: e.dma_start(
            ctx.sb["probs_tm"][:, 0:128].rearrange("p (t e) -> p t e", e=8),
            pown_in.rearrange("(t p) e -> p t e", p=128)))
    hin_tile = lambda t: hin[t * 128:(t + 1) * 128, :]
    if gathered:
        with ctx.stage():
            ctx.dma("sq", "qs", lambda e: e.dma_start(ctx.sb["Lst_sb"][:], LstD[:]))
            ctx.dma("pool", "qp", lambda e: e.dma_start(ctx.sb["identb"][:],
                                                        identD[:]))
    psX = ps[0] if gathered else ps[7]
    psA_, psB_ = (ps[4], ps[5]) if gathered else (ps[5], ps[6])
    if not ln_pre:
        psT4_ = [ps[2], ps[3], ps[4], ps[5]]
        emit_ln_fused(ctx, hin_tile, None, psT4_, None, None, None,
                      lnbf_dram=lnscr if gathered else None)
    emit_bisect_l0(ctx, ctx.sb["pg"], psX, 2048)
    with ctx.stage():
        for e_ in range(8):
            ctx.dma("sq", "qs", lambda e, e_=e_: e.dma_start(
                ctx.sb["thr8"][0:1, e_:e_ + 1], ctx.sb["lo"][e_ * 16:e_ * 16 + 1, 0:1]))
    with ctx.stage():
        ctx.op("pe", lambda e: e.matmul(psX[0:128, 0:8], ctx.sb["ones1"][0:1, 0:128],
                                        ctx.sb["thr8"][0:1, 0:8], start=True, stop=True))
    with ctx.stage():
        ctx.op("dve", lambda e: e.tensor_copy(ctx.sb["thr_bc"][:], psX[0:128, 0:8]))
    with ctx.stage():
        for t in range(NT):
            ctx.op("dve", lambda e, t=t: e.tensor_tensor(
                ctx.sb["sel128"][:, t * 8:(t + 1) * 8],
                ctx.sb["probs_tm"][:, t * 8:(t + 1) * 8], ctx.sb["thr_bc"][:],
                ALU.is_ge))
    with ctx.stage():
        ctx.op("dve", lambda e: e.tensor_mul(
            ctx.sb["selw_tm"][:, 0:128], ctx.sb["sel128"][:, 0:128],
            ctx.sb["probs_tm"][:, 0:128]))
    with ctx.stage():
        if gathered:
            ctx.op("dve", lambda e: e.tensor_scalar_mul(
                ctx.sb["selwh"][:, 0:128], ctx.sb["selw_tm"][:, 0:128], 0.5))
    if gathered:
        emit_moe_maps(ctx, 0, 640, ps[5])
        emit_moe_gath(ctx, 0, hin_tile, hout, w1h, w2h, lnscr, gbuf, ybuf,
                      psum=ps, cap=640, resid_scale=0.5)
    else:
        with ctx.stage():
            ctx.op("pe", lambda e: e.transpose(ps[7][0:128, 128:256],
                                               ctx.sb["selw_tm"][:, 0:128],
                                               ctx.sb["ident"][:]))
        with ctx.stage():
            ctx.op("dve", lambda e: e.tensor_copy(ctx.sb["sel128"][:, 0:128],
                                                  ps[7][0:128, 128:256]))
        with ctx.stage():
            ctx.dma("sq", "qs", lambda e: e.dma_start(scr[:, :],
                                                      ctx.sb["sel128"][:, 0:128]))
        def selrow_fn(ctx2, e_, half_):
            with ctx2.stage():
                ctx2.dma("sq", "qs", lambda en, e_=e_, half_=half_: en.dma_start(
                    ctx2.sb["selrow"][0:1, 0:1024].rearrange("a (t p) -> a t p", p=128),
                    scr.rearrange("(t e) p -> e t p", e=8)[e_:e_ + 1,
                        half_ * 8:(half_ + 1) * 8, :]))
        emit_moe_dense(ctx, 0, hin_tile, hout, w1_all=w1, w2_all=w2,
                       selrow_fn=selrow_fn, psum=ps, resid_scale=0.5)
    ctx.pop_phase()
    if standalone:
        _ctx_finish(ctx)
    return nc



def build_attn0a(ctx=None, io=None, pref=""):
    standalone = ctx is None
    nc = bass.Bass("TRN2", target_bir_lowering=False) if standalone else ctx.nc
    dram = lambda name, shape, dt, kind="ExternalInput": _dram(nc, io, pref, name, shape, dt, kind)
    hin = dram("hin", [TPC, D], F32, "ExternalInput")
    WqT = dram("WqT", [D, D], F32, "ExternalInput")
    WkT = dram("WkT", [D, D], F32, "ExternalInput")
    WvT = dram("WvT", [D, D], F32, "ExternalInput")
    bqc = dram("bqc", [128, 8], F32, "ExternalInput")
    bkc = dram("bkc", [128, 8], F32, "ExternalInput")
    bvr = dram("bvr", [1, D], F32, "ExternalInput")
    cosD = dram("cosD", [1024, TPC], F32, "ExternalInput")
    sinD = dram("sinD", [1024, TPC], F32, "ExternalInput")
    pswD = dram("pswD", [128, 128], F32, "ExternalInput")
    identD = dram("identD", [128, 128], F32, "ExternalInput")
    onesD = dram("onesD", [1, 512], F32, "ExternalInput")
    qr_out = dram("qr_out", [D, TPC], BF16, "ExternalOutput")
    q_out = dram("q_out", [D, TPC], BF16, "ExternalOutput")
    kv_out = dram("kv_out", [128, 1024], F32, "ExternalOutput")
    km_out = dram("km_out", [128, 8], F32, "ExternalOutput")
    ln_out = dram("ln_out", [TPC, D], F32, "ExternalOutput")

    if standalone:
        ctx = Ctx(nc)
    ctx.push_phase(pref)
    a = ctx.alloc
    for n in ("smal", "ssal", "mean", "var", "rstd", "nbias"):
        a(n, [128, 16], F32)
    a("xfm", [128, 8 * TPC], BF16)
    a("ident", [128, 128], F32)
    a("identb", [128, 128], BF16)
    a("ones1", [1, 512], F32)
    a("wbuf", [128, 2 * 1024], BF16)
    a("wv_sb", [128, 8 * 1024], BF16)
    a("vtm", [128, NT * 1024], BF16)
    a("csA", [128, 2 * 2048], F32)
    a("csB", [128, 2 * 2048], F32)
    a("psw", [128, 128], F32)
    a("qpre", [128, 2048], BF16)
    a("swp", [128, 2048], BF16)
    a("ta", [128, 2048], BF16)
    a("tb", [128, 2048], BF16)
    a("ro_bf", [128, 2048], BF16)
    a("pswb", [128, 128], BF16)
    a("krTc", [128, 2048], BF16)
    a("kvp", [128, 1024], F32)
    a("kmp", [128, 8], F32)
    a("bq_sb", [128, 8], F32)
    a("bk_sb", [128, 8], F32)
    a("bv_sb", [1, 1024], F32)
    ps = [ctx.alloc_psum("pp%d" % i, [128, 512], F32) for i in range(6)]
    pb = [ctx.alloc_psum("pb%d" % i, [128, 1024], BF16) for i in range(2)]

    with ctx.stage():
        ctx.dma("sq", "qs", lambda e: e.dma_start(ctx.sb["ident"][:], identD[:]))
        ctx.dma("sq", "qs", lambda e: e.dma_start(ctx.sb["ones1"][:], onesD[:]))
        ctx.dma("sq", "qs", lambda e: e.dma_start(ctx.sb["bq_sb"][:], bqc[:]))
        ctx.dma("sq", "qs", lambda e: e.dma_start(ctx.sb["bk_sb"][:], bkc[:]))
        ctx.dma("sq", "qs", lambda e: e.dma_start(ctx.sb["bv_sb"][:], bvr[:]))
        ctx.dma("pool", "qp", lambda e: e.dma_start(ctx.sb["identb"][:], identD[:]))
        ctx.dma("sq", "qs", lambda e: e.dma_start(ctx.sb["psw"][:], pswD[:]))
        ctx.dma("pool", "qp", lambda e: e.dma_start(ctx.sb["pswb"][:], pswD[:]))
        ctx.dma("pool", "qp", lambda e: e.dma_start(
            ctx.sb["wv_sb"][:, 0:8192].rearrange("p (k c) -> p k c", c=1024),
            WvT.rearrange("(k p) c -> p k c", p=128)))

    hin_tile = lambda t: hin[t * 128:(t + 1) * 128, :]
    def write_scr(t, ls):
        ctx.dma("pool", "qp", lambda e, t=t, ls=ls: e.dma_start(
            ln_out[t * 128:(t + 1) * 128, :], ls))
    emit_ln_fused(ctx, hin_tile, None, [ps[2], ps[3], ps[4], ps[5]], None,
                  None, None, write_scr=write_scr)
    xfm = ctx.sb["xfm"]
    vtm = ctx.sb["vtm"]

    for t in range(NT + 1):
        with ctx.stage():
            if t < NT:
                for half in range(2):
                    psx = ps[(t % 2) * 2 + half]
                    for k in range(8):
                        ctx.op("pe", lambda e, k=k, t=t, half=half, psx=psx:
                               e.matmul(
                            psx[:], xfm[:, k * TPC + t * 128:k * TPC + (t + 1) * 128],
                            ctx.sb["wv_sb"][:, k * 1024 + half * 512:
                                            k * 1024 + (half + 1) * 512],
                            start=(k == 0), stop=False))
                    ctx.op("pe", lambda e, half=half, psx=psx: e.matmul(
                        psx[:], ctx.sb["ones1"][0:1, 0:128],
                        ctx.sb["bv_sb"][0:1, half * 512:(half + 1) * 512],
                        start=False, stop=True, skip_group_check=True))
            if t >= 1:
                tx = t - 1
                for half in range(2):
                    psx = ps[(tx % 2) * 2 + half]
                    ctx.op("dve", lambda e, tx=tx, half=half, psx=psx:
                           e.tensor_copy(
                        vtm[:, tx * 1024 + half * 512:tx * 1024 + (half + 1) * 512],
                        psx[:]))

    def cs_load(gi):
        tau = gi % 8
        par = (gi % 2) * 2048
        ctx.dma("sq", "qs", lambda e, tau=tau, par=par: e.dma_start(
            ctx.sb["csA"][:, par:par + 2048], cosD[tau * 128:(tau + 1) * 128, :]))
        ctx.dma("sq", "qs", lambda e, tau=tau, par=par: e.dma_start(
            ctx.sb["csB"][:, par:par + 2048], sinD[tau * 128:(tau + 1) * 128, :]))

    def rope_chunk(gi, src_f32, dst_bf):
        par = (gi % 2) * 2048
        csA = ctx.sb["csA"][:, par:par + 2048]
        csB = ctx.sb["csB"][:, par:par + 2048]
        swp, ta, tb = ctx.sb["swp"], ctx.sb["ta"], ctx.sb["tb"]
        with ctx.stage():
            for g in range(4):
                ctx.op("pe", lambda e, g=g: e.matmul(
                    ps[g][:], ctx.sb["pswb"][:],
                    src_f32[:, g * 512:(g + 1) * 512], start=True, stop=True))
        with ctx.stage():
            for g in range(4):
                ctx.op("dve", lambda e, g=g: e.tensor_copy(
                    swp[:, g * 512:(g + 1) * 512], ps[g][:]))
        with ctx.stage():
            ctx.op("dve", lambda e: e.tensor_mul(ta[:], src_f32[:], csA))
            ctx.op("dve", lambda e: e.tensor_mul(tb[:], swp[:], csB))
        with ctx.stage():
            ctx.op("dve", lambda e: e.tensor_add(dst_bf[:], ta[:], tb[:]))

    for tensor_sel in ("q", "k"):
        wT = WqT if tensor_sel == "q" else WkT
        bias_sb = ctx.sb["bq_sb"] if tensor_sel == "q" else ctx.sb["bk_sb"]
        for tau in range(8):
            gi = (0 if tensor_sel == "q" else 8) + tau
            if gi == 0:
                with ctx.stage():
                    ctx.dma("pool", "qp", lambda e, wT=wT: e.dma_start(
                        ctx.sb["wbuf"][:, 0:1024].rearrange(
                            "p (k f) -> p k f", f=128),
                        wT[:, 0:128].rearrange("(k p) f -> p k f", p=128)))
                    cs_load(0)
            elif tau == 0:
                with ctx.stage():
                    ctx.dma("pool", "qp", lambda e, wT=wT: e.dma_start(
                        ctx.sb["wbuf"][:, 0:1024].rearrange(
                            "p (k f) -> p k f", f=128),
                        wT[:, 0:128].rearrange("(k p) f -> p k f", p=128)))
            with ctx.stage():
                if gi + 1 < 16:
                    cs_load(gi + 1)
                if tau + 1 < 8:
                    ctx.dma("pool", "qp", lambda e, tau=tau, wT=wT: e.dma_start(
                        ctx.sb["wbuf"][:, ((tau + 1) % 2) * 1024:
                                       ((tau + 1) % 2) * 1024 + 1024].rearrange(
                            "p (k f) -> p k f", f=128),
                        wT[:, (tau + 1) * 128:(tau + 2) * 128].rearrange(
                            "(k p) f -> p k f", p=128)))
                for g in range(4):
                    psx = ps[g]
                    for k in range(8):
                        ctx.op("pe", lambda e, k=k, g=g, psx=psx, tau=tau: e.matmul(
                            psx[:], ctx.sb["wbuf"][:, (tau % 2) * 1024 + k * 128:
                                                   (tau % 2) * 1024 + (k + 1) * 128],
                            xfm[:, k * TPC + g * 512:k * TPC + (g + 1) * 512],
                            start=(k == 0), stop=(k == 7)))
            with ctx.stage():
                for g in range(4):
                    eng = "act"
                    ctx.op(eng, lambda e, g=g, tau=tau, bias_sb=bias_sb: e.activation(
                        ctx.sb["qpre"][:, g * 512:(g + 1) * 512], ps[g][:],
                        AF.Gelu, bias=bias_sb[:, tau:tau + 1]))
            with ctx.stage():
                ctx.op("dve", lambda e: e.tensor_scalar_add(
                    ctx.sb["qpre"][:], ctx.sb["qpre"][:], 0.21))
            if tensor_sel == "k":
                with ctx.stage():
                    ctx.op("dve", lambda e, tau=tau: e.reduce_sum(
                        ctx.sb["kmp"][:, tau:tau + 1], ctx.sb["qpre"][:], axis=AX.X))
            rope_chunk(gi, ctx.sb["qpre"], ctx.sb["ro_bf"])
            if tensor_sel == "q":
                with ctx.stage():
                    ctx.dma("sq", "qs", lambda e, tau=tau: e.dma_start(
                        qr_out[tau * 128:(tau + 1) * 128, :], ctx.sb["ro_bf"][:]))
                    ctx.dma("sq", "qs", lambda e, tau=tau: e.dma_start(
                        q_out[tau * 128:(tau + 1) * 128, :], ctx.sb["qpre"][:]))
            else:
                with ctx.stage():
                    for t in range(NT):
                        pbx = pb[0] if t < 8 else pb[1]
                        ctx.op("pe", lambda e, t=t, pbx=pbx: e.transpose(
                            pbx[:, (t % 8) * 128:(t % 8) * 128 + 128],
                            ctx.sb["ro_bf"][:, t * 128:(t + 1) * 128],
                            ctx.sb["identb"][:]))
                with ctx.stage():
                    for t in range(NT):
                        pbx = pb[0] if t < 8 else pb[1]
                        eng = "dve" if t < 8 else "act"
                        ctx.op(eng, _cp(eng,
                            lambda t=t: ctx.sb["krTc"][:, t * 128:(t + 1) * 128],
                            lambda t=t, pbx=pbx: pbx[:, (t % 8) * 128:
                                                     (t % 8) * 128 + 128]))
                with ctx.stage():
                    for t in range(NT):
                        ctx.op("pe", lambda e, t=t, tau=tau: e.matmul(
                            ps[5][:, 0:128], ctx.sb["krTc"][:, t * 128:(t + 1) * 128],
                            vtm[:, t * 1024 + tau * 128: t * 1024 + (tau + 1) * 128],
                            start=(t == 0), stop=(t == NT - 1)))
                with ctx.stage():
                    ctx.op("dve", lambda e, tau=tau: e.tensor_copy(
                        ctx.sb["kvp"][:, tau * 128:(tau + 1) * 128], ps[5][:, 0:128]))
    with ctx.stage():
        ctx.dma("sq", "qs", lambda e: e.dma_start(kv_out[:], ctx.sb["kvp"][:]))
        ctx.dma("sq", "qs", lambda e: e.dma_start(km_out[:], ctx.sb["kmp"][:]))
    ctx.pop_phase()
    if standalone:
        _ctx_finish(ctx)
    return nc


def build_attn0b(ctx=None, io=None, pref="", reduced=False):
    standalone = ctx is None
    nc = bass.Bass("TRN2", target_bir_lowering=False) if standalone else ctx.nc
    dram = lambda name, shape, dt, kind="ExternalInput": _dram(nc, io, pref, name, shape, dt, kind)
    hin = dram("hin", [TPC, D], F32, "ExternalInput")
    lnin = dram("lnin", [TPC, D], F32, "ExternalInput")
    qr_in = dram("qr_in", [D, TPC], BF16, "ExternalInput")
    q_in = dram("q_in", [D, TPC], BF16, "ExternalInput")
    kvp4 = dram("kvp4", [4, 128, 1024], F32, "ExternalInput")
    kmp4 = dram("kmp4", [4, 128, 8], F32, "ExternalInput")
    kvmD = dram("kvmD", [128, 128], F32, "ExternalInput")
    hm2D = dram("hm2D", [128, 2], F32, "ExternalInput")
    gamD = dram("gamD", [1, D], F32, "ExternalInput")
    betD = dram("betD", [1, D], F32, "ExternalInput")
    identD = dram("identD", [128, 128], F32, "ExternalInput")
    onesD = dram("onesD", [1, 512], F32, "ExternalInput")
    hout = dram("hout", [TPC, D], F32, "ExternalOutput")

    if standalone:
        ctx = Ctx(nc)
    ctx.push_phase(pref)
    a = ctx.alloc
    a("ident", [128, 128], F32)
    a("ones1", [1, 512], F32)
    a("kva", [128, 1024], F32)
    a("kvb", [128, 1024], BF16)
    a("kvm", [128, 128], F32)
    a("km", [128, 8], F32)
    a("hm2", [128, 2], F32)
    a("bdall", [128, 8 * 16], BF16)
    a("bdf", [128, 2], F32)
    a("qchunk", [128, 2 * 2048], BF16)
    a("zden", [16, 2048], F32)
    a("zfm", [16, 2048], F32)
    a("ztm", [128, 16 * 16], F32)
    a("res", [128, 8 * 1024], F32)
    a("gbc", [128, 1024], F32)
    a("bbc", [128, 1024], F32)
    a("xres", [128, 2 * 1024], F32)
    a("tmp1", [128, 2 * 1024], F32)
    for n in ("smal", "ssal", "mean", "var", "rstd", "nbias"):
        a(n, [128, 16], F32)
    ps = [ctx.alloc_psum("pp%d" % i, [128, 512], F32) for i in range(8)]

    with ctx.stage():
        ctx.dma("sq", "qs", lambda e: e.dma_start(ctx.sb["ident"][:], identD[:]))
        ctx.dma("sq", "qs", lambda e: e.dma_start(ctx.sb["ones1"][:], onesD[:]))
        ctx.dma("sq", "qs", lambda e: e.dma_start(ctx.sb["kvm"][:], kvmD[:]))
        ctx.dma("sq", "qs", lambda e: e.dma_start(ctx.sb["hm2"][:], hm2D[:]))
        ctx.dma("sq", "qs", lambda e: e.dma_start(ctx.sb["kva"][:], kvp4[0]))
        ctx.dma("sq", "qs", lambda e: e.dma_start(ctx.sb["km"][:], kmp4[0]))
        if not reduced:
            ctx.dma("sq", "qs", lambda e: e.dma_start(ctx.sb["res"][:, 0:1024], kvp4[1]))
            ctx.dma("sq", "qs", lambda e: e.dma_start(ctx.sb["res"][:, 2048:2056], kmp4[1]))
    for i in ((1, 2, 3) if not reduced else ()):
        with ctx.stage():
            ctx.op("dve", lambda e: e.tensor_add(ctx.sb["kva"][:], ctx.sb["kva"][:],
                                                 ctx.sb["res"][:, 0:1024]))
            ctx.op("pool", lambda e: e.tensor_add(ctx.sb["km"][:], ctx.sb["km"][:],
                                                  ctx.sb["res"][:, 2048:2056]))
        if i < 3:
            with ctx.stage():
                ctx.dma("sq", "qs", lambda e, i=i: e.dma_start(
                    ctx.sb["res"][:, 0:1024], kvp4[i + 1]))
                ctx.dma("sq", "qs", lambda e, i=i: e.dma_start(
                    ctx.sb["res"][:, 2048:2056], kmp4[i + 1]))
    with ctx.stage():
        ctx.op("dve", lambda e: e.tensor_scalar_mul(ctx.sb["kva"][:],
                                                    ctx.sb["kva"][:], 1.0 / T))
        ctx.op("pool", lambda e: e.tensor_scalar_mul(ctx.sb["km"][:],
                                                     ctx.sb["km"][:], 1.0 / T))
    with ctx.stage():
        for tau in range(8):
            ctx.op("dve", lambda e, tau=tau: e.tensor_mul(
                ctx.sb["kvb"][:, tau * 128:(tau + 1) * 128],
                ctx.sb["kva"][:, tau * 128:(tau + 1) * 128], ctx.sb["kvm"][:]))
    with ctx.stage():
        ctx.op("dve", lambda e: e.memset(ctx.sb["bdall"][:], 0.0))
    with ctx.stage():
        for tau in range(8):
            ctx.op("dve", lambda e, tau=tau: e.tensor_scalar_mul(
                ctx.sb["bdall"][:, tau * 16 + 2 * tau: tau * 16 + 2 * tau + 2],
                ctx.sb["hm2"][:], ctx.sb["km"][:, tau:tau + 1]))
    with ctx.stage():
        ctx.dma("sq", "qs", lambda e: e.dma_start(
            ctx.sb["qchunk"][:, 0:2048], q_in[0:128, :]))
    for tau in range(8):
        with ctx.stage():
            if tau + 1 < 8:
                ctx.dma("sq", "qs", lambda e, tau=tau: e.dma_start(
                    ctx.sb["qchunk"][:, ((tau + 1) % 2) * 2048:
                                     ((tau + 1) % 2) * 2048 + 2048],
                    q_in[(tau + 1) * 128:(tau + 2) * 128, :]))
            for g in range(4):
                ctx.op("pe", lambda e, tau=tau, g=g: e.matmul(
                    ps[g][0:16, 0:512], ctx.sb["bdall"][:, tau * 16:(tau + 1) * 16],
                    ctx.sb["qchunk"][:, (tau % 2) * 2048 + g * 512:
                                     (tau % 2) * 2048 + (g + 1) * 512],
                    start=(tau == 0), stop=(tau == 7)))
    with ctx.stage():
        for g in range(4):
            ctx.op("dve", lambda e, g=g: e.tensor_scalar_add(
                ctx.sb["zden"][0:16, g * 512:(g + 1) * 512], ps[g][0:16, 0:512],
                1e-6))
    with ctx.stage():
        ctx.op("dve", lambda e: e.reciprocal(ctx.sb["zfm"][0:16, :],
                                             ctx.sb["zden"][0:16, :]))
    with ctx.stage():
        for t in range(NT):
            psx = ps[4] if t < 8 else ps[5]
            ctx.op("pe", lambda e, t=t, psx=psx: e.transpose(
                psx[:, (t % 8) * 16:(t % 8) * 16 + 16],
                ctx.sb["zfm"][0:16, t * 128:(t + 1) * 128], ctx.sb["ident"][0:16, 0:16]))
    with ctx.stage():
        ctx.op("dve", lambda e: e.tensor_copy(ctx.sb["ztm"][:, 0:128], ps[4][:, 0:128]))
        ctx.op("act", lambda e: e.copy(ctx.sb["ztm"][:, 128:256], ps[5][:, 0:128]))
    grow = a("grow", [1, 1024], F32)
    brow = a("brow", [1, 1024], F32)
    qsb = a("qsb", [128, 2 * 1024], BF16)
    with ctx.stage():
        ctx.dma("sq", "qs", lambda e: e.dma_start(grow[:], gamD[:]))
        ctx.dma("sq", "qs", lambda e: e.dma_start(brow[:], betD[:]))
    with ctx.stage():
        for g in range(2):
            ctx.op("pe", lambda e, g=g: e.matmul(
                ps[6][:, 0:512], ctx.sb["ones1"][0:1, 0:128],
                grow[0:1, g * 512:(g + 1) * 512], start=(g == 0), stop=True,
                skip_group_check=True) if False else e.matmul(
                (ps[6] if g == 0 else ps[7])[:, 0:512], ctx.sb["ones1"][0:1, 0:128],
                grow[0:1, g * 512:(g + 1) * 512], start=True, stop=True))
    with ctx.stage():
        ctx.op("dve", lambda e: e.tensor_copy(ctx.sb["gbc"][:, 0:512], ps[6][:, 0:512]))
        ctx.op("act", lambda e: e.copy(ctx.sb["gbc"][:, 512:1024], ps[7][:, 0:512]))
    with ctx.stage():
        for g in range(2):
            ctx.op("pe", lambda e, g=g: e.matmul(
                (ps[6] if g == 0 else ps[7])[:, 0:512], ctx.sb["ones1"][0:1, 0:128],
                brow[0:1, g * 512:(g + 1) * 512], start=True, stop=True))
    with ctx.stage():
        ctx.op("dve", lambda e: e.tensor_copy(ctx.sb["bbc"][:, 0:512], ps[6][:, 0:512]))
        ctx.op("act", lambda e: e.copy(ctx.sb["bbc"][:, 512:1024], ps[7][:, 0:512]))

    res = ctx.sb["res"]; ztm = ctx.sb["ztm"]
    sm = ctx.sb["smal"]; ss = ctx.sb["ssal"]
    mean = ctx.sb["mean"]; var = ctx.sb["var"]; rstd = ctx.sb["rstd"]
    tmp1 = ctx.sb["tmp1"]; xres = ctx.sb["xres"]
    for half in range(2):
        with ctx.stage():
            t0_ = half * 8
            ctx.dma("sq", "qs", lambda e, t0_=t0_: e.dma_start(
                res[:, 0:1024], lnin[t0_ * 128:(t0_ + 1) * 128, :]))
            ctx.dma("pool", "qp", lambda e, t0_=t0_: e.dma_start(
                qsb[:, 0:1024].rearrange("p (k c) -> p k c", c=128),
                qr_in[:, t0_ * 128:(t0_ + 1) * 128].rearrange(
                    "(k p) c -> p k c", p=128)))
        for j in range(10):
            with ctx.stage():
                if j + 1 < 8:
                    tn = half * 8 + j + 1
                    ctx.dma("sq", "qs", lambda e, tn=tn, j=j: e.dma_start(
                        res[:, (j + 1) * 1024:(j + 2) * 1024],
                        lnin[tn * 128:(tn + 1) * 128, :]))
                    ctx.dma("pool", "qp", lambda e, tn=tn, j=j: e.dma_start(
                        qsb[:, ((j + 1) % 2) * 1024:((j + 1) % 2) * 1024 + 1024]
                        .rearrange("p (k c) -> p k c", c=128),
                        qr_in[:, tn * 128:(tn + 1) * 128].rearrange(
                            "(k p) c -> p k c", p=128)))
                if j < 8:
                    pspair = (ps[0], ps[1]) if j % 2 == 0 else (ps[2], ps[3])
                    for tau in range(8):
                        psx = pspair[0] if tau < 4 else pspair[1]
                        ctx.op("pe", lambda e, tau=tau, psx=psx, j=j: e.matmul(
                            psx[:, (tau % 4) * 128:(tau % 4) * 128 + 128],
                            qsb[:, (j % 2) * 1024 + tau * 128:
                                (j % 2) * 1024 + (tau + 1) * 128],
                            ctx.sb["kvb"][:, tau * 128:(tau + 1) * 128],
                            start=True, stop=True))
                if 1 <= j <= 8:
                    jx = j - 1
                    t = half * 8 + jx
                    rsl = res[:, jx * 1024:(jx + 1) * 1024]
                    pspair = (ps[0], ps[1]) if jx % 2 == 0 else (ps[2], ps[3])
                    for tau in range(8):
                        psx = pspair[0] if tau < 4 else pspair[1]
                        for par in range(2):
                            zcol = ztm[:, t * 16 + 2 * tau + par:
                                       t * 16 + 2 * tau + par + 1]
                            ctx.op("dve", lambda e, tau=tau, par=par, psx=psx,
                                   zcol=zcol, rsl=rsl: e.scalar_tensor_tensor(
                                rsl[:, tau * 128 + par * 64:
                                    tau * 128 + (par + 1) * 64],
                                psx[:, (tau % 4) * 128 + par * 64:
                                    (tau % 4) * 128 + (par + 1) * 64],
                                zcol, rsl[:, tau * 128 + par * 64:
                                          tau * 128 + (par + 1) * 64],
                                ALU.mult, ALU.add))
                if 2 <= j:
                    jy = j - 2
                    rsly = res[:, jy * 1024:(jy + 1) * 1024]
                    ctx.op("act", lambda e, rsly=rsly, jy=jy: e.activation(
                        ctx.sb["tmp1"][:, 0:1024], rsly, AF.Square,
                        accum_out=ss[:, jy:jy + 1]))
                    ctx.op("dve", lambda e, rsly=rsly, jy=jy: e.reduce_sum(
                        sm[:, jy:jy + 1], rsly, axis=AX.X))
        with ctx.stage():
            ctx.op("dve", lambda e: e.tensor_scalar_mul(mean[:, 0:8], sm[:, 0:8],
                                                        1.0 / D))
        with ctx.stage():
            ctx.op("dve", lambda e: e.scalar_tensor_tensor(
                var[:, 0:8], mean[:, 0:8], -1.0, mean[:, 0:8], ALU.mult, ALU.mult))
        with ctx.stage():
            ctx.op("dve", lambda e: e.scalar_tensor_tensor(
                var[:, 0:8], ss[:, 0:8], 1.0 / D, var[:, 0:8], ALU.mult, ALU.add))
        with ctx.stage():
            ctx.op("dve", lambda e: e.tensor_scalar_add(var[:, 0:8], var[:, 0:8],
                                                        1e-12))
        with ctx.stage():
            ctx.op("act", lambda e: e.activation(rstd[:, 0:8], var[:, 0:8], AF.Sqrt))
        with ctx.stage():
            ctx.op("dve", lambda e: e.reciprocal(rstd[:, 0:8], rstd[:, 0:8]))
        with ctx.stage():
            t0_ = half * 8
            ctx.dma("sq", "qs", lambda e, t0_=t0_: e.dma_start(
                xres[:, 0:1024], hin[t0_ * 128:(t0_ + 1) * 128, :]))
        for j in range(9):
            with ctx.stage():
                if j < 8:
                    t = half * 8 + j
                    rsl = res[:, j * 1024:(j + 1) * 1024]
                    tslot = tmp1[:, (j % 2) * 1024:(j % 2) * 1024 + 1024]
                    xslot = xres[:, (j % 2) * 1024:(j % 2) * 1024 + 1024]
                    if j + 1 < 8:
                        tn = half * 8 + j + 1
                        ctx.dma("sq", "qs", lambda e, tn=tn, j=j: e.dma_start(
                            xres[:, ((j + 1) % 2) * 1024:((j + 1) % 2) * 1024 + 1024],
                            hin[tn * 128:(tn + 1) * 128, :]))
                    ctx.op("dve", lambda e, rsl=rsl, j=j, tslot=tslot:
                           e.tensor_scalar(
                        tslot, rsl, mean[:, j:j + 1], rstd[:, j:j + 1],
                        ALU.subtract, ALU.mult))
                    ctx.op("dve", lambda e, tslot=tslot: e.tensor_mul(
                        tslot, tslot, ctx.sb["gbc"][:]))
                    ctx.op("dve", lambda e, tslot=tslot: e.tensor_add(
                        tslot, tslot, ctx.sb["bbc"][:]))
                    ctx.op("dve", lambda e, tslot=tslot, xslot=xslot: e.tensor_add(
                        tslot, tslot, xslot))
                if j >= 1:
                    ty = half * 8 + j - 1
                    ctx.dma("sq", "qs", lambda e, ty=ty, j=j: e.dma_start(
                        hout[ty * 128:(ty + 1) * 128, :],
                        tmp1[:, ((j - 1) % 2) * 1024:((j - 1) % 2) * 1024 + 1024]))
    ctx.pop_phase()
    if standalone:
        _ctx_finish(ctx)
    return nc



def build_attn1a(ctx=None, io=None, pref=""):
    standalone = ctx is None
    nc = bass.Bass("TRN2", target_bir_lowering=False) if standalone else ctx.nc
    dram = lambda name, shape, dt, kind="ExternalInput": _dram(nc, io, pref, name, shape, dt, kind)
    hin = dram("hin", [TPC, D], F32, "ExternalInput")
    qkvT = dram("qkvT", [D, 3072], F32, "ExternalInput")
    zT = dram("zT", [D, D], F32, "ExternalInput")
    baT = dram("baT", [D, 16], F32, "ExternalInput")
    identD = dram("identD", [128, 128], F32, "ExternalInput")
    mixed_out = dram("mixed_out", [3072, TPC], BF16, "ExternalOutput")
    z_out = dram("z_out", [D, TPC], F32, "ExternalOutput")
    ba_out = dram("ba_out", [16, TPC], F32, "ExternalOutput")

    if standalone:
        ctx = Ctx(nc)
    ctx.push_phase(pref)
    a = ctx.alloc
    for n in ("smal", "ssal", "mean", "var", "rstd", "nbias"):
        a(n, [128, 16], F32)
    a("xfm", [128, 8 * TPC], BF16)
    a("ident", [128, 128], F32)
    a("wbuf", [128, 2 * 1024], BF16)
    a("obuf", [128, 2 * 2048], F32)
    a("obufb", [128, 2 * 2048], BF16)
    a("ba_sb", [128, 128], BF16)
    a("bafm", [16, 2048], F32)
    ps = [ctx.alloc_psum("pp%d" % i, [128, 512], F32) for i in range(8)]

    with ctx.stage():
        ctx.dma("sq", "qs", lambda e: e.dma_start(ctx.sb["ident"][:], identD[:]))
        ctx.dma("pool", "qp", lambda e: e.dma_start(
            ctx.sb["ba_sb"][:, 0:128].rearrange("p (k e) -> p k e", e=16),
            baT.rearrange("(k p) e -> p k e", p=128)))
    hin_tile = lambda t: hin[t * 128:(t + 1) * 128, :]
    emit_ln_fused(ctx, hin_tile, None, [ps[4], ps[5], ps[6], ps[7]], None,
                  None, None)
    xfm = ctx.sb["xfm"]

    cc_hook = (io or {}).get("cc_hook")
    seq = [("qkv", i) for i in range(24)] + [("ba", 0)] + \
          [("z", i) for i in range(8)]

    def w_load(j):
        kind, i = seq[j]
        if kind == "ba":
            return None
        wT_ap = qkvT if kind == "qkv" else zT
        rb = j % 2
        return lambda e, i=i, rb=rb, wT_ap=wT_ap: e.dma_start(
            ctx.sb["wbuf"][:, rb * 1024:(rb + 1) * 1024].rearrange(
                "p (k f) -> p k f", f=128),
            wT_ap[:, i * 128:(i + 1) * 128].rearrange("(k p) f -> p k f", p=128))

    def out_ap(j):
        kind, i = seq[j]
        if kind == "qkv":
            return mixed_out[i * 128:(i + 1) * 128, :]
        if kind == "z":
            return z_out[i * 128:(i + 1) * 128, :]
        return ba_out[:]

    with ctx.stage():
        ctx.dma("pool", "qp", w_load(0))
    NSEQ = len(seq)
    for j in range(NSEQ + 2):
        with ctx.stage():
            if j < NSEQ:
                kind, i = seq[j]
                quad = ps[0:4] if j % 2 == 0 else ps[4:8]
                rb = j % 2
                for g in range(4):
                    for k in range(8):
                        if kind == "ba":
                            ctx.op("pe", lambda e, g=g, k=k, quad=quad: e.matmul(
                                quad[g][0:16, 0:512],
                                ctx.sb["ba_sb"][:, k * 16:(k + 1) * 16],
                                xfm[:, k * TPC + g * 512:k * TPC + (g + 1) * 512],
                                start=(k == 0), stop=(k == 7)))
                        else:
                            ctx.op("pe", lambda e, g=g, k=k, rb=rb, quad=quad:
                                   e.matmul(
                                quad[g][:],
                                ctx.sb["wbuf"][:, rb * 1024 + k * 128:
                                               rb * 1024 + (k + 1) * 128],
                                xfm[:, k * TPC + g * 512:k * TPC + (g + 1) * 512],
                                start=(k == 0), stop=(k == 7)))
                if j + 1 < NSEQ:
                    wl = w_load(j + 1)
                    if wl is not None:
                        ctx.dma("pool", "qp", wl)
            if 1 <= j <= NSEQ:
                jx = j - 1
                kindx, _ = seq[jx]
                quadx = ps[0:4] if jx % 2 == 0 else ps[4:8]
                ob = ctx.sb["obuf"]
                for g in range(4):
                    eng = "dve" if g % 2 == 0 else "act"
                    if kindx == "ba":
                        ctx.op(eng, _cp(eng,
                            lambda g=g: ctx.sb["bafm"][0:16, g * 512:(g + 1) * 512],
                            lambda g=g, quadx=quadx: quadx[g][0:16, 0:512]))
                    else:
                        obx = ob if kindx == "z" else ctx.sb["obufb"]
                        ctx.op(eng, _cp(eng,
                            lambda g=g, jx=jx, obx=obx: obx[:, (jx % 2) * 2048 +
                                                            g * 512:(jx % 2) * 2048 +
                                                            (g + 1) * 512],
                            lambda g=g, quadx=quadx: quadx[g][:]))
            if 2 <= j:
                jy = j - 2
                kindy, _ = seq[jy]
                if kindy == "ba":
                    ctx.dma("sq", "qs", lambda e, jy=jy: e.dma_start(
                        out_ap(jy), ctx.sb["bafm"][0:16, :]))
                else:
                    oby = ctx.sb["obuf"] if kindy == "z" else ctx.sb["obufb"]
                    ctx.dma("sq", "qs", lambda e, jy=jy, oby=oby: e.dma_start(
                        out_ap(jy),
                        oby[:, (jy % 2) * 2048:(jy % 2) * 2048 + 2048]))
        if cc_hook is not None:
            done = j - 2
            if done in (7, 15, 23, 24):
                cc_hook({7: 0, 15: 1, 23: 2, 24: 3}[done])
    ctx.pop_phase()
    if standalone:
        _ctx_finish(ctx)
    return nc


def build_attn1c(ctx=None, io=None, pref="", gather=False):
    standalone = ctx is None
    nc = bass.Bass("TRN2", target_bir_lowering=False) if standalone else ctx.nc
    dram = lambda name, shape, dt, kind="ExternalInput": _dram(nc, io, pref, name, shape, dt, kind)
    hin = dram("hin", [TPC, D], F32, "ExternalInput")
    if gather:
        o_g = dram("o_g", [1024, T], BF16, "ExternalInput")
        o_flat = o_g.rearrange("(o a) b -> o (a b)", o=1)
        goD = dram("goD", [128, 8], I32, "ExternalInput")
    else:
        o_in = dram("o_in", [D, TPC], BF16, "ExternalInput")
    z_in = dram("z_in", [D, TPC], F32, "ExternalInput")
    outT = dram("outT", [D, D], F32, "ExternalInput")
    hout = dram("hout", [TPC, D], F32, "ExternalOutput")

    if standalone:
        ctx = Ctx(nc)
    ctx.push_phase(pref)
    a = ctx.alloc
    a("og", [128, 8 * TPC], BF16)
    a("wo", [128, 8 * 1024], BF16)
    a("zc", [128, 3 * 2048], F32)
    a("oc", [128, 3 * 2048], BF16)
    a("xres", [128, 3 * 1024], F32)
    a("outt", [128, 2 * 1024], F32)
    ps = [ctx.alloc_psum("pp%d" % i, [128, 512], F32) for i in range(4)]
    if gather:
        a("go_sb", [128, 8], I32)
    with ctx.stage():
        ctx.dma("pool", "qp", lambda e: e.dma_start(
            ctx.sb["wo"][:, 0:8192].rearrange("p (k c) -> p k c", c=1024),
            outT.rearrange("(k p) c -> p k c", p=128)))
        if gather:
            ctx.dma("sq", "qs", lambda e: e.dma_start(ctx.sb["go_sb"][:], goD[:]))
    zcb = ctx.sb["zc"]; ocb = ctx.sb["oc"]

    def _ocdma(vc):
        if gather:
            ho = ctx.bcreg(1024 * T - 1)
            return lambda e, vc=vc, h=ho: e.indirect_dma_start(
                ocb[:, (vc % 3) * 2048:(vc % 3) * 2048 + 2048], None, o_flat,
                IndirectOffsetOnAxis(ctx.sb["go_sb"][:, vc:vc + 1], 1),
                bounds_check=h["r"], oob_is_err=False)
        return lambda e, vc=vc: e.dma_start(
            ocb[:, (vc % 3) * 2048:(vc % 3) * 2048 + 2048],
            o_in[vc * 128:(vc + 1) * 128, :])

    with ctx.stage():
        ctx.dma("sq", "qs", lambda e: e.dma_start(zcb[:, 0:2048], z_in[0:128, :]))
        ctx.dma("pool", "qp", _ocdma(0))
    for vc in range(10):
        with ctx.stage():
            if vc + 1 < 8:
                ctx.dma("sq", "qs", lambda e, vc=vc: e.dma_start(
                    zcb[:, ((vc + 1) % 3) * 2048:((vc + 1) % 3) * 2048 + 2048],
                    z_in[(vc + 1) * 128:(vc + 2) * 128, :]))
                ctx.dma("pool", "qp", _ocdma(vc + 1))
            if vc < 8:
                ctx.op("act", lambda e, vc=vc: e.activation(
                    zcb[:, (vc % 3) * 2048:(vc % 3) * 2048 + 2048],
                    zcb[:, (vc % 3) * 2048:(vc % 3) * 2048 + 2048], AF.Silu))
            if 1 <= vc <= 8:
                vx = vc - 1
                ctx.op("dve", lambda e, vx=vx: e.tensor_mul(
                    ctx.sb["og"][:, vx * TPC:(vx + 1) * TPC],
                    ocb[:, (vx % 3) * 2048:(vx % 3) * 2048 + 2048],
                    zcb[:, (vx % 3) * 2048:(vx % 3) * 2048 + 2048]))
    with ctx.stage():
        ctx.dma("sq", "qs", lambda e: e.dma_start(ctx.sb["xres"][:, 0:1024],
                                                  hin[0:128, :]))
    for t in range(NT + 2):
        with ctx.stage():
            if t + 1 < NT:
                ctx.dma("sq", "qs", lambda e, t=t: e.dma_start(
                    ctx.sb["xres"][:, ((t + 1) % 3) * 1024:
                                   ((t + 1) % 3) * 1024 + 1024],
                    hin[(t + 1) * 128:(t + 2) * 128, :]))
            if t < NT:
                pp = (ps[0], ps[1]) if t % 2 == 0 else (ps[2], ps[3])
                for half in range(2):
                    for k in range(8):
                        ctx.op("pe", lambda e, t=t, half=half, k=k, pp=pp: e.matmul(
                            pp[half][:],
                            ctx.sb["og"][:, k * TPC + t * 128:
                                         k * TPC + (t + 1) * 128],
                            ctx.sb["wo"][:, k * 1024 + half * 512:
                                         k * 1024 + (half + 1) * 512],
                            start=(k == 0), stop=(k == 7)))
            if 1 <= t <= NT:
                tx = t - 1
                ppx = (ps[0], ps[1]) if tx % 2 == 0 else (ps[2], ps[3])
                for half in range(2):
                    ctx.op("dve", lambda e, half=half, tx=tx, ppx=ppx:
                           e.scalar_tensor_tensor(
                        ctx.sb["outt"][:, (tx % 2) * 1024 + half * 512:
                                       (tx % 2) * 1024 + (half + 1) * 512],
                        ppx[half][:], 1.0,
                        ctx.sb["xres"][:, (tx % 3) * 1024 + half * 512:
                                       (tx % 3) * 1024 + (half + 1) * 512],
                        ALU.bypass, ALU.add))
            if 2 <= t:
                ty = t - 2
                ctx.dma("sq", "qs", lambda e, ty=ty: e.dma_start(
                    hout[ty * 128:(ty + 1) * 128, :],
                    ctx.sb["outt"][:, (ty % 2) * 1024:(ty % 2) * 1024 + 1024]))
    ctx.pop_phase()
    if standalone:
        _ctx_finish(ctx)
    return nc


def _merge_streams(ctx, streams, offsets):
    L = max(o + len(s.stages) for s, o in zip(streams, offsets))
    for k in range(L):
        items = []
        for s, o in zip(streams, offsets):
            j = k - o
            if 0 <= j < len(s.stages):
                items.extend(s.stages[j])
        if not items:
            continue
        for eng, q, fn in items:
            if eng == "__mark__":
                fn()
        items = [it for it in items if it[0] != "__mark__"]
        if not items:
            continue
        with ctx.stage():
            for eng, q, fn in items:
                if q is None:
                    ctx.op(eng, fn)
                else:
                    ctx.dma(eng, q, fn)


class _Stream:
    def __init__(self):
        self.stages = [[]]

    def op(self, eng, fn):
        self.stages[-1].append((eng, None, fn))

    def dma(self, eng, q, fn):
        self.stages[-1].append((eng, q, fn))

    def mark(self, fn):
        self.stages[-1].append(("__mark__", None, fn))

    def nxt(self):
        self.stages.append([])


def build_attn1b(ctx=None, io=None, pref="", gather=False):
    standalone = ctx is None
    nc = bass.Bass("TRN2", target_bir_lowering=False) if standalone else ctx.nc
    dram = lambda name, shape, dt, kind="ExternalInput": _dram(nc, io, pref, name, shape, dt, kind)
    if gather:
        mix_gq = dram("mix_gq", [4096, TPC], BF16, "ExternalInput")
        mix_gk = dram("mix_gk", [4096, TPC], BF16, "ExternalInput")
        mix_gv = dram("mix_gv", [4096, TPC], BF16, "ExternalInput")
        mix_gba = dram("mix_gba", [64, TPC], F32, "ExternalInput")
        mflat = [t.rearrange("(o a) b -> o (a b)", o=1)
                 for t in (mix_gq, mix_gk, mix_gv)]
        mba_flat = mix_gba.rearrange("(o a) b -> o (a b)", o=1)
        gqD = dram("gqD", [128, 48], I32, "ExternalInput")
        gbaD = dram("gbaD", [64, 4], I32, "ExternalInput")
    else:
        qkv_in = dram("qkv_in", [2, 3, 128, T], F32, "ExternalInput")
        ba_in = dram("ba_in", [2, 2, T], F32, "ExternalInput")
    cw_in = dram("cw_in", [2, 3, 128, 4], F32, "ExternalInput")
    dtb_in = dram("dtb_in", [2, 1], F32, "ExternalInput")
    alog_in = dram("alog_in", [2, 1], F32, "ExternalInput")
    nwD = dram("nwD", [1, 128], F32, "ExternalInput")
    identD = dram("identD", [128, 128], F32, "ExternalInput")
    onesD = dram("onesD", [1, 512], F32, "ExternalInput")
    mLsD = dram("mLsD", [128, 128], F32, "ExternalInput")
    mUiD = dram("mUiD", [128, 128], F32, "ExternalInput")
    triuD = dram("triuD", [128, 128], F32, "ExternalInput")
    o_out = dram("o_out", [2, 128, T], BF16, "ExternalOutput")
    scrg = [dram("scrg%d" % p, [64, 128], F32, "Internal") for p in range(2)]
    oscr = [dram("oscr%d" % p, [128, T], BF16, "Internal") for p in range(2)]

    if standalone:
        ctx = Ctx(nc)
    ctx.push_phase(pref)
    a = ctx.alloc
    a("ident", [128, 128], F32)
    a("identb", [128, 128], BF16)
    a("ones1", [1, 512], F32)
    a("mLs", [128, 128], F32)
    a("mUi", [128, 128], F32)
    a("triu", [128, 128], F32)
    a("normbc", [128, 128], F32)
    nwrow = a("nwrow", [1, 128], F32)
    a("growB", [33, T], F32)
    a("ones33", [33, 128], F32)
    for p in range(2):
        s = str(p)
        a("qbf" + s, [128, T], BF16)
        a("kbf" + s, [128, T], BF16)
        a("vbf" + s, [128, T], BF16)
        a("cvs" + s, [128, 1027], BF16)
        a("cva" + s, [128, 1024], F32)
        a("scal" + s, [1, 8], F32)
        a("cwcol" + s, [128, 4], F32)
        a("cols" + s, [128, 8], F32)
        a("ba_tm" + s, [128, 128], F32)
        a("beta" + s, [128, 64], F32)
        a("nbeta" + s, [128, 64], F32)
        a("gcum" + s, [128, 64], F32)
        a("scr1" + s, [128, 128], F32)
        a("eAll" + s, [128, 384], F32)
        a("NN0" + s, [128, 256], BF16)
        a("NN1" + s, [128, 256], BF16)
        a("MT" + s, [128, 128], BF16)
        a("gQ" + s, [128, 128], BF16)
        a("Tst" + s, [128, 256], BF16)
        a("Kbrev" + s, [128, 128], BF16)
        a("identg" + s, [128, 128], BF16)
        a("MUW" + s, [128, 256], BF16)
        a("GT" + s, [128, 128], BF16)
        a("KTUW" + s, [128, 256], BF16)
        a("NWTK" + s, [128, 128], BF16)
        a("Sbf" + s, [128, 128], BF16)
        a("Sf32" + s, [128, 128], F32)
        a("Of32" + s, [128, 128], F32)
        a("Obf8" + s, [128, 1024], BF16)
        a("osum" + s, [128, 64], F32)
    sb = ctx.sb
    pA = [ctx.alloc_psum("pA%d" % p, [128, 512], F32) for p in range(2)]
    pB = [ctx.alloc_psum("pB%d" % p, [128, 512], F32) for p in range(2)]
    pC = [ctx.alloc_psum("pC%d" % p, [128, 512], F32) for p in range(2)]
    pb = [ctx.alloc_psum("pbb%d" % p, [128, 1024], BF16) for p in range(2)]

    NEL = 4096 * TPC
    NELBA = 64 * TPC
    with ctx.stage():
        ctx.dma("sq", "qs", lambda e: e.dma_start(sb["ident"][:], identD[:]))
        ctx.dma("sq", "qs", lambda e: e.dma_start(sb["mLs"][:], mLsD[:]))
        ctx.dma("sq", "qs", lambda e: e.dma_start(sb["mUi"][:], mUiD[:]))
        ctx.dma("sq", "qs", lambda e: e.dma_start(sb["triu"][:], triuD[:]))
        ctx.dma("sq", "qs", lambda e: e.dma_start(sb["ones1"][:], onesD[:]))
        ctx.dma("sq", "qs", lambda e: e.dma_start(nwrow[:], nwD[:]))
        ctx.dma("sq", "qs", lambda e: e.dma_start(sb["ones33"][0:1, 0:128],
                                                  onesD[0:1, 0:128]))
        ctx.dma("sq", "qs", lambda e: e.dma_start(sb["ones33"][32:33, 0:128],
                                                  onesD[0:1, 0:128]))
        ctx.dma("pool", "qp", lambda e: e.dma_start(sb["identb"][:], identD[:]))
        if gather:
            a("gq_sb", [128, 48], I32)
            a("gba_sb", [64, 4], I32)
            ctx.dma("sq", "qs", lambda e: e.dma_start(sb["gq_sb"][:], gqD[:]))
            ctx.dma("sq", "qs", lambda e: e.dma_start(sb["gba_sb"][:], gbaD[:]))
            hfull = ctx.bcreg(NEL - 1)
            hba = ctx.bcreg(NELBA - 1)
        else:
            hfull = hba = None
    with ctx.stage():
        ctx.op("pe", lambda e: e.matmul(pA[0][:, 0:128], sb["ones1"][0:1, 0:128],
                                        nwrow[0:1, 0:128], start=True, stop=True))
    with ctx.stage():
        ctx.op("dve", lambda e: e.tensor_copy(sb["normbc"][:], pA[0][:, 0:128]))

    NCHUNK = T // CCH

    def pair_stream(p):
        S = _Stream()
        s = str(p)
        prow = p * 32
        bA, bB, bC, pbP = pA[p], pB[p], pC[p], pb[p]
        qbf, kbf, vbf = sb["qbf" + s], sb["kbf" + s], sb["vbf" + s]
        cvs, cva = sb["cvs" + s], sb["cva" + s]
        scal, cwcol = sb["scal" + s], sb["cwcol" + s]
        cols = sb["cols" + s]
        colA, colB = cols[:, 0:1], cols[:, 1:2]
        colg, ecolg = cols[:, 2:3], cols[:, 3:4]
        ecol2, bcg = cols[:, 4:5], cols[:, 5:6]
        ba_tm, beta, nbeta = sb["ba_tm" + s], sb["beta" + s], sb["nbeta" + s]
        gcum, scr1, eAll = sb["gcum" + s], sb["scr1" + s], sb["eAll" + s]
        NN = [sb["NN0" + s], sb["NN1" + s]]
        MT, gQ, Tst = sb["MT" + s], sb["gQ" + s], sb["Tst" + s]
        Kbrev, identg = sb["Kbrev" + s], sb["identg" + s]
        MUW, GT, KTUW = sb["MUW" + s], sb["GT" + s], sb["KTUW" + s]
        NWTK, Sbf, Sf32 = sb["NWTK" + s], sb["Sbf" + s], sb["Sf32" + s]
        Of32, Obf8, osum = sb["Of32" + s], sb["Obf8" + s], sb["osum" + s]
        growB = sb["growB"]

        cc_base = (io or {}).get("cc_base")
        for blk, dst in ((0, qbf), (1, kbf), (2, vbf)):
            if cc_base is not None:
                S.mark(lambda blk=blk: ctx.mask_at("cc", cc_base + blk + 1))
            S.dma("pool", "qp", lambda e, p=p, blk=blk: e.dma_start(
                cwcol[:], cw_in[p, blk]))
            S.nxt()
            for st in range(8):
                if st == 0:
                    S.op("dve", lambda e: e.memset(cvs[:, 0:3], 0.0))
                else:
                    S.op("dve", lambda e: e.tensor_copy(cvs[:, 0:3],
                                                        cvs[:, 1024:1027]))
                S.nxt()
                if gather:
                    col = p * 24 + blk * 8 + st
                    S.dma("pool", "qp", lambda e, col=col, blk=blk:
                          e.indirect_dma_start(
                              cvs[:, 3:1027], None, mflat[blk],
                              IndirectOffsetOnAxis(sb["gq_sb"][:, col:col + 1], 1),
                              bounds_check=hfull["r"], oob_is_err=False))
                else:
                    S.dma("pool", "qp", lambda e, p=p, blk=blk, st=st: e.dma_start(
                        cvs[:, 3:1027], qkv_in[p, blk][:, st * 1024:(st + 1) * 1024]))
                S.nxt()
                S.op("dve", lambda e: e.tensor_single_scalar(
                    cva[:], cvs[:, 3:1027], cwcol[:, 3:4], ALU.mult))
                for tau in range(3):
                    sh = 3 - tau
                    S.op("dve", lambda e, tau=tau, sh=sh: e.scalar_tensor_tensor(
                        cva[:], cvs[:, 3 - sh:1027 - sh], cwcol[:, tau:tau + 1],
                        cva[:], ALU.mult, ALU.add))
                S.nxt()
                S.op("dve", lambda e, dst=dst, st=st: e.tensor_copy(
                    dst[:, st * 1024:(st + 1) * 1024], cva[:]))
                S.nxt()
        if cc_base is not None:
            S.mark(lambda: ctx.mask_at("cc", cc_base + 4))
        S.dma("sq", "qs", lambda e, p=p: e.dma_start(scal[0:1, 0:1],
                                                     dtb_in[p:p + 1, 0:1]))
        S.dma("sq", "qs", lambda e, p=p: e.dma_start(scal[0:1, 1:2],
                                                     alog_in[p:p + 1, 0:1]))
        S.nxt()
        S.op("act", lambda e: e.activation(scal[0:1, 2:3], scal[0:1, 1:2], AF.Exp))
        S.nxt()
        S.op("dve", lambda e: e.tensor_scalar_mul(scal[0:1, 3:4], scal[0:1, 2:3],
                                                  -1.0))
        S.nxt()
        S.op("pe", lambda e: e.matmul(bA[:, 0:1], sb["ones1"][0:1, 0:128],
                                      scal[0:1, 0:1], start=True, stop=True))
        S.op("pe", lambda e: e.matmul(bA[:, 2:3], sb["ones1"][0:1, 0:128],
                                      scal[0:1, 3:4], start=True, stop=True))
        S.nxt()
        S.op("dve", lambda e: e.tensor_copy(colA, bA[:, 0:1]))
        S.op("dve", lambda e: e.tensor_copy(colB, bA[:, 2:3]))
        S.nxt()
        for row, dst0 in ((0, 0), (1, 64)):
            if gather:
                col = p * 2 + row
                S.dma("pool", "qp", lambda e, col=col: e.indirect_dma_start(
                    scr1[0:64, 0:128], None, mba_flat,
                    IndirectOffsetOnAxis(sb["gba_sb"][:, col:col + 1], 1),
                    bounds_check=hba["r"], oob_is_err=False))
            else:
                S.dma("sq", "qs", lambda e, p=p, row=row: e.dma_start(
                    scr1[0:64, 0:128],
                    ba_in[p, row].rearrange("(c q) -> c q", q=128)))
            S.nxt()
            S.op("pe", lambda e: e.transpose(bA[:, 0:64], scr1[0:64, 0:128],
                                             sb["ident"][0:64, 0:64]))
            S.nxt()
            S.op("dve", lambda e, dst0=dst0: e.tensor_copy(
                ba_tm[:, dst0:dst0 + 64], bA[:, 0:64]))
            S.nxt()
        S.op("act", lambda e: e.activation(beta[:], ba_tm[:, 0:64], AF.Sigmoid))
        S.nxt()
        S.op("dve", lambda e: e.tensor_scalar_mul(nbeta[:], beta[:], -1.0))
        S.op("dve", lambda e: e.tensor_single_scalar(
            gcum[:], ba_tm[:, 64:128], colA, ALU.add))
        S.nxt()
        S.op("dve", lambda e: e.tensor_scalar(scr1[:, 0:64], gcum[:], 0.0, None,
                                              ALU.max))
        S.op("dve", lambda e: e.tensor_scalar_mul(scr1[:, 64:128], gcum[:], -1.0))
        S.nxt()
        S.op("dve", lambda e: e.tensor_tensor(scr1[:, 64:128], gcum[:],
                                              scr1[:, 64:128], ALU.max))
        S.nxt()
        S.op("dve", lambda e: e.tensor_scalar_mul(scr1[:, 64:128],
                                                  scr1[:, 64:128], -1.0))
        S.nxt()
        S.op("act", lambda e: e.activation(scr1[:, 64:128], scr1[:, 64:128],
                                           AF.Exp))
        S.nxt()
        S.op("dve", lambda e: e.tensor_scalar_add(scr1[:, 64:128],
                                                  scr1[:, 64:128], 1.0))
        S.nxt()
        S.op("act", lambda e: e.activation(scr1[:, 64:128], scr1[:, 64:128],
                                           AF.Ln))
        S.nxt()
        S.op("dve", lambda e: e.tensor_add(scr1[:, 0:64], scr1[:, 0:64],
                                           scr1[:, 64:128]))
        S.nxt()
        S.op("dve", lambda e: e.tensor_single_scalar(
            scr1[:, 64:128], scr1[:, 0:64], colB, ALU.mult))
        S.nxt()
        S.op("pe", lambda e: e.matmul(bA[:, 0:64], sb["triu"][:],
                                      scr1[:, 64:128], start=True, stop=True))
        S.nxt()
        S.op("dve", lambda e: e.tensor_copy(gcum[:], bA[:, 0:64]))
        S.nxt()
        S.op("pe", lambda e: e.transpose(bA[0:64, 64:192], gcum[:, 0:64],
                                         sb["ident"][:]))
        S.nxt()
        S.op("dve", lambda e: e.tensor_copy(scr1[0:64, 0:128], bA[0:64, 64:192]))
        S.nxt()
        S.dma("sq", "qs", lambda e, p=p: e.dma_start(scrg[p][:, :],
                                                     scr1[0:64, 0:128]))
        S.nxt()
        S.dma("pool", "qp", lambda e, p=p, prow=prow: e.dma_start(
            growB[prow:prow + 1, 0:T].rearrange("a (c q) -> a c q", q=128),
            scrg[p].rearrange("(o c) q -> o c q", o=1)))
        S.nxt()
        S.op("dve", lambda e: e.memset(Sbf[:], 0.0))
        S.nxt()

        for c in range(NCHUNK):
            Qc = qbf[:, c * 128:(c + 1) * 128]
            Kc = kbf[:, c * 128:(c + 1) * 128]
            Vc = vbf[:, c * 128:(c + 1) * 128]
            gcc = gcum[:, c:c + 1]
            bc = beta[:, c:c + 1]
            nbc = nbeta[:, c:c + 1]
            S.op("pe", lambda e, c=c, prow=prow: e.matmul(
                bA[:, 0:128], sb["ones33"][prow:prow + 1, 0:128],
                growB[prow:prow + 1, c * 128:(c + 1) * 128], start=True, stop=True))
            S.op("pe", lambda e, Kc=Kc: e.matmul(bA[:, 128:256], Kc, Kc,
                                                 start=True, stop=True))
            S.op("pe", lambda e, Kc=Kc, Qc=Qc: e.matmul(bA[:, 256:384], Kc, Qc,
                                                        start=True, stop=True))
            S.op("pe", lambda e, Vc=Vc: e.transpose(pbP[:, 0:128], Vc,
                                                    sb["identb"][:]))
            S.op("pe", lambda e, Kc=Kc: e.transpose(pbP[:, 128:256], Kc,
                                                    sb["identb"][:]))
            S.nxt()
            S.op("dve", lambda e: e.tensor_copy(eAll[:, 256:384], bA[:, 0:128]))
            S.op("dve", lambda e, gcc=gcc: e.tensor_single_scalar(
                scr1[:], sb["mLs"][:], gcc, ALU.add))
            S.op("dve", lambda e, gcc=gcc: e.scalar_tensor_tensor(
                eAll[:, 128:256], bA[:, 0:128], gcc, sb["mUi"][:],
                ALU.subtract, ALU.add))
            S.op("dve", lambda e, bc=bc: e.tensor_single_scalar(
                Tst[:, 0:128], pbP[:, 0:128], bc, ALU.mult))
            S.op("dve", lambda e: e.tensor_tensor(eAll[:, 0:128], scr1[:],
                                                  bA[:, 0:128], ALU.subtract))
            S.op("dve", lambda e, gcc=gcc: e.tensor_single_scalar(
                colg, bA[:, 127:128], gcc, ALU.subtract))
            S.nxt()
            S.op("act", lambda e: e.activation(eAll[:, 0:384], eAll[:, 0:384],
                                               AF.Exp))
            S.op("act", lambda e: e.activation(ecolg, colg, AF.Exp))
            S.op("act", lambda e, gcc=gcc: e.activation(ecol2, gcc, AF.Exp))
            S.nxt()
            S.op("dve", lambda e, nbc=nbc: e.scalar_tensor_tensor(
                NN[0][:, 0:128], bA[:, 128:256], nbc, eAll[:, 0:128],
                ALU.mult, ALU.mult))
            S.op("dve", lambda e: e.tensor_mul(MT[:], eAll[:, 128:256],
                                               bA[:, 256:384]))
            S.op("dve", lambda e, Qc=Qc: e.tensor_mul(gQ[:], Qc,
                                                      eAll[:, 256:384]))
            S.op("dve", lambda e, bc=bc: e.tensor_mul(bcg, ecol2, bc))
            S.op("dve", lambda e: e.tensor_single_scalar(
                Kbrev[:], pbP[:, 128:256], ecolg, ALU.mult))
            S.op("dve", lambda e: e.tensor_single_scalar(
                identg[:], sb["identb"][:], eAll[:, 383:384], ALU.mult))
            S.nxt()
            S.op("pe", lambda e: e.transpose(pbP[:, 256:384], NN[0][:, 0:128],
                                             sb["identb"][:]))
            S.nxt()
            S.op("dve", lambda e: e.tensor_copy(NN[0][:, 128:256],
                                                pbP[:, 256:384]))
            S.op("dve", lambda e: e.tensor_single_scalar(
                Tst[:, 128:256], pbP[:, 128:256], bcg, ALU.mult))
            S.nxt()
            cur, nxt = 0, 1
            for st in range(7):
                Ncur = NN[cur][:, 0:128]
                NTcur = NN[cur][:, 128:256]
                S.op("pe", lambda e, NTcur=NTcur: e.matmul(
                    bB[:, 0:256], NTcur, Tst[:], start=True, stop=True))
                if st < 6:
                    S.op("pe", lambda e, NTcur=NTcur, Ncur=Ncur: e.matmul(
                        bB[:, 256:384], NTcur, Ncur, start=True, stop=True))
                    S.op("pe", lambda e, NTcur=NTcur, Ncur=Ncur: e.matmul(
                        bB[:, 384:512], Ncur, NTcur, start=True, stop=True))
                S.nxt()
                S.op("dve", lambda e: e.scalar_tensor_tensor(
                    Tst[:], bB[:, 0:256], 1.0, Tst[:], ALU.bypass, ALU.add))
                if st < 6:
                    S.op("dve", lambda e, nxt=nxt: e.tensor_copy(
                        NN[nxt][:], bB[:, 256:512]))
                    cur, nxt = nxt, cur
                S.nxt()
            S.op("pe", lambda e: e.matmul(bC[:, 0:256], MT[:], Tst[:],
                                          start=True, stop=True))
            S.nxt()
            S.op("dve", lambda e: e.tensor_copy(MUW[:], bC[:, 0:256]))
            S.nxt()
            S.op("pe", lambda e: e.transpose(pbP[:, 384:512], MUW[:, 128:256],
                                             sb["identb"][:]))
            S.op("pe", lambda e: e.matmul(bC[:, 256:512], Kbrev[:], Tst[:],
                                          start=True, stop=True))
            S.nxt()
            S.op("dve", lambda e: e.scalar_tensor_tensor(
                GT[:], pbP[:, 384:512], -1.0, gQ[:], ALU.mult, ALU.add))
            S.op("dve", lambda e: e.tensor_copy(KTUW[:], bC[:, 256:512]))
            S.nxt()
            S.op("pe", lambda e: e.transpose(pbP[:, 512:640], KTUW[:, 128:256],
                                             sb["identb"][:]))
            S.nxt()
            S.op("dve", lambda e: e.tensor_scalar_mul(NWTK[:], pbP[:, 512:640],
                                                      -1.0))
            S.nxt()
            S.op("pe", lambda e: e.matmul(bC[:, 0:128], GT[:], Sbf[:],
                                          start=True, stop=True))
            S.op("pe", lambda e: e.matmul(bC[:, 128:256], NWTK[:], Sbf[:],
                                          start=True, stop=False))
            S.op("pe", lambda e: e.matmul(bC[:, 128:256], identg[:], Sbf[:],
                                          start=False, stop=True,
                                          skip_group_check=True))
            S.nxt()
            S.op("dve", lambda e: e.scalar_tensor_tensor(
                Of32[:], bC[:, 0:128], 1.0, MUW[:, 0:128], ALU.bypass, ALU.add))
            S.op("dve", lambda e: e.tensor_add(Sf32[:], bC[:, 128:256],
                                               KTUW[:, 0:128]))
            S.op("dve", lambda e: e.tensor_mul(scr1[:], Of32[:], Of32[:]))
            S.op("dve", lambda e: e.tensor_copy(Sbf[:], Sf32[:]))
            S.op("dve", lambda e, c=c: e.tensor_copy(
                Obf8[:, (c % 8) * 128:(c % 8) * 128 + 128], Of32[:]))
            S.op("dve", lambda e, c=c: e.reduce_sum(osum[:, c:c + 1], scr1[:],
                                                    axis=AX.X))
            S.nxt()
            if c % 8 == 7:
                S.dma("sq", "qs", lambda e, p=p, c=c: e.dma_start(
                    oscr[p][:, (c - 7) * 128:(c + 1) * 128], Obf8[:]))
                S.nxt()
        S.op("dve", lambda e: e.tensor_scalar(osum[:], osum[:], 1.0 / 128.0,
                                              1e-6, ALU.mult, ALU.add))
        S.nxt()
        S.op("act", lambda e: e.activation(osum[:], osum[:], AF.Sqrt))
        S.nxt()
        S.op("dve", lambda e: e.reciprocal(osum[:], osum[:]))
        S.nxt()
        for g in range(8):
            S.dma("sq", "qs", lambda e, p=p, g=g: e.dma_start(
                Obf8[:], oscr[p][:, g * 1024:(g + 1) * 1024]))
            S.nxt()
            for j in range(8):
                S.op("dve", lambda e, g=g, j=j: e.scalar_tensor_tensor(
                    Obf8[:, j * 128:(j + 1) * 128],
                    Obf8[:, j * 128:(j + 1) * 128],
                    osum[:, g * 8 + j:g * 8 + j + 1], sb["normbc"][:],
                    ALU.mult, ALU.mult))
            S.nxt()
            for j in range(8):
                S.op("pe", lambda e, j=j: e.transpose(
                    pbP[:, j * 128:(j + 1) * 128],
                    Obf8[:, j * 128:(j + 1) * 128], sb["identb"][:]))
            S.nxt()
            for j in range(8):
                S.op("dve", lambda e, j=j: e.tensor_copy(
                    qbf[:, j * 128:(j + 1) * 128], pbP[:, j * 128:(j + 1) * 128]))
            S.nxt()
            S.dma("sq", "qs", lambda e, p=p, g=g: e.dma_start(
                o_out[p][:, g * 1024:(g + 1) * 1024], qbf[:, 0:1024]))
            S.nxt()
        return S

    s0 = pair_stream(0)
    s1 = pair_stream(1)
    _merge_streams(ctx, [s0, s1], [0, 1])
    ctx.pop_phase()
    if standalone:
        _ctx_finish(ctx)
    return nc




_BUILD_CACHE = {}


def _get(name):
    if name not in _BUILD_CACHE:
        _BUILD_CACHE[name] = globals()["build_" + name]()
    return _BUILD_CACHE[name]


def _run(nc, in_maps):
    return run_bass_kernel_spmd(nc, in_maps, core_ids=list(range(NCORES)))


def _moe0(h, sh):
    res = _run(_get("moe0a"), [{
        "hin": h[c], "gT": sh["gateT"], "identD": sh["ident"],
    } for c in range(8)])
    probs = np.concatenate([res.results[c]["probs_out"] for c in range(8)], axis=0)
    in2 = []
    for c in range(8):
        b = c // 4
        pg = np.ascontiguousarray(probs[b * T:(b + 1) * T].T)
        pown = np.ascontiguousarray(probs[c * TPC:(c + 1) * TPC])
        in2.append({
            "hin": h[c], "pglob": pg, "pown": pown,
            "w1h_l0": sh["w1h_l0"], "w2h_l0": sh["w2h_l0"],
            "LstD": sh["Lstrict"],
            "identD": sh["ident"], "onesD": sh["ones_row"],
            "bdD": sh["bd16"], "bdTD": sh["bd16T"],
            "coff": np.zeros((1, 1), np.float32),
        })
    res = _run(_get("moe0b"), in2)
    return [res.results[c]["hout"] for c in range(8)]


def _moe1(h, sh):
    b1c = np.ascontiguousarray(
        sh["l1b1"].reshape(8, 32, 128).transpose(2, 0, 1).reshape(128, 256))
    b2r = np.ascontiguousarray(sh["l1b2"].reshape(1, 8192))
    res = _run(_get("moe1"), [{
        "hin": h[c],
        "w1h_l1": sh["w1h_l1"], "w2h_l1": sh["w2h_l1"],
        "LstD": sh["Lstrict"], "b2c_l1": sh["l1b2c"],
        "wrT": sh["WrT"], "br": sh["br"], "b1c": b1c,
        "identD": sh["ident"], "onesD": sh["ones_row"],
    } for c in range(8)])
    return [res.results[c]["hout"] for c in range(8)]


def _attn0(h, sh, pc):
    bqc = np.ascontiguousarray(sh["bq"].reshape(8, 128).T)
    bkc = np.ascontiguousarray(sh["bk"].reshape(8, 128).T)
    res = _run(_get("attn0a"), [{
        "hin": h[c], "WqT": sh["WqT"], "WkT": sh["WkT"], "WvT": sh["WvT"],
        "bqc": bqc, "bkc": bkc, "bvr": sh["bv"],
        "cosD": pc[c]["cosDup"], "sinD": pc[c]["sinDup"], "pswD": sh["pswapT"],
        "identD": sh["ident"], "onesD": sh["ones_row"],
    } for c in range(8)])
    in2 = []
    for c in range(8):
        b = c // 4
        grp = list(range(b * 4, b * 4 + 4))
        in2.append({
            "hin": h[c], "lnin": res.results[c]["ln_out"],
            "qr_in": res.results[c]["qr_out"], "q_in": res.results[c]["q_out"],
            "kvp4": np.stack([res.results[g]["kv_out"] for g in grp]),
            "kmp4": np.stack([res.results[g]["km_out"] for g in grp]),
            "kvmD": sh["kvmask"], "hm2D": sh["headmask2"],
            "gamD": sh["gam0"], "betD": sh["bet0"],
            "identD": sh["ident"], "onesD": sh["ones_row"],
        })
    res2 = _run(_get("attn0b"), in2)
    return [res2.results[c]["hout"] for c in range(8)]


def _attn1(h, sh):
    res = _run(_get("attn1a"), [{
        "hin": h[c], "qkvT": sh["qkvT"], "zT": sh["zT"], "baT": sh["baT"],
        "identD": sh["ident"],
    } for c in range(8)])
    mixed = [res.results[c]["mixed_out"] for c in range(8)]
    ba = [res.results[c]["ba_out"] for c in range(8)]
    in2 = []
    for c in range(8):
        b = c // 4
        cores = list(range(b * 4, b * 4 + 4))
        qkv_pair = np.zeros((2, 3, 128, T), np.float32)
        ba_pair = np.zeros((2, 2, T), np.float32)
        cw_pair = np.zeros((2, 3, 128, 4), np.float32)
        dtb_pair = np.zeros((2, 1), np.float32)
        alog_pair = np.zeros((2, 1), np.float32)
        for j in range(2):
            hh = (c % 4) * 2 + j
            for g, src_core in enumerate(cores):
                sl = slice(g * TPC, (g + 1) * TPC)
                qkv_pair[j, 0, :, sl] = mixed[src_core][hh * 128:(hh + 1) * 128]
                qkv_pair[j, 1, :, sl] = mixed[src_core][1024 + hh * 128:1024 + (hh + 1) * 128]
                qkv_pair[j, 2, :, sl] = mixed[src_core][2048 + hh * 128:2048 + (hh + 1) * 128]
                ba_pair[j, 0, sl] = ba[src_core][hh]
                ba_pair[j, 1, sl] = ba[src_core][8 + hh]
            cw_pair[j, 0] = sh["convw"][hh * 128:(hh + 1) * 128]
            cw_pair[j, 1] = sh["convw"][1024 + hh * 128:1024 + (hh + 1) * 128]
            cw_pair[j, 2] = sh["convw"][2048 + hh * 128:2048 + (hh + 1) * 128]
            dtb_pair[j, 0] = sh["dtb"][hh, 0]
            alog_pair[j, 0] = sh["Alog"][hh, 0]
        in2.append({
            "qkv_in": qkv_pair, "ba_in": ba_pair, "cw_in": cw_pair,
            "dtb_in": dtb_pair, "alog_in": alog_pair,
            "nwD": sh["normw"], "identD": sh["ident"], "onesD": sh["ones_row"],
            "mUsD": sh["maskUs"], "mLsD": sh["maskLs"], "mUiD": sh["maskUi"],
            "triuD": sh["triu_incl"],
        })
    res2 = _run(_get("attn1b"), in2)
    in3 = []
    for c in range(8):
        b = c // 4
        sl = slice((c % 4) * TPC, (c % 4 + 1) * TPC)
        o_mine = np.zeros((1024, TPC), bfnp)
        for hh in range(8):
            owner = b * 4 + hh // 2
            j = hh % 2
            o_mine[hh * 128:(hh + 1) * 128] = res2.results[owner]["o_out"][j][:, sl]
        in3.append({
            "hin": h[c], "o_in": o_mine, "z_in": res.results[c]["z_out"],
            "outT": sh["outT"],
        })
    res3 = _run(_get("attn1c"), in3)
    return [res3.results[c]["hout"] for c in range(8)]


def kernel(**inputs):
    import os
    if os.environ.get("KERNEL_UNFUSED", "0") != "1":
        return kernel_fused(**inputs)
    sh, pc = host_prep(inputs)
    h = [np.ascontiguousarray(pc[c]["x0"]) for c in range(8)]
    h = _moe0(h, sh)
    h = _attn0(h, sh, pc)
    h = _moe0(h, sh)
    h = _moe1(h, sh)
    h = _attn1(h, sh)
    h = _moe1(h, sh)
    out = np.concatenate([np.asarray(x, np.float32) for x in h], axis=0)
    return out.reshape(B, T, D)



GROUPS4 = [[0, 1, 2, 3], [4, 5, 6, 7]]


def emit_pglob_glue(ctx, pref, nc, probs_loc, identD):
    probs_g = nc.dram_tensor(pref + "probs_g", [T, 8], F32).ap()
    pglob = nc.dram_tensor(pref + "pglob", [8, T], F32).ap()
    ctx.push_phase(pref)
    a = ctx.alloc
    a("ptile4", [128, 64], F32)
    a("prow4", [8, 1024], F32)
    a("gident", [128, 128], F32)
    psx2 = [ctx.alloc_psum("pst%d" % i, [128, 512], F32) for i in range(2)]
    NIT = 16
    with ctx.stage():
        ctx.dma("sq", "qs", lambda e: e.dma_start(ctx.sb["gident"][:], identD[:]))
        ctx.cc(lambda e: e.collective_compute(
            "AllGather", ALU.bypass, GROUPS4, [probs_loc.opt()], [probs_g.opt()]))
    for i in range(NIT + 3):
        with ctx.stage():
            if i < NIT:
                ctx.dma("sq", "qs", lambda e, i=i: e.dma_start(
                    ctx.sb["ptile4"][:, (i % 2) * 32:(i % 2) * 32 + 32].rearrange(
                        "p (a e) -> p a e", e=8),
                    probs_g[i * 512:(i + 1) * 512, :].rearrange(
                        "(a p) e -> p a e", p=128)))
            if 1 <= i <= NIT:
                x = i - 1
                for aa in range(4):
                    ctx.op("pe", lambda e, x=x, aa=aa: e.transpose(
                        psx2[x % 2][0:8, aa * 128:(aa + 1) * 128],
                        ctx.sb["ptile4"][:, (x % 2) * 32 + aa * 8:
                                         (x % 2) * 32 + (aa + 1) * 8],
                        ctx.sb["gident"][:]))
            if 2 <= i <= NIT + 1:
                x = i - 2
                ctx.op("dve", lambda e, x=x: e.tensor_copy(
                    ctx.sb["prow4"][0:8, (x % 2) * 512:(x % 2) * 512 + 512],
                    psx2[x % 2][0:8, 0:512]))
            if 3 <= i:
                x = i - 3
                ctx.dma("sq", "qs", lambda e, x=x: e.dma_start(
                    pglob[:, x * 512:(x + 1) * 512],
                    ctx.sb["prow4"][0:8, (x % 2) * 512:(x % 2) * 512 + 512]))
    ctx.pop_phase()
    return pglob


def build_fused():
    nc = bass.Bass("TRN2", target_bir_lowering=False)
    E = lambda n, s, dt: nc.dram_tensor(n, list(s), dt, kind="ExternalInput").ap()
    I_ = lambda n, s, dt: nc.dram_tensor(n, list(s), dt).ap()
    x0 = E("x0", [TPC, D], F32)
    hout = nc.dram_tensor("hout", [TPC, D], F32, kind="ExternalOutput").ap()
    com = {
        "identD": E("identD", [128, 128], F32),
        "onesD": E("onesD", [1, 512], F32),
    }
    w = {
        "gT": E("gT", [1024, 8], F32),
        "w1h_l0": E("w1h_l0", [NE, 32, 128, 8, 128], F32),
        "w2h_l0": E("w2h_l0", [NE, 8, 128, 32, 128], F32),
        "w1h_l1": E("w1h_l1", [NE, 32, 128, 8, 128], F32),
        "w2h_l1": E("w2h_l1", [NE, 8, 128, 32, 128], F32),
        "LstD": E("LstD", [128, 128], F32),
        "b2c_l1": E("b2c_l1", [128, 64], F32),
        "coff": E("coff", [1, 1], F32),
        "bdD": E("bdD", [128, 8], F32),
        "bdTD": E("bdTD", [8, 128], F32),
        "WqT": E("WqT", [D, D], F32), "WkT": E("WkT", [D, D], F32),
        "WvT": E("WvT", [D, D], F32),
        "bqc": E("bqc", [128, 8], F32), "bkc": E("bkc", [128, 8], F32),
        "bvr": E("bvr", [1, D], F32),
        "cosD": E("cosD", [1024, TPC], F32), "sinD": E("sinD", [1024, TPC], F32),
        "pswD": E("pswD", [128, 128], F32),
        "kvmD": E("kvmD", [128, 128], F32), "hm2D": E("hm2D", [128, 2], F32),
        "gamD": E("gamD", [1, D], F32), "betD": E("betD", [1, D], F32),
        "wrT": E("wrT", [1024, 8], F32), "br": E("br", [8, 1], F32),
        "b1c": E("b1c", [128, 256], F32),
        "qkvT": E("qkvT", [D, 3072], F32), "zT": E("zT", [D, D], F32),
        "baT": E("baT", [D, 16], F32),
        "cw_in": E("cw_in", [2, 3, 128, 4], F32),
        "dtb_in": E("dtb_in", [2, 1], F32), "alog_in": E("alog_in", [2, 1], F32),
        "nwD": E("nwD", [1, 128], F32),
        "mUsD": E("mUsD", [128, 128], F32), "mLsD": E("mLsD", [128, 128], F32),
        "mUiD": E("mUiD", [128, 128], F32), "triuD": E("triuD", [128, 128], F32),
        "outT": E("outT", [D, D], F32),
        "gqD": E("gqD", [128, 48], I32), "gbaD": E("gbaD", [64, 4], I32),
        "goD": E("goD", [128, 8], I32),
    }
    hh = [I_("hh%d" % i, [TPC, D], F32) for i in range(6)]
    probs1 = I_("probs1", [TPC, 8], F32)
    probs2 = I_("probs2", [TPC, 8], F32)
    qr_s = I_("qr_s", [D, TPC], BF16)
    q_s = I_("q_s", [D, TPC], BF16)
    kv_s = I_("kv_s", [128, 1024], F32)
    km_s = I_("km_s", [128, 8], F32)
    kvr = I_("kvr", [128, 1024], F32)
    kmr = I_("kmr", [128, 8], F32)
    ln_s = I_("ln_s", [TPC, D], F32)
    mix_sb = I_("mix_sb", [3072, TPC], BF16)
    ba_s = I_("ba_s", [16, TPC], F32)
    mix_gq = I_("mix_gq", [4096, TPC], BF16)
    mix_gk = I_("mix_gk", [4096, TPC], BF16)
    mix_gv = I_("mix_gv", [4096, TPC], BF16)
    mix_gba = I_("mix_gba", [64, TPC], F32)
    z_s = I_("z_s", [D, TPC], F32)
    o_sflat = I_("o_sflat", [256, T], BF16)
    o_g = I_("o_g", [1024, T], BF16)

    ctx = Ctx(nc)

    def moe0(hin, hres, probs_loc, pref):
        lnscr = nc.dram_tensor(pref + "lnscr", [TPC, 1024], BF16).ap()
        io = dict(com); io.update(w)
        io["hin"] = hin; io["probs_out"] = probs_loc
        io["lnscr"] = lnscr
        build_moe0a(ctx, io, pref + "a_")
        pglob = emit_pglob_glue(ctx, pref + "g_", nc, probs_loc, com["identD"])
        io2 = dict(com); io2.update(w)
        io2["hin"] = hin; io2["hout"] = hres
        io2["pglob"] = pglob; io2["pown"] = probs_loc
        io2["lnscr"] = lnscr; io2["ln_pre"] = True
        build_moe0b(ctx, io2, pref + "b_")

    def moe1(hin, hres, pref):
        io = dict(com); io.update(w)
        io["hin"] = hin; io["hout"] = hres
        build_moe1(ctx, io, pref)

    def attn0(hin, hres, pref):
        io = dict(com); io.update(w)
        io.update({"hin": hin, "qr_out": qr_s, "q_out": q_s, "kv_out": kv_s,
                   "km_out": km_s, "ln_out": ln_s})
        build_attn0a(ctx, io, pref + "a_")
        with ctx.stage():
            ctx.cc(lambda e: e.collective_compute(
                "AllReduce", ALU.add, GROUPS4, [kv_s.opt()], [kvr.opt()]))
            ctx.cc(lambda e: e.collective_compute(
                "AllReduce", ALU.add, GROUPS4, [km_s.opt()], [kmr.opt()]))
        io2 = dict(com); io2.update(w)
        io2.update({"hin": hin, "lnin": ln_s, "qr_in": qr_s, "q_in": q_s,
                    "kvp4": kvr.rearrange("(o p) c -> o p c", o=1),
                    "kmp4": kmr.rearrange("(o p) c -> o p c", o=1),
                    "hout": hres})
        build_attn0b(ctx, io2, pref + "b_", reduced=True)

    def attn1(hin, hres, pref):
        chunks = [(mix_sb[0:1024, :], mix_gq), (mix_sb[1024:2048, :], mix_gk),
                  (mix_sb[2048:3072, :], mix_gv), (ba_s, mix_gba)]

        def cc_hook(j):
            with ctx.stage():
                ctx.cc(lambda e, j=j: e.collective_compute(
                    "AllGather", ALU.bypass, GROUPS4,
                    [chunks[j][0].opt()], [chunks[j][1].opt()]))

        io = dict(com); io.update(w)
        io.update({"hin": hin, "mixed_out": mix_sb,
                   "ba_out": ba_s, "z_out": z_s,
                   "cc_hook": cc_hook})
        cc0 = ctx.cnt["cc"]
        ctx.mask_sem("cc")
        build_attn1a(ctx, io, pref + "a_")
        io2 = dict(com); io2.update(w)
        io2.update({"cc_base": cc0,
                    "mix_gq": mix_gq, "mix_gk": mix_gk, "mix_gv": mix_gv,
                    "mix_gba": mix_gba,
                    "o_out": o_sflat.rearrange("(j p) t -> j p t", p=128)})
        build_attn1b(ctx, io2, pref + "b_", gather=True)
        ctx.unmask_sem("cc")
        with ctx.stage():
            ctx.cc(lambda e: e.collective_compute(
                "AllGather", ALU.bypass, GROUPS4, [o_sflat.opt()], [o_g.opt()]))
        io3 = dict(com); io3.update(w)
        io3.update({"hin": hin, "o_g": o_g, "z_in": z_s, "hout": hres})
        build_attn1c(ctx, io3, pref + "c_", gather=True)

    moe0(x0, hh[0], probs1, "p1")
    attn0(hh[0], hh[1], "p2")
    moe0(hh[1], hh[2], probs2, "p3")
    moe1(hh[2], hh[3], "p4")
    attn1(hh[3], hh[4], "p5")
    io = dict(com); io.update(w)
    io["hin"] = hh[4]; io["hout"] = hout
    build_moe1(ctx, io, "p6")
    _ctx_finish(ctx)
    return nc


def _fused_inputs(sh, pc, c, x0c):
    b1c = np.ascontiguousarray(
        sh["l1b1"].reshape(8, 32, 128).transpose(2, 0, 1).reshape(128, 256))
    b2r = np.ascontiguousarray(sh["l1b2"].reshape(1, 8192))
    bqc = np.ascontiguousarray(sh["bq"].reshape(8, 128).T)
    bkc = np.ascontiguousarray(sh["bk"].reshape(8, 128).T)
    cw_pair = np.zeros((2, 3, 128, 4), np.float32)
    dtb_pair = np.zeros((2, 1), np.float32)
    alog_pair = np.zeros((2, 1), np.float32)
    gq = np.zeros((128, 48), np.int32)
    gba = np.zeros((64, 4), np.int32)
    go = np.zeros((128, 8), np.int32)
    p_ = np.arange(128)
    for j in range(2):
        hh_ = (c % 4) * 2 + j
        for blk in range(3):
            cw_pair[j, blk] = sh["convw"][blk * 1024 + hh_ * 128:
                                          blk * 1024 + (hh_ + 1) * 128]
            for st in range(8):
                gq[:, j * 24 + blk * 8 + st] = \
                    ((st // 2) * 1024 + hh_ * 128 + p_) * TPC + (st % 2) * 1024
        for row in range(2):
            for r in range(4):
                for cc_ in range(16):
                    gba[r * 16 + cc_, j * 2 + row] = \
                        (r * 16 + row * 8 + hh_) * TPC + cc_ * 128
        dtb_pair[j, 0] = sh["dtb"][hh_, 0]
        alog_pair[j, 0] = sh["Alog"][hh_, 0]
    for vh in range(8):
        r = vh // 2
        jj = vh % 2
        go[:, vh] = (r * 256 + jj * 128 + p_) * T + (c % 4) * TPC
    return {
        "x0": x0c,
        "identD": sh["ident"], "onesD": sh["ones_row"],
        "gT": sh["gateT"],
        "w1h_l0": sh["w1h_l0"], "w2h_l0": sh["w2h_l0"],
        "w1h_l1": sh["w1h_l1"], "w2h_l1": sh["w2h_l1"],
        "LstD": sh["Lstrict"], "b2c_l1": sh["l1b2c"],
        "coff": np.zeros((1, 1), np.float32),
        "bdD": sh["bd16"], "bdTD": sh["bd16T"],
        "WqT": sh["WqT"], "WkT": sh["WkT"], "WvT": sh["WvT"],
        "bqc": bqc, "bkc": bkc, "bvr": sh["bv"],
        "cosD": pc[c]["cosDup"], "sinD": pc[c]["sinDup"], "pswD": sh["pswapT"],
        "kvmD": sh["kvmask"], "hm2D": sh["headmask2"],
        "gamD": sh["gam0"], "betD": sh["bet0"],
        "wrT": sh["WrT"], "br": sh["br"], "b1c": b1c,
        "qkvT": sh["qkvT"], "zT": sh["zT"], "baT": sh["baT"],
        "cw_in": cw_pair, "dtb_in": dtb_pair, "alog_in": alog_pair,
        "nwD": sh["normw"],
        "mUsD": sh["maskUs"], "mLsD": sh["maskLs"], "mUiD": sh["maskUi"],
        "triuD": sh["triu_incl"], "outT": sh["outT"],
        "gqD": gq, "gbaD": gba, "goD": go,
    }


def kernel_fused(**inputs):
    sh, pc = host_prep(inputs)
    nc = _get("fused")
    in_maps = [_fused_inputs(sh, pc, c, np.ascontiguousarray(pc[c]["x0"]))
               for c in range(8)]
    res = _run(nc, in_maps)
    out = np.concatenate([np.asarray(res.results[c]["hout"], np.float32)
                          for c in range(8)], axis=0)
    return out.reshape(B, T, D)



BIGOOB = 1.0e6


def emit_moe_maps(ctx, layer, cap, psB):
    sb = ctx.sb
    a_ = ctx.alloc
    a_("mk8", [128, 128], F32)
    a_("rp8", [128, 128], F32)
    a_("rt8", [128, 8], F32)
    a_("rb8", [128, 8], F32)
    a_("sf8", [128, 128], F32)
    a_("omap", [128, 128], I32)
    a_("oel", [128, 128], I32)
    a_("ones16", [128, 16], F32)
    if layer == 1:
        a_("oall", [128, 16], F32)
        a_("oalli", [128, 16], I32)
    with ctx.stage():
        ctx.op("dve", lambda e: e.memset(sb["ones16"][:], 0.0))
    with ctx.stage():
        ctx.op("dve", lambda e: e.tensor_scalar_add(sb["ones16"][:], sb["ones16"][:],
                                                    1.0))
    with ctx.stage():
        for e_ in range(NE):
            if layer == 1:
                ctx.op("dve", lambda e, e_=e_: e.tensor_single_scalar(
                    sb["mk8"][:, e_ * 16:(e_ + 1) * 16], sb["idxf"][:, 0:16],
                    float(e_), ALU.is_equal))
            else:
                ctx.op("dve", lambda e, e_=e_: e.tensor_copy(
                    sb["mk8"][:, e_ * 16:(e_ + 1) * 16],
                    sb["sel128"][:, 0:128].rearrange("p (t e) -> p e t", e=8)[:, e_, :]))
    with ctx.stage():
        for e_ in range(NE):
            ctx.op("dve", lambda e, e_=e_: e.tensor_tensor_scan(
                sb["rp8"][:, e_ * 16:(e_ + 1) * 16], sb["ones16"][:],
                sb["mk8"][:, e_ * 16:(e_ + 1) * 16], 0.0, ALU.mult, ALU.add))
        ctx.op("dve", lambda e: e.tensor_reduce(
            sb["rt8"][:, 0:8], sb["mk8"][:, 0:128].rearrange(
                "p (e t) -> p e t", t=16), AX.X, ALU.add))
    with ctx.stage():
        for e_ in range(NE):
            ctx.op("pe", lambda e, e_=e_: e.matmul(
                psB[:, e_:e_ + 1], ctx.sb["Lst_sb"][:], sb["rt8"][:, e_:e_ + 1],
                start=True, stop=True))
    with ctx.stage():
        ctx.op("dve", lambda e: e.tensor_copy(sb["rb8"][:, 0:8], psB[:, 0:8]))
    with ctx.stage():
        for e_ in range(NE):
            ctx.op("dve", lambda e, e_=e_: e.tensor_scalar(
                sb["sf8"][:, e_ * 16:(e_ + 1) * 16],
                sb["rp8"][:, e_ * 16:(e_ + 1) * 16], 1.0,
                sb["rb8"][:, e_:e_ + 1], ALU.subtract, ALU.add))
    with ctx.stage():
        for e_ in range(NE):
            ctx.op("dve", lambda e, e_=e_: e.tensor_scalar_add(
                sb["sf8"][:, e_ * 16:(e_ + 1) * 16],
                sb["sf8"][:, e_ * 16:(e_ + 1) * 16], float(e_ * cap - BIGOOB)))
    with ctx.stage():
        for e_ in range(NE):
            ctx.op("dve", lambda e, e_=e_: e.tensor_mul(
                sb["sf8"][:, e_ * 16:(e_ + 1) * 16],
                sb["sf8"][:, e_ * 16:(e_ + 1) * 16],
                sb["mk8"][:, e_ * 16:(e_ + 1) * 16]))
    with ctx.stage():
        ctx.op("dve", lambda e: e.tensor_scalar_add(sb["sf8"][:], sb["sf8"][:],
                                                    float(BIGOOB)))
    with ctx.stage():
        ctx.op("dve", lambda e: e.tensor_copy(sb["omap"][:], sb["sf8"][:]))
        ctx.op("dve", lambda e: e.tensor_scalar_mul(sb["rp8"][:], sb["sf8"][:],
                                                    1024.0))
        if layer == 1:
            ctx.op("dve", lambda e: e.tensor_tensor(
                sb["sf8"][:, 0:64], sb["sf8"][:, 0:64], sb["sf8"][:, 64:128],
                ALU.min))
    with ctx.stage():
        ctx.op("dve", lambda e: e.tensor_copy(sb["oel"][:], sb["rp8"][:]))
        if layer == 1:
            ctx.op("dve", lambda e: e.tensor_tensor(
                sb["sf8"][:, 0:32], sb["sf8"][:, 0:32], sb["sf8"][:, 32:64],
                ALU.min))
    if layer == 1:
        with ctx.stage():
            ctx.op("dve", lambda e: e.tensor_tensor(
                sb["oall"][:], sb["sf8"][:, 0:16], sb["sf8"][:, 16:32], ALU.min))
        with ctx.stage():
            ctx.op("dve", lambda e: e.tensor_copy(sb["oalli"][:], sb["oall"][:]))
            ctx.op("dve", lambda e: e.tensor_scalar_mul(sb["oall"][:],
                                                        sb["oall"][:], 1024.0))
        with ctx.stage():
            ctx.op("dve", lambda e: e.tensor_copy(sb["oel"][:, 0:16], sb["oall"][:]))


def emit_moe_gath(ctx, layer, hin_tile, hout_dram, w1h, w2h, lnscr, gbuf, ybuf,
                  b1_sb=None, b2c_sb=None, psum=None, cap=512, resid_scale=0.5,
                  selw_src=None):
    p0, p1, p2, p3, p4, p5 = psum[:6]
    sb = ctx.sb
    a_ = ctx.alloc
    ntil = cap // 128
    w512 = min(cap, 512)
    a_("lntile", [128, 2 * 1024], BF16)
    a_("xg", [128, 8 * 640], BF16)
    a_("hg", [128, 32 * 640], BF16)
    a_("w1buf2", [128, 2 * 1024], BF16)
    a_("w2buf2", [128, 2 * 4096], BF16)
    a_("ydc", [128, 2 * 640], BF16)
    a_("ysb", [128, 5 * 1024], BF16)
    a_("gdst", [128, 2 * 4096], BF16)
    a_("xres3", [128, 3 * 1024], F32)
    act_fn = AF.Gelu if layer == 0 else AF.Relu
    omap = sb["omap"]
    import os
    cutoff = int(os.environ.get("MOE_GATH_CUT", "999999"))
    with ctx.stage():
        ctx.op("dve", lambda e: e.memset(sb["gdst"][:], 0.0))
        ctx.op("pool", lambda e: e.memset(sb["xres3"][:], 0.0))

    for t in range(NT + 1):
        with ctx.stage():
            if t < NT:
                ctx.dma("sq", "qs", lambda e, t=t: e.dma_start(
                    sb["lntile"][:, (t % 2) * 1024:(t % 2) * 1024 + 1024],
                    lnscr[t * 128:(t + 1) * 128, :]))
            if t >= 1:
                tx = t - 1
                hrow = ctx.bcreg(8 * cap - 1)
                if layer == 1:
                    ctx.dma("pool", "qp", lambda e, tx=tx, h=hrow: e.indirect_dma_start(
                        gbuf[:], IndirectOffsetOnAxis(sb["oalli"][:, tx:tx + 1], 0),
                        sb["lntile"][:, (tx % 2) * 1024:(tx % 2) * 1024 + 1024], None,
                        bounds_check=h["r"], oob_is_err=False))
                else:
                    ctx.dma("pool", "qp", lambda e, tx=tx, h=hrow:
                            e.indirect_dma_start(
                        gbuf[:],
                        IndirectOffsetOnAxis(omap[:, tx:tx + 1], 0),
                        sb["lntile"][:, (tx % 2) * 1024:(tx % 2) * 1024 + 1024],
                        None, bounds_check=h["r"], oob_is_err=False))
    for e_ in (range(NE) if cutoff >= 2 else []):
        for s in range(ntil + 2):
            with ctx.stage():
                if s < ntil:
                    ctx.dma("sq", "qs", lambda e, s=s, e_=e_: e.dma_start(
                        sb["lntile"][:, (s % 2) * 1024:(s % 2) * 1024 + 1024],
                        gbuf[e_ * cap + s * 128: e_ * cap + (s + 1) * 128, :]))
                if 1 <= s <= ntil:
                    sx = s - 1
                    for k in range(8):
                        ctx.op("pe", lambda e, k=k, sx=sx: e.transpose(
                            ctx.psb[sx % 2][:, k * 128:(k + 1) * 128],
                            sb["lntile"][:, (sx % 2) * 1024 + k * 128:
                                         (sx % 2) * 1024 + (k + 1) * 128],
                            sb["identb"][:]))
                if 2 <= s:
                    sx = s - 2
                    for k in range(8):
                        ctx.op("dve", lambda e, k=k, sx=sx, e_=e_: e.tensor_copy(
                            sb["xg"][:, k * cap + sx * 128:
                                     k * cap + (sx + 1) * 128],
                            ctx.psb[sx % 2][:, k * 128:(k + 1) * 128]))
        for f in (range(-1, 33) if cutoff >= 22 else []):
            with ctx.stage():
                if layer == 0 and e_ + 1 < NE:
                    if 0 <= f < NT:
                        ctx.dma("sq", "qs", lambda e, f=f: e.dma_start(
                            sb["lntile"][:, (f % 2) * 1024:(f % 2) * 1024 + 1024],
                            lnscr[f * 128:(f + 1) * 128, :]))
                    if 1 <= f <= NT:
                        tx = f - 1
                        hrow = ctx.bcreg(8 * cap - 1)
                        ctx.dma("pool", "qp", lambda e, tx=tx, e_=e_, h=hrow:
                                e.indirect_dma_start(
                            gbuf[:],
                            IndirectOffsetOnAxis(omap[:, (e_ + 1) * 16 + tx:
                                                      (e_ + 1) * 16 + tx + 1], 0),
                            sb["lntile"][:, (tx % 2) * 1024:(tx % 2) * 1024 + 1024],
                            None, bounds_check=h["r"], oob_is_err=False))
                if f + 1 < 32:
                    ctx.dma("pool", "qp", lambda e, f=f, e_=e_: e.dma_start(
                        sb["w1buf2"][:, ((f + 1) % 2) * 1024:
                                     ((f + 1) % 2) * 1024 + 1024].rearrange(
                            "p (k f) -> p k f", f=128),
                        w1h[e_, f + 1].rearrange("p k f -> p k f")))
                if 0 <= f < 32:
                    pa, pb = (p0, p1) if f % 2 == 0 else (p2, p3)
                    for k in range(8):
                        ctx.op("pe", lambda e, k=k, f=f, pa=pa: e.matmul(
                            pa[:, 0:w512],
                            sb["w1buf2"][:, (f % 2) * 1024 + k * 128:
                                         (f % 2) * 1024 + (k + 1) * 128],
                            sb["xg"][:, k * cap: k * cap + w512],
                            start=(k == 0), stop=(k == 7)))
                        if cap > 512:
                            ctx.op("pe", lambda e, k=k, f=f, pb=pb: e.matmul(
                                pb[:, 0:cap - 512],
                                sb["w1buf2"][:, (f % 2) * 1024 + k * 128:
                                             (f % 2) * 1024 + (k + 1) * 128],
                                sb["xg"][:, k * cap + 512: k * cap + cap],
                                start=(k == 0), stop=(k == 7)))
                if 1 <= f <= 32:
                    fx = f - 1
                    pa, pb = (p0, p1) if fx % 2 == 0 else (p2, p3)
                    bias = None if layer == 0 else b1_sb[:, e_ * 32 + fx:
                                                         e_ * 32 + fx + 1]
                    if bias is None:
                        ctx.op("act", lambda e, fx=fx, pa=pa: e.activation(
                            sb["hg"][:, fx * cap: fx * cap + w512], pa[:, 0:w512],
                            AF.Gelu))
                        if cap > 512:
                            ctx.op("act", lambda e, fx=fx, pb=pb: e.activation(
                                sb["hg"][:, fx * cap + 512: (fx + 1) * cap],
                                pb[:, 0:cap - 512], AF.Gelu))
                    else:
                        ctx.op("act", lambda e, fx=fx, pa=pa, bias=bias: e.activation(
                            sb["hg"][:, fx * cap: fx * cap + w512], pa[:, 0:w512],
                            AF.Relu, bias=bias))
                        if cap > 512:
                            ctx.op("dve", lambda e, fx=fx, pb=pb, bias=bias:
                                   e.tensor_scalar(
                                sb["hg"][:, fx * cap + 512: (fx + 1) * cap],
                                pb[:, 0:cap - 512], bias, 0.0, ALU.add, ALU.max))
        for dc in (range(-1, 8 + 3) if cutoff >= 231 else []):
            with ctx.stage():
                if -1 <= dc < 7:
                    ctx.dma("pool", "qp", lambda e, dc=dc, e_=e_: e.dma_start(
                        sb["w2buf2"][:, ((dc + 1) % 2) * 4096:
                                     ((dc + 1) % 2) * 4096 + 4096].rearrange(
                            "p (k c) -> p k c", c=128),
                        w2h[e_, dc + 1].rearrange("p k c -> p k c")))
                if 0 <= dc < 8:
                    pa, pb = (p0, p1) if dc % 2 == 0 else (p2, p3)
                    for f in range(32):
                        ctx.op("pe", lambda e, f=f, dc=dc, pa=pa: e.matmul(
                            pa[:, 0:w512],
                            sb["w2buf2"][:, (dc % 2) * 4096 + f * 128:
                                         (dc % 2) * 4096 + (f + 1) * 128],
                            sb["hg"][:, f * cap: f * cap + w512],
                            start=(f == 0), stop=(f == 31)))
                        if cap > 512:
                            ctx.op("pe", lambda e, f=f, dc=dc, pb=pb: e.matmul(
                                pb[:, 0:cap - 512],
                                sb["w2buf2"][:, (dc % 2) * 4096 + f * 128:
                                             (dc % 2) * 4096 + (f + 1) * 128],
                                sb["hg"][:, f * cap + 512: f * cap + cap],
                                start=(f == 0), stop=(f == 31)))
                if 1 <= dc <= 8 and cutoff >= 232:
                    dx = dc - 1
                    pa, pb = (p0, p1) if dx % 2 == 0 else (p2, p3)
                    if layer == 1:
                        bcol = b2c_sb[:, e_ * 8 + dx: e_ * 8 + dx + 1]
                        ctx.op("dve", lambda e, dx=dx, pa=pa, bcol=bcol:
                               e.tensor_scalar_add(
                            sb["ydc"][:, (dx % 2) * 640: (dx % 2) * 640 + w512],
                            pa[:, 0:w512], bcol))
                        if cap > 512:
                            ctx.op("dve", lambda e, dx=dx, pb=pb, bcol=bcol:
                                   e.tensor_scalar_add(
                                sb["ydc"][:, (dx % 2) * 640 + 512:
                                          (dx % 2) * 640 + cap],
                                pb[:, 0:cap - 512], bcol))
                    else:
                        ctx.op("dve", lambda e, dx=dx, pa=pa: e.tensor_copy(
                            sb["ydc"][:, (dx % 2) * 640: (dx % 2) * 640 + w512],
                            pa[:, 0:w512]))
                        if cap > 512:
                            ctx.op("dve", lambda e, dx=dx, pb=pb: e.tensor_copy(
                                sb["ydc"][:, (dx % 2) * 640 + 512:
                                          (dx % 2) * 640 + cap],
                                pb[:, 0:cap - 512]))
                if 2 <= dc <= 9 and cutoff >= 233:
                    dx = dc - 2
                    psx = ctx.psb[dx % 2]
                    for s in range(ntil):
                        ctx.op("pe", lambda e, s=s, dx=dx, psx=psx: e.transpose(
                            psx[:, s * 128:(s + 1) * 128],
                            sb["ydc"][:, (dx % 2) * 640 + s * 128:
                                      (dx % 2) * 640 + (s + 1) * 128],
                            sb["identb"][:]))
                if 3 <= dc <= 10 and cutoff >= 234:
                    dx = dc - 3
                    psx = ctx.psb[dx % 2]
                    for s in range(ntil):
                        ctx.op("dve", lambda e, s=s, dx=dx, psx=psx: e.tensor_copy(
                            sb["ysb"][:, s * 1024 + dx * 128:
                                      s * 1024 + (dx + 1) * 128],
                            psx[:, s * 128:(s + 1) * 128]))
        if cutoff >= 235:
            with ctx.stage():
                for s in range(ntil):
                    ctx.dma("act", "qa", lambda e, s=s, e_=e_: e.dma_start(
                        ybuf[e_ * cap + s * 128: e_ * cap + (s + 1) * 128, :],
                        sb["ysb"][:, s * 1024:(s + 1) * 1024]))
    ybflat = ybuf.rearrange("(o a) b -> o (a b)", o=1)
    if cutoff < 236:
        for t in range(NT):
            with ctx.stage():
                ctx.dma("sq", "qs", lambda e, t=t: e.dma_start(
                    sb["xres3"][:, 0:1024], hin_tile(t)))
            with ctx.stage():
                ctx.dma("sq", "qs", lambda e, t=t: e.dma_start(
                    hout_dram[t * 128:(t + 1) * 128, :], sb["xres3"][:, 0:1024]))
        return
    if layer == 1:
        for t in range(NT + 3):
            with ctx.stage():
                if t < NT:
                    ctx.dma("sq", "qs", lambda e, t=t: e.dma_start(
                        sb["xres3"][:, (t % 3) * 1024:(t % 3) * 1024 + 1024],
                        hin_tile(t)))
                    hel = ctx.bcreg(8 * cap * 1024 - 1)
                    ctx.dma("pool", "qp", lambda e, t=t, h=hel: e.indirect_dma_start(
                        sb["gdst"][:, (t % 2) * 4096:(t % 2) * 4096 + 1024], None,
                        ybflat, IndirectOffsetOnAxis(
                            _oix(ctx, 1, t), 1),
                        bounds_check=h["r"], oob_is_err=False))
                if 1 <= t <= NT:
                    tx = t - 1
                    ctx.op("dve", lambda e, tx=tx: e.scalar_tensor_tensor(
                        sb["xres3"][:, (tx % 3) * 1024:(tx % 3) * 1024 + 1024],
                        sb["gdst"][:, (tx % 2) * 4096:(tx % 2) * 4096 + 1024],
                        resid_scale,
                        sb["xres3"][:, (tx % 3) * 1024:(tx % 3) * 1024 + 1024],
                        ALU.mult, ALU.add))
                if 2 <= t <= NT + 1:
                    tx = t - 2
                    ctx.dma("act", "qa", lambda e, tx=tx: e.dma_start(
                        hout_dram[tx * 128:(tx + 1) * 128, :],
                        sb["xres3"][:, (tx % 3) * 1024:(tx % 3) * 1024 + 1024]))
    else:
        steps = [(t, g) for t in range(NT) for g in range(2)]
        for si in range(len(steps) + 2):
            with ctx.stage():
                if si < len(steps):
                    t, g = steps[si]
                    if g == 0:
                        ctx.dma("sq", "qs", lambda e, t=t: e.dma_start(
                            sb["xres3"][:, (t % 3) * 1024:(t % 3) * 1024 + 1024],
                            hin_tile(t)))
                    hel = ctx.bcreg(8 * cap * 1024 - 1)
                    for j in range(4):
                        e_ = g * 4 + j
                        ctx.dma("pool", "qp", lambda e, t=t, e_=e_, j=j, si=si, h=hel:
                                e.indirect_dma_start(
                            sb["gdst"][:, (si % 2) * 4096 + j * 1024:
                                       (si % 2) * 4096 + (j + 1) * 1024], None,
                            ybflat, IndirectOffsetOnAxis(
                                _oix(ctx, 0, t, e_), 1),
                            bounds_check=h["r"], oob_is_err=False))
                if 1 <= si <= len(steps):
                    tx, gx = steps[si - 1]
                    for j in range(4):
                        e_ = gx * 4 + j
                        wcol = sb["selwh"][:, tx * 8 + e_: tx * 8 + e_ + 1]
                        ctx.op("dve", lambda e, tx=tx, j=j, si=si, wcol=wcol:
                               e.scalar_tensor_tensor(
                            sb["xres3"][:, (tx % 3) * 1024:(tx % 3) * 1024 + 1024],
                            sb["gdst"][:, ((si - 1) % 2) * 4096 + j * 1024:
                                       ((si - 1) % 2) * 4096 + (j + 1) * 1024],
                            wcol,
                            sb["xres3"][:, (tx % 3) * 1024:(tx % 3) * 1024 + 1024],
                            ALU.mult, ALU.add))
                if si >= 2 and steps[si - 2][1] == 1:
                    tx = steps[si - 2][0]
                    ctx.dma("act", "qa", lambda e, tx=tx: e.dma_start(
                        hout_dram[tx * 128:(tx + 1) * 128, :],
                        sb["xres3"][:, (tx % 3) * 1024:(tx % 3) * 1024 + 1024]))


def _oix(ctx, layer, t, e_=0):
    if layer == 1:
        return ctx.sb["oel"][:, t:t + 1]
    return ctx.sb["oel"][:, e_ * 16 + t: e_ * 16 + t + 1]
